# version 03
# baseline (speedup 1.0000x reference)
import numpy as np
import ml_dtypes
from contextlib import ExitStack
import concourse.bass as bass
import concourse.mybir as mybir
from concourse.bass_utils import run_bass_kernel_spmd

F32 = mybir.dt.float32
BF16 = mybir.dt.bfloat16
AF = mybir.ActivationFunctionType
ALU = mybir.AluOpType
AX = mybir.AxisListType
BF = ml_dtypes.bfloat16

NCORES = 8
B, L, D = 8, 8192, 1024
ROWS = 1024
PRE = 128
RB = ROWS + PRE
TOK = B * RB
TOWN = B * ROWS
LN_EPS = 1e-5

SC_W = 128
SC_LC = 128
SC_STEPS = SC_W + SC_LC
SC_F = 512
SC_BLK = 8
NBLK = SC_STEPS // SC_BLK
WBLK = SC_W // SC_BLK
BLKC = SC_BLK * SC_F
SC_H = SC_F // 2


class S:

    def __init__(self, h):
        self.h = h
        self.n = 0

    def inc(self, ins, k=1):
        self.n += k
        ins.then_inc(self.h, k)
        return ins


class Pool:
    def __init__(self, nc, es, n=48):
        self.sems = [S(es.enter_context(nc.semaphore(f"sp{i}"))) for i in range(n)]
        self.i = 0

    def take(self):
        s = self.sems[self.i % len(self.sems)]
        self.i += 1
        return s


class Ph:
    def __init__(self, nc, pool, deps):
        self.nc = nc
        self.pool = pool
        self.deps = deps
        self.done = []

    def sem(self):
        return self.pool.take()

    def entry(self, eng):
        for (s, c) in self.deps:
            eng.wait_ge(s.h, c)


_LN_N = [0]


def ph_ln(nc, es, pool, deps, Tn, x_d, gam_d, bet_d, xn_d, z_d, double):
    ph = Ph(nc, pool, deps)
    NTIL = Tn // 128
    _LN_N[0] += 1
    pfx = f"lnp{_LN_N[0]}"
    with ExitStack() as ees:
        E = ees.enter_context
        x_sb = E(nc.sbuf_tensor(pfx + "x", [128, 2 * D], F32))
        xn_sb = E(nc.sbuf_tensor(pfx + "xn", [128, 2 * D], F32))
        z_sb = E(nc.sbuf_tensor(pfx + "z", [128, 2 * D], BF16))
        g_sb = E(nc.sbuf_tensor(pfx + "g", [128, D], F32))
        be_sb = E(nc.sbuf_tensor(pfx + "be", [128, D], F32))
        sq_sb = E(nc.sbuf_tensor(pfx + "sq", [128, D], F32))
        st_sb = E(nc.sbuf_tensor(pfx + "st", [128, 16], F32))
        eps_sb = E(nc.sbuf_tensor(pfx + "eps", [128, 1], F32))
        cld = ph.sem()
        lde, ldo = ph.sem(), ph.sem()
        ld = [lde, ldo]
        sq1 = ph.sem()
        sq2 = ph.sem()
        rst = ph.sem()
        xnd = ph.sem()
        zd = ph.sem()
        sx = [ph.sem(), ph.sem()]
        sz = [ph.sem(), ph.sem()]
        vv = ph.sem()
        b = [s.n for s in (cld, lde, ldo, sq1, sq2, rst, xnd, zd, sx[0], sx[1],
                           sz[0], sz[1], vv)]
        (bcld, blde, bldo, bsq1, bsq2, brst, bxnd, bzd, bsx0, bsx1,
         bsz0, bsz1, bvv) = b
        bld = [blde, bldo]
        bsx_ = [bsx0, bsx1]
        bsz_ = [bsz0, bsz1]
        block = E(nc.Block())

        @block.sync
        def _(sy):
            ph.entry(sy)
            cld.inc(sy.dma_start(out=g_sb[:], in_=gam_d), 16)
            cld.inc(sy.dma_start(out=be_sb[:], in_=bet_d), 16)
            for i in range(NTIL):
                par = i % 2
                if i >= 2:
                    sy.wait_ge(xnd.h, bxnd + i - 1)
                ld[par].inc(sy.dma_start(out=x_sb[:, par * D:(par + 1) * D],
                                         in_=x_d[i * 128:(i + 1) * 128, :]), 16)
                sy.wait_ge(zd.h, bzd + i)
                if i >= 1:
                    sz[(i - 1) % 2].inc(
                        sy.dma_start(out=z_d[(i - 1) * 128:i * 128, :],
                                     in_=z_sb[:, ((i - 1) % 2) * D:((i - 1) % 2) * D + D]),
                        16)
            sy.wait_ge(zd.h, bzd + NTIL)
            sz[(NTIL - 1) % 2].inc(
                sy.dma_start(out=z_d[(NTIL - 1) * 128:NTIL * 128, :],
                             in_=z_sb[:, ((NTIL - 1) % 2) * D:((NTIL - 1) % 2) * D + D]),
                16)

        @block.gpsimd
        def _(g):
            ph.entry(g)
            for i in range(NTIL):
                par = i % 2
                g.wait_ge(xnd.h, bxnd + i + 1)
                sx[par].inc(g.dma_start(out=xn_d[i * 128:(i + 1) * 128, :],
                                        in_=xn_sb[:, par * D:(par + 1) * D]), 16)

        @block.scalar
        def _(s):
            ph.entry(s)
            for i in range(NTIL):
                par = i % 2
                p0 = 8 * par
                s.wait_ge(ld[par].h, bld[par] + 16 * (i // 2 + 1))
                sq1.inc(s.activation(sq_sb[:, 0:D], x_sb[:, par * D:(par + 1) * D],
                                     AF.Square, accum_out=st_sb[:, p0 + 0:p0 + 1]))
                s.wait_ge(rst.h, brst + (4 * i if double else 2 * i) + 1)
                sq1.inc(s.activation(st_sb[:, p0 + 3:p0 + 4], st_sb[:, p0 + 3:p0 + 4],
                                     AF.Sqrt, bias=eps_sb[:, 0:1]))
                if double:
                    s.wait_ge(xnd.h, bxnd + i + 1)
                    sq2.inc(s.activation(sq_sb[:, 0:D], xn_sb[:, par * D:(par + 1) * D],
                                         AF.Square, accum_out=st_sb[:, p0 + 4:p0 + 5]))
                    s.wait_ge(rst.h, brst + 4 * i + 3)
                    sq2.inc(s.activation(st_sb[:, p0 + 7:p0 + 8], st_sb[:, p0 + 7:p0 + 8],
                                         AF.Sqrt, bias=eps_sb[:, 0:1]))

        @block.vector
        def _(v):
            ph.entry(v)
            v.memset(eps_sb[:], LN_EPS)
            vvc = [0]

            def tick(ins):
                vvc[0] += 1
                vv.inc(ins)

            def syn():
                v.wait_ge(vv.h, bvv + vvc[0])

            for i in range(NTIL):
                par = i % 2
                p0 = 8 * par
                xap = x_sb[:, par * D:(par + 1) * D]
                xnap = xn_sb[:, par * D:(par + 1) * D]
                zap = z_sb[:, par * D:(par + 1) * D]
                v.wait_ge(cld.h, bcld + 32)
                v.wait_ge(ld[par].h, bld[par] + 16 * (i // 2 + 1))
                tick(v.reduce_sum(st_sb[:, p0 + 1:p0 + 2], xap, axis=AX.X))
                syn()
                tick(v.tensor_scalar_mul(st_sb[:, p0 + 1:p0 + 2],
                                         st_sb[:, p0 + 1:p0 + 2], 1.0 / D))
                v.wait_ge(sq1.h, bsq1 + 2 * i + 1)
                syn()
                tick(v.tensor_mul(st_sb[:, p0 + 2:p0 + 3], st_sb[:, p0 + 1:p0 + 2],
                                  st_sb[:, p0 + 1:p0 + 2]))
                syn()
                rst.inc(v.scalar_tensor_tensor(st_sb[:, p0 + 3:p0 + 4],
                                               st_sb[:, p0 + 0:p0 + 1], 1.0 / D,
                                               st_sb[:, p0 + 2:p0 + 3],
                                               ALU.mult, ALU.subtract))
                v.wait_ge(sq1.h, bsq1 + 2 * i + 2)
                tick(v.reciprocal(st_sb[:, p0 + 3:p0 + 4], st_sb[:, p0 + 3:p0 + 4]))
                rst.inc(v.engine_nop())
                if i >= 2:
                    v.wait_ge(sx[par].h, bsx_[par] + 16 * (i // 2))
                syn()
                xnd.inc(v.tensor_scalar(xnap, xap, st_sb[:, p0 + 1:p0 + 2],
                                        st_sb[:, p0 + 3:p0 + 4], ALU.subtract, ALU.mult))
                if i >= 2:
                    v.wait_ge(sz[par].h, bsz_[par] + 16 * (i // 2))
                if double:
                    tick(v.reduce_sum(st_sb[:, p0 + 5:p0 + 6], xnap, axis=AX.X))
                    syn()
                    tick(v.tensor_scalar_mul(st_sb[:, p0 + 5:p0 + 6],
                                             st_sb[:, p0 + 5:p0 + 6], 1.0 / D))
                    v.wait_ge(sq2.h, bsq2 + 2 * i + 1)
                    syn()
                    tick(v.tensor_mul(st_sb[:, p0 + 6:p0 + 7], st_sb[:, p0 + 5:p0 + 6],
                                      st_sb[:, p0 + 5:p0 + 6]))
                    syn()
                    rst.inc(v.scalar_tensor_tensor(st_sb[:, p0 + 7:p0 + 8],
                                                   st_sb[:, p0 + 4:p0 + 5], 1.0 / D,
                                                   st_sb[:, p0 + 6:p0 + 7],
                                                   ALU.mult, ALU.subtract))
                    v.wait_ge(sq2.h, bsq2 + 2 * i + 2)
                    tick(v.reciprocal(st_sb[:, p0 + 7:p0 + 8], st_sb[:, p0 + 7:p0 + 8]))
                    rst.inc(v.engine_nop())
                    syn()
                    v.tensor_scalar(zap, xnap, st_sb[:, p0 + 5:p0 + 6],
                                    st_sb[:, p0 + 7:p0 + 8], ALU.subtract, ALU.mult)
                    v.tensor_mul(zap, zap, g_sb[:])
                else:
                    v.tensor_mul(zap, xnap, g_sb[:])
                zd.inc(v.tensor_add(zap, zap, be_sb[:]))

        ph.done = [(sx[0], sx[0].n), (sx[1], sx[1].n),
                   (sz[0], sz[0].n), (sz[1], sz[1].n)]
    return ph



def ph_mmb(nc, es, pool, deps, name, Ttok, K, Nout, lhs, w_d, out_d,
           out_dtype=BF16, bias_d=None, acts=None, relu_sq=False, extras=()):
    ph = Ph(nc, pool, deps)
    KC = K // 128
    TG = Ttok // 512
    NT2 = max(1, Nout // 512)
    NW = min(Nout, 512)
    MCH = Ttok // 128
    NE = len(extras)
    kind, src_d = lhs
    use_scalar = (acts is not None) or relu_sq
    assert not (use_scalar and NE)
    with ExitStack() as ees:
        E = ees.enter_context
        w_sb = E(nc.sbuf_tensor(name + "w", [128, KC * Nout], BF16))
        a_sb = E(nc.sbuf_tensor(name + "a", [128, 2 * KC * 512], BF16))
        bias_sb = E(nc.sbuf_tensor(name + "b", [128, Nout], F32)) if bias_d is not None else None
        e_sb = E(nc.sbuf_tensor(name + "e", [128, max(1, 2 * NE * Nout)], F32))
        t_sb = E(nc.sbuf_tensor(name + "t", [128, 2 * NW], F32))
        t2_sb = E(nc.sbuf_tensor(name + "t2", [128, 2 * NW], F32))
        o_sb = E(nc.sbuf_tensor(name + "o", [128, 2 * Nout], out_dtype))
        ps = E(nc.psum_tensor(name + "ps", [128, 4 * 512], F32))
        wld = ph.sem(); ald = [ph.sem(), ph.sem()]; eld = [ph.sem(), ph.sem()]
        mmd = ph.sem(); vps = ph.sem(); sev = ph.sem(); vev = ph.sem()
        stp = [ph.sem(), ph.sem()]
        bwld = wld.n; bald = [x.n for x in ald]; beld = [x.n for x in eld]
        bmmd = mmd.n; bvps = vps.n; bsev = sev.n; bvev = vev.n
        bstp = [x.n for x in stp]
        n_w_dma = KC + (1 if bias_d is not None else 0)
        fin, bfin, finpm = (sev, bsev, NT2) if use_scalar else (vev, bvev, 1)
        block = E(nc.Block())

        @block.gpsimd
        def _(g):
            ph.entry(g)
            for k in range(KC):
                wld.inc(g.dma_start(out=w_sb[:, k * Nout:(k + 1) * Nout],
                                    in_=w_d[k * 128:(k + 1) * 128, :]), 16)
            if bias_d is not None:
                wld.inc(g.dma_start(out=bias_sb[:], in_=bias_d), 16)
            for mt in range(MCH):
                par = mt % 2
                for ei, (rmap, op) in enumerate(extras):
                    eld[par].inc(g.dma_start(
                        out=e_sb[:, (par * NE + ei) * Nout:(par * NE + ei + 1) * Nout],
                        in_=rmap(mt)), 16)
                g.wait_ge(fin.h, bfin + finpm * (mt + 1))
                stp[par].inc(g.dma_start(out=out_d[mt * 128:(mt + 1) * 128, :],
                                         in_=o_sb[:, par * Nout:(par + 1) * Nout]), 16)

        @block.sync
        def _(sy):
            ph.entry(sy)
            for gi in range(TG):
                par = gi % 2
                if gi >= 2:
                    sy.wait_ge(mmd.h, bmmd + (gi - 1) * 4 * NT2)
                if kind == "t2":
                    for k in range(KC):
                        ald[par].inc(sy.dma_start_transpose(
                            a_sb[:, par * KC * 512 + k * 512:par * KC * 512 + (k + 1) * 512],
                            src_d[gi * 512:(gi + 1) * 512, k * 128:(k + 1) * 128]), 16)
                else:
                    ald[par].inc(sy.dma_start(
                        out=a_sb[:, par * KC * 512:(par + 1) * KC * 512]
                            .rearrange("p (k n) -> p k n", k=KC),
                        in_=src_d.rearrange("(k p) t -> p k t", p=128)
                        [:, :, gi * 512:(gi + 1) * 512]), 16)


        @block.tensor
        def _(t):
            ph.entry(t)
            t.wait_ge(wld.h, bwld + 16 * n_w_dma)
            for gi in range(TG):
                par = gi % 2
                nld = KC if kind == "t2" else 1
                t.wait_ge(ald[par].h, bald[par] + 16 * nld * (gi // 2 + 1))
                for m in range(4):
                    for nt in range(NT2):
                        idx = (gi * 4 + m) * NT2 + nt
                        bank = idx % 4
                        if idx >= 4:
                            t.wait_ge(vps.h, bvps + idx - 3)
                        for k in range(KC):
                            ins = t.matmul(
                                ps[:, bank * 512:bank * 512 + NW],
                                a_sb[:, par * KC * 512 + k * 512 + m * 128:
                                        par * KC * 512 + k * 512 + (m + 1) * 128],
                                w_sb[:, k * Nout + nt * NW:k * Nout + nt * NW + NW],
                                start=(k == 0), stop=(k == KC - 1))
                        mmd.inc(ins)

        @block.vector
        def _(v):
            ph.entry(v)
            for mt in range(MCH):
                par = mt % 2
                if NE:
                    v.wait_ge(eld[par].h, beld[par] + 16 * NE * (mt // 2 + 1))
                if mt >= 2:
                    v.wait_ge(stp[par].h, bstp[par] + 16 * (mt // 2))
                for nt in range(NT2):
                    idx = mt * NT2 + nt
                    v.wait_ge(mmd.h, bmmd + idx + 1)
                    if use_scalar and idx >= 2:
                        v.wait_ge(sev.h, bsev + idx - 1)
                    tgt = (t_sb[:, (idx % 2) * NW:(idx % 2) * NW + NW]
                           if (use_scalar or NE)
                           else o_sb[:, par * Nout + nt * NW:par * Nout + nt * NW + NW])
                    if bias_d is not None:
                        ins = v.tensor_tensor(tgt, ps[:, (idx % 4) * 512:(idx % 4) * 512 + NW],
                                              bias_sb[:, nt * NW:nt * NW + NW], ALU.add)
                    else:
                        ins = v.tensor_copy(tgt, ps[:, (idx % 4) * 512:(idx % 4) * 512 + NW])
                    vps.inc(ins)
                    if NE:
                        src = t_sb[:, (idx % 2) * NW:(idx % 2) * NW + NW]
                        for ei, (rmap, op) in enumerate(extras):
                            eap = e_sb[:, (par * NE + ei) * Nout + nt * NW:
                                          (par * NE + ei) * Nout + nt * NW + NW]
                            last = (ei == NE - 1)
                            dst = (o_sb[:, par * Nout + nt * NW:par * Nout + nt * NW + NW]
                                   if last else src)
                            ins = v.tensor_tensor(dst, src, eap, op)
                    if not use_scalar and nt == NT2 - 1:
                        vev.inc(ins)

        @block.scalar
        def _(s):
            ph.entry(s)
            if not use_scalar:
                return
            for mt in range(MCH):
                par = mt % 2
                for nt in range(NT2):
                    idx = mt * NT2 + nt
                    s.wait_ge(vps.h, bvps + idx + 1)
                    if mt >= 2 and nt == 0:
                        s.wait_ge(stp[par].h, bstp[par] + 16 * (mt // 2))
                    ins = None
                    for (lo, hi, fn) in acts:
                        if hi <= nt * NW or lo >= (nt + 1) * NW:
                            continue
                        clo = max(lo - nt * NW, 0)
                        chi = min(hi - nt * NW, NW)
                        srcap = t_sb[:, (idx % 2) * NW + clo:(idx % 2) * NW + chi]
                        if relu_sq:
                            dst = t2_sb[:, (idx % 2) * NW + clo:(idx % 2) * NW + chi]
                        else:
                            dst = o_sb[:, par * Nout + nt * NW + clo:
                                          par * Nout + nt * NW + chi]
                        ins = s.activation(dst, srcap, fn)
                    if relu_sq:
                        ins = s.activation(o_sb[:, par * Nout + nt * NW:
                                                   par * Nout + nt * NW + NW],
                                           t2_sb[:, (idx % 2) * NW:(idx % 2) * NW + NW],
                                           AF.Square)
                    sev.inc(ins)

        ph.done = [(stp[0], stp[0].n), (stp[1], stp[1].n)]
    return ph


def ph_ug(nc, es, pool, deps, z_d, w_d, bias_d, mask_d, pu_d, pg_d):
    ph = Ph(nc, pool, deps)
    K, Nout = D, 2 * D
    KC, NT2, NW = 8, 4, 512
    TG = TOK // 512
    MCH = TOK // 128
    pu3 = pu_d.rearrange("(tl q) f -> tl q f", q=128)
    pg3 = pg_d.rearrange("(tl q) f -> tl q f", q=128)

    def targets(mt):
        b, cb = mt // 9, mt % 9
        tg = []
        if cb == 0:
            tg.append((0, 0))
        else:
            tg.append((cb - 1, SC_W))
            if cb <= 7:
                tg.append((cb, 0))
        return b, tg

    nst = [0, 0]
    st_after = []
    for mt in range(MCH):
        b, tg = targets(mt)
        nst[mt % 2] += 16 * 2 * len(tg)
        st_after.append((mt % 2, nst[0], nst[1]))

    with ExitStack() as ees:
        E = ees.enter_context
        w_sb = E(nc.sbuf_tensor("ugw", [128, KC * Nout], BF16))
        a_sb = E(nc.sbuf_tensor("uga", [128, 2 * KC * 512], BF16))
        bias_sb = E(nc.sbuf_tensor("ugb", [128, Nout], F32))
        t_sb = E(nc.sbuf_tensor("ugt", [128, 2 * NW], F32))
        o_sb = E(nc.sbuf_tensor("ugo", [128, 2 * Nout], BF16))
        m_sb = E(nc.sbuf_tensor("ugm", [128, 1], F32))
        ps = E(nc.psum_tensor("ugps", [128, 4 * 512], F32))
        wld = ph.sem(); ald = [ph.sem(), ph.sem()]
        mmd = ph.sem(); vps = ph.sem(); sev = ph.sem(); vev = ph.sem()
        stp = [ph.sem(), ph.sem()]
        bwld = wld.n; bald = [x.n for x in ald]; bmmd = mmd.n; bvps = vps.n
        bsev = sev.n; bvev = vev.n; bstp = [x.n for x in stp]
        block = E(nc.Block())

        @block.gpsimd
        def _(g):
            ph.entry(g)
            for k in range(KC):
                wld.inc(g.dma_start(out=w_sb[:, k * Nout:(k + 1) * Nout],
                                    in_=w_d[k * 128:(k + 1) * 128, :]), 16)
            wld.inc(g.dma_start(out=bias_sb[:], in_=bias_d), 16)
            wld.inc(g.dma_start(out=m_sb[:], in_=mask_d), 16)
            for mt in range(MCH):
                par = mt % 2
                b, tg = targets(mt)
                ispre = (mt % 9 == 0)
                if ispre:
                    g.wait_ge(vev.h, bvev + len([m for m in range(mt + 1) if m % 9 == 0]))
                else:
                    g.wait_ge(sev.h, bsev + NT2 * (mt + 1))
                for (k, tlb) in tg:
                    q0 = k * 16 + b * 2
                    srcu = o_sb[:, par * Nout:par * Nout + 1024] \
                        .rearrange("p (gb f) -> p gb f", gb=2)
                    srcg = o_sb[:, par * Nout + 1024:par * Nout + 2048] \
                        .rearrange("p (gb f) -> p gb f", gb=2)
                    stp[par].inc(g.dma_start(out=pu3[tlb:tlb + 128, q0:q0 + 2, :],
                                             in_=srcu), 16)
                    stp[par].inc(g.dma_start(out=pg3[tlb:tlb + 128, q0:q0 + 2, :],
                                             in_=srcg), 16)

        @block.sync
        def _(sy):
            ph.entry(sy)
            for gi in range(TG):
                par = gi % 2
                if gi >= 2:
                    sy.wait_ge(mmd.h, bmmd + (gi - 1) * 4 * NT2)
                for k in range(KC):
                    ald[par].inc(sy.dma_start_transpose(
                        a_sb[:, par * KC * 512 + k * 512:par * KC * 512 + (k + 1) * 512],
                        z_d[gi * 512:(gi + 1) * 512, k * 128:(k + 1) * 128]), 16)

        @block.tensor
        def _(t):
            ph.entry(t)
            t.wait_ge(wld.h, bwld + 16 * (KC + 2))
            for gi in range(TG):
                par = gi % 2
                t.wait_ge(ald[par].h, bald[par] + 16 * KC * (gi // 2 + 1))
                for m in range(4):
                    for nt in range(NT2):
                        idx = (gi * 4 + m) * NT2 + nt
                        bank = idx % 4
                        if idx >= 4:
                            t.wait_ge(vps.h, bvps + idx - 3)
                        for k in range(KC):
                            ins = t.matmul(
                                ps[:, bank * 512:bank * 512 + NW],
                                a_sb[:, par * KC * 512 + k * 512 + m * 128:
                                        par * KC * 512 + k * 512 + (m + 1) * 128],
                                w_sb[:, k * Nout + nt * NW:k * Nout + nt * NW + NW],
                                start=(k == 0), stop=(k == KC - 1))
                        mmd.inc(ins)

        @block.vector
        def _(v):
            ph.entry(v)
            for mt in range(MCH):
                par = mt % 2
                ispre = (mt % 9 == 0)
                if mt >= 2:
                    pp, c0_, c1_ = st_after[mt - 2]
                    v.wait_ge(stp[par].h, bstp[par] + (c0_ if par == 0 else c1_))
                for nt in range(NT2):
                    idx = mt * NT2 + nt
                    v.wait_ge(mmd.h, bmmd + idx + 1)
                    if idx >= 2:
                        v.wait_ge(sev.h, bsev + idx - 1)
                    vps.inc(v.tensor_tensor(t_sb[:, (idx % 2) * NW:(idx % 2) * NW + NW],
                                            ps[:, (idx % 4) * 512:(idx % 4) * 512 + NW],
                                            bias_sb[:, nt * NW:nt * NW + NW], ALU.add))
                if ispre:
                    v.wait_ge(sev.h, bsev + NT2 * (mt + 1))
                    vev.inc(v.tensor_scalar_mul(o_sb[:, par * Nout:(par + 1) * Nout],
                                                o_sb[:, par * Nout:(par + 1) * Nout],
                                                m_sb[:, 0:1]))

        @block.scalar
        def _(s):
            ph.entry(s)
            for mt in range(MCH):
                par = mt % 2
                for nt in range(NT2):
                    idx = mt * NT2 + nt
                    s.wait_ge(vps.h, bvps + idx + 1)
                    if mt >= 2 and nt == 0:
                        pp, c0_, c1_ = st_after[mt - 2]
                        s.wait_ge(stp[par].h, bstp[par] + (c0_ if par == 0 else c1_))
                    fn = AF.Copy if nt < 2 else AF.Sigmoid
                    sev.inc(s.activation(o_sb[:, par * Nout + nt * NW:
                                                 par * Nout + nt * NW + NW],
                                         t_sb[:, (idx % 2) * NW:(idx % 2) * NW + NW], fn))

        ph.done = [(stp[0], stp[0].n), (stp[1], stp[1].n)]
    return ph


def ph_scan0(nc, es, pool, deps, pu_d, pg_d, a_d, hs_d):
    ph = Ph(nc, pool, deps)
    H = SC_H
    HW = 32
    NW_ = SC_STEPS // HW
    WOWN = SC_W // HW
    HWC = HW * SC_F
    puq = pu_d.rearrange("(tl q) f -> q tl f", q=128)
    pgq = pg_d.rearrange("(tl q) f -> q tl f", q=128)
    hs5 = hs_d.rearrange("(b k t) (gb f) -> b k gb t f", b=8, k=8, gb=2)
    with ExitStack() as ees:
        E = ees.enter_context
        u_sb = E(nc.sbuf_tensor("scu", [128, 2 * BLKC], BF16))
        g_sb = E(nc.sbuf_tensor("scg", [128, 2 * BLKC], BF16))
        h_sb = E(nc.sbuf_tensor("sch", [128, 2 * HWC], BF16))
        a_sb = E(nc.sbuf_tensor("sca", [128, SC_F], F32))
        z_sb = E(nc.sbuf_tensor("scz", [128, SC_F], BF16))
        v_sb = E(nc.sbuf_tensor("scv", [128, SC_F], F32))
        c_sb = E(nc.sbuf_tensor("scc", [128, SC_F], BF16))
        d_sb = E(nc.sbuf_tensor("scd", [128, SC_F], BF16))
        cld = ph.sem()
        ld = [ph.sem(), ph.sem()]
        vs = ph.sem(); cs = ph.sem()
        st3 = [ph.sem(), ph.sem(), ph.sem()]
        bcld = cld.n; bld = [x.n for x in ld]; bvs = vs.n; bcs = cs.n
        bst3 = [x.n for x in st3]
        shares = [list(range(0, 22)), list(range(22, 43)), list(range(43, 64))]
        block = E(nc.Block())

        def emit_spills(eng, qi, w):
            t0 = w * HW - SC_W
            par = w % 2
            eng.wait_ge(vs.h, bvs + 4 * HW * (w + 1))
            for pair in shares[qi]:
                k, b = pair // 8, pair % 8
                q0 = k * 16 + b * 2
                st3[qi].inc(eng.dma_start(
                    out=hs5[b, k, :, t0:t0 + HW, :],
                    in_=h_sb[par * HWC:par * HWC, 0:0] if False else
                        h_sb[q0:q0 + 2, par * HWC:(par + 1) * HWC]
                        .rearrange("p (t f) -> p t f", t=HW)), 16)

        @block.sync
        def _(sy):
            ph.entry(sy)
            cld.inc(sy.dma_start(out=a_sb[:], in_=a_d), 16)
            for w in range(WOWN, NW_):
                emit_spills(sy, 1, w)

        @block.scalar
        def _(s_):
            ph.entry(s_)
            spw = iter(range(WOWN, NW_))
            nxt = next(spw, None)
            for s in range(SC_STEPS):
                for hf in range(2):
                    c0, c1 = hf * H, (hf + 1) * H
                    s_.wait_ge(vs.h, bvs + 4 * s + 1 + hf)
                    cs.inc(s_.activation(c_sb[:, c0:c1], v_sb[:, c0:c1], AF.Tanh))
                if nxt is not None and s == (nxt + 1) * HW - 1:
                    emit_spills(s_, 2, nxt)
                    nxt = next(spw, None)

        @block.gpsimd
        def _(g):
            ph.entry(g)
            for blk in range(NBLK):
                par = blk % 2
                if blk >= 2:
                    g.wait_ge(vs.h, bvs + 4 * SC_BLK * (blk - 1))
                ld[par].inc(g.dma_start(
                    out=u_sb[:, par * BLKC:(par + 1) * BLKC]
                        .rearrange("p (t f) -> p t f", t=SC_BLK),
                    in_=puq[:, blk * SC_BLK:(blk + 1) * SC_BLK, :]), 16)
                ld[par].inc(g.dma_start(
                    out=g_sb[:, par * BLKC:(par + 1) * BLKC]
                        .rearrange("p (t f) -> p t f", t=SC_BLK),
                    in_=pgq[:, blk * SC_BLK:(blk + 1) * SC_BLK, :]), 16)
                if blk % 4 == 0 and blk // 4 - 1 >= WOWN:
                    emit_spills(g, 0, blk // 4 - 1)
            for w in range(NBLK // 4 - 1, NW_):
                if w >= WOWN:
                    emit_spills(g, 0, w)

        @block.vector
        def _(v):
            ph.entry(v)
            v.memset(z_sb[:], 0.0)
            for s in range(SC_STEPS):
                blk = s // SC_BLK
                sl = s % SC_BLK
                par = blk % 2
                w = s // HW
                wl = s % HW
                wpar = w % 2
                if sl == 0:
                    if blk == 0:
                        v.wait_ge(cld.h, bcld + 16)
                    v.wait_ge(ld[par].h, bld[par] + 32 * (blk // 2 + 1))
                if wl == 0 and w >= 2 and w - 2 >= WOWN:
                    for qi in range(3):
                        nsp = len(shares[qi]) * 16 * (w - 2 - WOWN + 1)
                        v.wait_ge(st3[qi].h, bst3[qi] + nsp)
                if s == 0:
                    hprev = z_sb[:]
                elif wl == 0:
                    hprev = h_sb[:, (1 - wpar) * HWC + (HW - 1) * SC_F:
                                    (1 - wpar) * HWC + HW * SC_F]
                else:
                    hprev = h_sb[:, wpar * HWC + (wl - 1) * SC_F:wpar * HWC + wl * SC_F]
                uap = u_sb[:, par * BLKC + sl * SC_F:par * BLKC + (sl + 1) * SC_F]
                gap = g_sb[:, par * BLKC + sl * SC_F:par * BLKC + (sl + 1) * SC_F]
                hap = h_sb[:, wpar * HWC + wl * SC_F:wpar * HWC + (wl + 1) * SC_F]
                for hf in range(2):
                    c0, c1 = hf * H, (hf + 1) * H
                    if s >= 1:
                        v.wait_ge(cs.h, bcs + 2 * (s - 1) + hf + 1)
                    v.tensor_mul(v_sb[:, c0:c1], a_sb[:, c0:c1], hprev[:, c0:c1])
                    vs.inc(v.tensor_add(v_sb[:, c0:c1], v_sb[:, c0:c1], uap[:, c0:c1]))
                for hf in range(2):
                    c0, c1 = hf * H, (hf + 1) * H
                    v.wait_ge(cs.h, bcs + 2 * s + hf + 1)
                    v.tensor_sub(d_sb[:, c0:c1], hprev[:, c0:c1], c_sb[:, c0:c1])
                    v.tensor_mul(d_sb[:, c0:c1], gap[:, c0:c1], d_sb[:, c0:c1])
                    vs.inc(v.tensor_add(hap[:, c0:c1], d_sb[:, c0:c1], c_sb[:, c0:c1]))

        ph.done = [(st3[i], st3[i].n) for i in range(3)]
    return ph


def ph_shift(nc, es, pool, deps, xn_d, mur_d, muk_d, xr_d, xk_d):
    ph = Ph(nc, pool, deps)
    with ExitStack() as ees:
        E = ees.enter_context
        xc_sb = E(nc.sbuf_tensor("shx", [128, 2 * RB], BF16))
        d_sb = E(nc.sbuf_tensor("shd", [128, ROWS], F32))
        o_sb = E(nc.sbuf_tensor("sho", [128, 2 * 2 * ROWS], BF16))
        mu_sb = E(nc.sbuf_tensor("shmu", [128, 16], F32))
        cld = ph.sem(); tld = [ph.sem(), ph.sem()]
        dn = ph.sem(); stp = [ph.sem(), ph.sem()]
        bcld = cld.n; btld = [x.n for x in tld]; bdn = dn.n; bstp = [x.n for x in stp]
        block = E(nc.Block())

        @block.sync
        def _(sy):
            ph.entry(sy)
            cld.inc(sy.dma_start(out=mu_sb[:, 0:8], in_=mur_d), 16)
            cld.inc(sy.dma_start(out=mu_sb[:, 8:16], in_=muk_d), 16)
            for i in range(64):
                c, b = i // 8, i % 8
                par = i % 2
                if i >= 2:
                    sy.wait_ge(dn.h, bdn + i - 1)
                tld[par].inc(sy.dma_start_transpose(
                    xc_sb[:, par * RB:(par + 1) * RB],
                    xn_d[b * RB:(b + 1) * RB, c * 128:(c + 1) * 128]), 16)

        @block.vector
        def _(v):
            ph.entry(v)
            v.wait_ge(cld.h, bcld + 32)
            for i in range(64):
                c, b = i // 8, i % 8
                par = i % 2
                v.wait_ge(tld[par].h, btld[par] + 16 * (i // 2 + 1))
                if i >= 2:
                    v.wait_ge(stp[par].h, bstp[par] + 32 * (i // 2))
                xo = xc_sb[:, par * RB + PRE:par * RB + RB]
                xp = xc_sb[:, par * RB + PRE - 1:par * RB + RB - 1]
                v.tensor_sub(d_sb[:], xo, xp)
                v.scalar_tensor_tensor(o_sb[:, par * 2 * ROWS:par * 2 * ROWS + ROWS],
                                       d_sb[:], mu_sb[:, c:c + 1], xp,
                                       ALU.mult, ALU.add)
                dn.inc(v.scalar_tensor_tensor(
                    o_sb[:, par * 2 * ROWS + ROWS:par * 2 * ROWS + 2 * ROWS],
                    d_sb[:], mu_sb[:, 8 + c:8 + c + 1], xp, ALU.mult, ALU.add))

        @block.scalar
        def _(s):
            ph.entry(s)
            for i in range(64):
                c, b = i // 8, i % 8
                par = i % 2
                s.wait_ge(dn.h, bdn + i + 1)
                stp[par].inc(s.dma_start(out=xr_d[c * 128:(c + 1) * 128,
                                                  b * ROWS:(b + 1) * ROWS],
                                         in_=o_sb[:, par * 2 * ROWS:par * 2 * ROWS + ROWS]),
                             16)
                stp[par].inc(s.dma_start(out=xk_d[c * 128:(c + 1) * 128,
                                                  b * ROWS:(b + 1) * ROWS],
                                         in_=o_sb[:, par * 2 * ROWS + ROWS:
                                                     par * 2 * ROWS + 2 * ROWS]), 16)

        ph.done = [(stp[0], stp[0].n), (stp[1], stp[1].n)]
    return ph

def ph_au_states(nc, es, pool, deps, h_d, wau_d, oh_d, agin_d, agout_d, states_d):
    ph = Ph(nc, pool, deps)
    KC = 16
    TG = TOWN // 512
    with ExitStack() as ees:
        E = ees.enter_context
        w_sb = E(nc.sbuf_tensor("auw", [128, KC * 64], BF16))
        a_sb = E(nc.sbuf_tensor("aua", [128, 2 * KC * 512], BF16))
        au_sb = E(nc.sbuf_tensor("auau", [64, TOWN], F32))
        al_sb = E(nc.sbuf_tensor("aual", [128, ROWS], F32))
        ul_sb = E(nc.sbuf_tensor("auul", [128, ROWS], F32))
        ones_sb = E(nc.sbuf_tensor("auon", [128, ROWS], F32))
        sl_sb = E(nc.sbuf_tensor("ausl", [128, ROWS], F32))
        cl_sb = E(nc.sbuf_tensor("aucl", [128, ROWS], F32))
        agi_sb = E(nc.sbuf_tensor("auag", [128, 2], F32))
        aa_sb = E(nc.sbuf_tensor("auaa", [128, 8], F32))
        uu_sb = E(nc.sbuf_tensor("auuu", [128, 8], F32))
        ss_sb = E(nc.sbuf_tensor("auss", [128, 8], F32))
        oh_sb = E(nc.sbuf_tensor("auoh", [128, 8], F32))
        si_sb = E(nc.sbuf_tensor("ausi", [128, 1], F32))
        st_out = E(nc.sbuf_tensor("aust", [128, ROWS], F32))
        ps = E(nc.psum_tensor("aups", [64, 2 * 512], F32))
        wld = ph.sem(); ald = [ph.sem(), ph.sem()]
        mmd = ph.sem(); sev = ph.sem(); locd = ph.sem(); vv = ph.sem()
        cc = ph.sem(); agl = ph.sem(); std = ph.sem()
        bwld = wld.n; bald = [x.n for x in ald]; bmmd = mmd.n; bsev = sev.n
        blocd = locd.n; bvv = vv.n; bcc = cc.n; bagl = agl.n; bstd = std.n
        block = E(nc.Block())

        @block.sync
        def _(sy):
            ph.entry(sy)
            for gi in range(TG):
                par = gi % 2
                if gi >= 2:
                    sy.wait_ge(mmd.h, bmmd + gi - 1)
                for k in range(KC):
                    ald[par].inc(sy.dma_start_transpose(
                        a_sb[:, par * KC * 512 + k * 512:par * KC * 512 + (k + 1) * 512],
                        h_d[gi * 512:(gi + 1) * 512, k * 128:(k + 1) * 128]), 16)

        @block.tensor
        def _(t):
            ph.entry(t)
            t.wait_ge(wld.h, bwld + 16 * 2)
            for gi in range(TG):
                par = gi % 2
                t.wait_ge(ald[par].h, bald[par] + 16 * KC * (gi // 2 + 1))
                bank = gi % 2
                if gi >= 2:
                    t.wait_ge(sev.h, bsev + gi - 1)
                for k in range(KC):
                    ins = t.matmul(ps[:, bank * 512:(bank + 1) * 512],
                                   w_sb[:, k * 64:(k + 1) * 64],
                                   a_sb[:, par * KC * 512 + k * 512:
                                           par * KC * 512 + (k + 1) * 512],
                                   start=(k == 0), stop=(k == KC - 1))
                mmd.inc(ins)

        @block.scalar
        def _(s):
            ph.entry(s)
            for gi in range(TG):
                bank = gi % 2
                s.wait_ge(mmd.h, bmmd + gi + 1)
                s.activation(au_sb[0:16, gi * 512:(gi + 1) * 512],
                             ps[0:16, bank * 512:(bank + 1) * 512], AF.Sigmoid)
                sev.inc(s.activation(au_sb[32:48, gi * 512:(gi + 1) * 512],
                                     ps[32:48, bank * 512:(bank + 1) * 512], AF.Copy))

        @block.gpsimd
        def _(g):
            ph.entry(g)
            wld.inc(g.dma_start(out=w_sb[:, 0:KC * 64].rearrange("p (k n) -> p k n", k=KC),
                                in_=wau_d.rearrange("(k p) n -> p k n", p=128)), 16)
            wld.inc(g.dma_start(out=oh_sb[:], in_=oh_d), 16)
            g.wait_ge(sev.h, bsev + TG)
            for b in range(8):
                locd.inc(g.dma_start(out=al_sb[b * 16:(b + 1) * 16, :],
                                     in_=au_sb[0:16, b * ROWS:(b + 1) * ROWS]), 16)
                locd.inc(g.dma_start(out=ul_sb[b * 16:(b + 1) * 16, :],
                                     in_=au_sb[32:48, b * ROWS:(b + 1) * ROWS]), 16)
            g.wait_ge(vv.h, bvv + 2)
            agl.inc(g.dma_start(out=agin_d, in_=agi_sb[:]), 16)
            g.wait_ge(agl.h, bagl + 16)
            cc.inc(g.collective_compute(
                "AllGather", ALU.bypass, replica_groups=[list(range(NCORES))],
                ins=[agin_d.opt()], outs=[agout_d.opt()]))
            g.wait_ge(cc.h, bcc + 1)
            for j in range(8):
                agl.inc(g.dma_start(out=aa_sb[:, j:j + 1],
                                    in_=agout_d[j * 128:(j + 1) * 128, 0:1],
                                    allow_slow_non_contiguous=True), 16)
                agl.inc(g.dma_start(out=uu_sb[:, j:j + 1],
                                    in_=agout_d[j * 128:(j + 1) * 128, 1:2],
                                    allow_slow_non_contiguous=True), 16)
            g.wait_ge(vv.h, bvv + 6)
            std.inc(g.dma_start(out=states_d, in_=st_out[:]), 16)
            g.wait_ge(std.h, bstd + 16)

        @block.vector
        def _(v):
            ph.entry(v)
            v.memset(ones_sb[:], 1.0)
            v.wait_ge(locd.h, blocd + 16 * 16)
            v.tensor_tensor_scan(sl_sb[:], al_sb[:], ul_sb[:], 0.0, ALU.mult, ALU.add)
            vv.inc(v.tensor_tensor_scan(cl_sb[:], al_sb[:], ones_sb[:], 1.0,
                                        ALU.mult, ALU.mult))
            v.wait_ge(vv.h, bvv + 1)
            v.tensor_copy(agi_sb[:, 0:1], cl_sb[:, ROWS - 1:ROWS])
            vv.inc(v.tensor_copy(agi_sb[:, 1:2], sl_sb[:, ROWS - 1:ROWS]))
            v.wait_ge(agl.h, bagl + 16 * 17)
            vv.inc(v.tensor_tensor_scan(ss_sb[:], aa_sb[:], uu_sb[:], 0.0,
                                        ALU.mult, ALU.add))
            v.wait_ge(vv.h, bvv + 3)
            vv.inc(v.tensor_mul(ss_sb[:], ss_sb[:], oh_sb[:]))
            v.wait_ge(vv.h, bvv + 4)
            vv.inc(v.reduce_sum(si_sb[:], ss_sb[:], axis=AX.X))
            v.wait_ge(vv.h, bvv + 5)
            vv.inc(v.scalar_tensor_tensor(st_out[:], cl_sb[:], si_sb[:, 0:1],
                                          sl_sb[:], ALU.mult, ALU.add))

        ph.done = [(std, std.n)]
    return ph


def ph_s1p(nc, es, pool, deps, states_d, cssm_d, woT_d, bout_d, xn1_d, s1p_d):
    ph = Ph(nc, pool, deps)
    with ExitStack() as ees:
        E = ees.enter_context
        c_sb = E(nc.sbuf_tensor("s1c", [128, 16 * 16], BF16))
        wo_sb = E(nc.sbuf_tensor("s1wo", [128, 16 * D], BF16))
        gf_sb = E(nc.sbuf_tensor("s1gf", [16, D], F32))
        gb_sb = E(nc.sbuf_tensor("s1gb", [16, D], BF16))
        sts_sb = E(nc.sbuf_tensor("s1st", [16, TOWN], BF16))
        bo_sb = E(nc.sbuf_tensor("s1bo", [128, D], F32))
        e_sb = E(nc.sbuf_tensor("s1e", [128, 2 * D], BF16))
        t_sb = E(nc.sbuf_tensor("s1t", [128, 2 * 512], F32))
        o_sb = E(nc.sbuf_tensor("s1o", [128, 2 * D], BF16))
        psg = E(nc.psum_tensor("s1pg", [16, 1024], F32))
        ps = E(nc.psum_tensor("s1ps", [128, 4 * 512], F32))
        wld = ph.sem(); eld = [ph.sem(), ph.sem()]; gdn = ph.sem()
        mmd = ph.sem(); vev = ph.sem(); stp = [ph.sem(), ph.sem()]
        bwld = wld.n; beld = [x.n for x in eld]; bgdn = gdn.n
        bmmd = mmd.n; bvev = vev.n; bstp = [x.n for x in stp]
        block = E(nc.Block())

        @block.gpsimd
        def _(g):
            ph.entry(g)
            for k in range(16):
                wld.inc(g.dma_start(out=c_sb[:, k * 16:(k + 1) * 16],
                                    in_=cssm_d[k * 128:(k + 1) * 128, :]), 16)
            for k in range(16):
                wld.inc(g.dma_start(out=wo_sb[:, k * D:(k + 1) * D],
                                    in_=woT_d[k * 128:(k + 1) * 128, :]), 16)
            wld.inc(g.dma_start(out=bo_sb[:], in_=bout_d), 16)
            wld.inc(g.dma_start(out=sts_sb[:].rearrange("p (b r) -> p b r", b=8),
                                in_=states_d.rearrange("(b n) r -> n b r", n=16)), 16)
            for mt in range(TOWN // 128):
                par = mt % 2
                g.wait_ge(vev.h, bvev + mt + 1)
                stp[par].inc(g.dma_start(out=s1p_d[mt * 128:(mt + 1) * 128, :],
                                         in_=o_sb[:, par * D:(par + 1) * D]), 16)

        @block.sync
        def _(sy):
            ph.entry(sy)
            for mt in range(TOWN // 128):
                par = mt % 2
                if mt >= 2:
                    sy.wait_ge(vev.h, bvev + mt - 1)
                eld[par].inc(sy.dma_start(out=e_sb[:, par * D:(par + 1) * D],
                                          in_=xn1_d[mt * 128:(mt + 1) * 128, :]), 16)

        @block.tensor
        def _(t):
            ph.entry(t)
            t.wait_ge(wld.h, bwld + 16 * 34)
            for k in range(16):
                ins = t.matmul(psg[:, 0:512], c_sb[:, k * 16:(k + 1) * 16],
                               wo_sb[:, k * D:k * D + 512], start=(k == 0), stop=(k == 15))
            for k in range(16):
                ins = t.matmul(psg[:, 512:1024], c_sb[:, k * 16:(k + 1) * 16],
                               wo_sb[:, k * D + 512:(k + 1) * D],
                               start=(k == 0), stop=(k == 15))
            mmd.inc(ins)
            t.wait_ge(gdn.h, bgdn + 1)
            for mt in range(TOWN // 128):
                for nt in range(2):
                    idx = mt * 2 + nt
                    bank = idx % 4
                    if idx >= 4:
                        t.wait_ge(vev.h, bvev + (idx - 3 + 1) // 2)
                    mmd.inc(t.matmul(ps[:, bank * 512:bank * 512 + 512],
                                     sts_sb[:, mt * 128:(mt + 1) * 128],
                                     gb_sb[:, nt * 512:(nt + 1) * 512],
                                     start=True, stop=True))

        @block.vector
        def _(v):
            ph.entry(v)
            v.wait_ge(mmd.h, bmmd + 1)
            v.tensor_copy(gf_sb[:], psg[:, 0:1024])
            gdn.inc(v.tensor_copy(gb_sb[:], gf_sb[:]))
            for mt in range(TOWN // 128):
                par = mt % 2
                v.wait_ge(eld[par].h, beld[par] + 16 * (mt // 2 + 1))
                if mt >= 2:
                    v.wait_ge(stp[par].h, bstp[par] + 16 * (mt // 2))
                for nt in range(2):
                    idx = mt * 2 + nt
                    bank = idx % 4
                    v.wait_ge(mmd.h, bmmd + idx + 2)
                    v.tensor_tensor(t_sb[:, nt * 512:(nt + 1) * 512],
                                    ps[:, bank * 512:bank * 512 + 512],
                                    bo_sb[:, nt * 512:(nt + 1) * 512], ALU.add)
                ins = v.tensor_tensor(o_sb[:, par * D:(par + 1) * D], t_sb[:, 0:1024],
                                      e_sb[:, par * D:(par + 1) * D], ALU.add)
                vev.inc(ins)

        ph.done = [(stp[0], stp[0].n), (stp[1], stp[1].n)]
    return ph


def ph_mma(nc, es, pool, deps, name, K, Mout, src_d, w_d, out_d):
    ph = Ph(nc, pool, deps)
    KC = K // 128
    MC = Mout // 128
    TG = TOWN // 512
    with ExitStack() as ees:
        E = ees.enter_context
        w_sb = E(nc.sbuf_tensor(name + "w", [128, KC * Mout], BF16))
        a_sb = E(nc.sbuf_tensor(name + "a", [128, 2 * KC * 512], BF16))
        o_sb = E(nc.sbuf_tensor(name + "o", [128, 2 * 512], BF16))
        ps = E(nc.psum_tensor(name + "ps", [128, 4 * 512], F32))
        wld = ph.sem(); ald = [ph.sem(), ph.sem()]
        mmd = ph.sem(); vev = ph.sem(); stp = [ph.sem(), ph.sem()]
        bwld = wld.n; bald = [x.n for x in ald]; bmmd = mmd.n; bvev = vev.n
        bstp = [x.n for x in stp]
        block = E(nc.Block())

        @block.gpsimd
        def _(g):
            ph.entry(g)
            for k in range(KC):
                wld.inc(g.dma_start(out=w_sb[:, k * Mout:(k + 1) * Mout],
                                    in_=w_d[k * 128:(k + 1) * 128, :]), 16)
            for gi in range(TG):
                for m in range(MC):
                    idx = gi * MC + m
                    par = idx % 2
                    g.wait_ge(vev.h, bvev + idx + 1)
                    stp[par].inc(g.dma_start(
                        out=out_d[m * 128:(m + 1) * 128, gi * 512:(gi + 1) * 512],
                        in_=o_sb[:, par * 512:(par + 1) * 512]), 16)

        @block.sync
        def _(sy):
            ph.entry(sy)
            for gi in range(TG):
                par = gi % 2
                if gi >= 2:
                    sy.wait_ge(mmd.h, bmmd + (gi - 1) * MC)
                for k in range(KC):
                    ald[par].inc(sy.dma_start_transpose(
                        a_sb[:, par * KC * 512 + k * 512:par * KC * 512 + (k + 1) * 512],
                        src_d[gi * 512:(gi + 1) * 512, k * 128:(k + 1) * 128]), 16)

        @block.tensor
        def _(t):
            ph.entry(t)
            t.wait_ge(wld.h, bwld + 16 * KC)
            for gi in range(TG):
                par = gi % 2
                t.wait_ge(ald[par].h, bald[par] + 16 * KC * (gi // 2 + 1))
                for m in range(MC):
                    idx = gi * MC + m
                    bank = idx % 4
                    if idx >= 4:
                        t.wait_ge(vev.h, bvev + idx - 3)
                    for k in range(KC):
                        ins = t.matmul(ps[:, bank * 512:(bank + 1) * 512],
                                       w_sb[:, k * Mout + m * 128:k * Mout + (m + 1) * 128],
                                       a_sb[:, par * KC * 512 + k * 512:
                                               par * KC * 512 + (k + 1) * 512],
                                       start=(k == 0), stop=(k == KC - 1))
                    mmd.inc(ins)

        @block.vector
        def _(v):
            ph.entry(v)
            for gi in range(TG):
                for m in range(MC):
                    idx = gi * MC + m
                    par = idx % 2
                    bank = idx % 4
                    v.wait_ge(mmd.h, bmmd + idx + 1)
                    if idx >= 2:
                        v.wait_ge(stp[par].h, bstp[par] + 16 * (idx // 2))
                    vev.inc(v.tensor_copy(o_sb[:, par * 512:(par + 1) * 512],
                                          ps[:, bank * 512:(bank + 1) * 512]))

        ph.done = [(stp[0], stp[0].n), (stp[1], stp[1].n)]
    return ph

def ph_ffn1loop(nc, es, pool, deps, xw_d, u3_d, b3_d, c0_d, id_d, cf_d, RC=512):
    ph = Ph(nc, pool, deps)
    NPASS = 1024 // RC
    MC = D // 128
    NT = RC
    NB = B

    def mm_after(step, m, gi):
        return step * 3 * MC + m * 3 + gi + 1

    def upd_after(p, t, m):
        return p * (1 + NB * MC) + 1 + t * MC + m + 1

    def upd_init(p):
        return p * (1 + NB * MC) + 1

    with ExitStack() as ees:
        E = ees.enter_context
        u_sb = E(nc.sbuf_tensor("f1u", [128, MC * 3 * D], BF16))
        id_sb = E(nc.sbuf_tensor("f1id", [128, 128], BF16))
        b_sb = E(nc.sbuf_tensor("f1b", [128, 24], F32))
        c_sb = E(nc.sbuf_tensor("f1c", [128, MC * RC], F32))
        cb_sb = E(nc.sbuf_tensor("f1cb", [128, 2 * MC * RC], BF16))
        xw_sb = E(nc.sbuf_tensor("f1xw", [128, 2 * 3 * MC * RC], BF16))
        ga_sb = E(nc.sbuf_tensor("f1ga", [128, 2 * 3 * NT], F32))
        ps = E(nc.psum_tensor("f1ps", [128, 8 * 512], F32))
        wld = ph.sem(); xld = [ph.sem(), ph.sem()]
        mmd = ph.sem(); gev = ph.sem(); upd = ph.sem(); stq = ph.sem()
        bwld = wld.n; bxld = [x.n for x in xld]; bmmd = mmd.n
        bgev = gev.n; bupd = upd.n; bstq = stq.n
        block = E(nc.Block())

        @block.gpsimd
        def _(g):
            ph.entry(g)
            for k in range(MC):
                wld.inc(g.dma_start(out=u_sb[:, k * 3 * D:(k + 1) * 3 * D],
                                    in_=u3_d[k * 128:(k + 1) * 128, :]), 16)
            wld.inc(g.dma_start(out=id_sb[:], in_=id_d), 16)
            wld.inc(g.dma_start(out=b_sb[:], in_=b3_d), 16)
            for p in range(NPASS):
                if p >= 1:
                    g.wait_ge(stq.h, bstq + 16 * p)
                wld.inc(g.dma_start(out=c_sb[:].rearrange("p (m r) -> p m r", m=MC),
                                    in_=c0_d.rearrange("(m p) r -> p m r", p=128)
                                    [:, :, p * RC:(p + 1) * RC]), 16)
                for t in range(NB):
                    step = p * NB + t
                    if step >= 2:
                        g.wait_ge(gev.h, bgev + (step - 1) * 3 * MC)
                    for gi in range(3):
                        xld[t % 2].inc(g.dma_start(
                            out=xw_sb[:, ((t % 2) * 3 + gi) * MC * RC:
                                         ((t % 2) * 3 + gi + 1) * MC * RC]
                                .rearrange("p (m r) -> p m r", m=MC),
                            in_=xw_d.rearrange("(m p) t -> p m t", p=128)
                                [:, gi * MC:(gi + 1) * MC,
                                 t * 1024 + p * RC:t * 1024 + (p + 1) * RC]), 16)
                g.wait_ge(upd.h, bupd + upd_after(p, NB - 1, MC - 1))
                stq.inc(g.dma_start(out=cf_d.rearrange("(m p) r -> p m r", p=128)
                                    [:, :, p * RC:(p + 1) * RC],
                                    in_=c_sb[:].rearrange("p (m r) -> p m r", m=MC)), 16)
            g.wait_ge(stq.h, bstq + 16 * NPASS)

        @block.tensor
        def _(t_):
            ph.entry(t_)
            t_.wait_ge(wld.h, bwld + 16 * (MC + 2))
            for p in range(NPASS):
                t_.wait_ge(wld.h, bwld + 16 * (MC + 2 + p + 1))
                for t in range(NB):
                    step = p * NB + t
                    t_.wait_ge(xld[t % 2].h, bxld[t % 2] + 48 * (step // 2 + 1))
                    t_.wait_ge(upd.h, bupd + (upd_after(p, t - 1, MC - 1) if t > 0
                                              else upd_init(p)))
                    for m in range(MC):
                        for gi in range(3):
                            bidx = gi * 2 + (m % 2)
                            if m >= 2:
                                t_.wait_ge(gev.h, bgev + mm_after(step, m - 2, gi))
                            elif step >= 1:
                                t_.wait_ge(gev.h, bgev + mm_after(step - 1, MC - 2 + m, gi))
                            t_.matmul(ps[:, bidx * 512:bidx * 512 + NT], id_sb[:],
                                      xw_sb[:, ((t % 2) * 3 + gi) * MC * RC + m * RC:
                                               ((t % 2) * 3 + gi) * MC * RC + m * RC + NT],
                                      start=True, stop=False)
                            for k in range(MC):
                                ins = t_.matmul(
                                    ps[:, bidx * 512:bidx * 512 + NT],
                                    u_sb[:, k * 3 * D + gi * D + m * 128:
                                            k * 3 * D + gi * D + (m + 1) * 128],
                                    cb_sb[:, ((step - 1) % 2) * MC * RC + k * RC:
                                             ((step - 1) % 2) * MC * RC + k * RC + NT],
                                    start=False, stop=(k == MC - 1))
                            mmd.inc(ins)

        @block.scalar
        def _(s):
            ph.entry(s)
            AFS = [AF.Sigmoid, AF.Sigmoid, AF.Tanh]
            for p in range(NPASS):
                for t in range(NB):
                    step = p * NB + t
                    for m in range(MC):
                        for gi in range(3):
                            s.wait_ge(mmd.h, bmmd + mm_after(step, m, gi))
                            if m >= 1:
                                s.wait_ge(upd.h, bupd + upd_after(p, t, m - 1))
                            elif step >= 1:
                                pp, tt = (p, t - 1) if t > 0 else (p - 1, NB - 1)
                                s.wait_ge(upd.h, bupd + upd_after(pp, tt, MC - 1))
                            bidx = gi * 2 + (m % 2)
                            gev.inc(s.activation(
                                ga_sb[:, ((step % 2) * 3 + gi) * NT:
                                         ((step % 2) * 3 + gi) * NT + NT],
                                ps[:, bidx * 512:bidx * 512 + NT],
                                AFS[gi], bias=b_sb[:, gi * MC + m:gi * MC + m + 1]))

        @block.vector
        def _(v):
            ph.entry(v)
            for p in range(NPASS):
                v.wait_ge(wld.h, bwld + 16 * (MC + 2 + p + 1))
                for k in range(MC):
                    ins = v.tensor_copy(cb_sb[:, MC * RC + k * RC:MC * RC + (k + 1) * RC],
                                        c_sb[:, k * RC:(k + 1) * RC])
                upd.inc(ins)
                for t in range(NB):
                    step = p * NB + t
                    for m in range(MC):
                        v.wait_ge(gev.h, bgev + step * 3 * MC + m * 3 + 3)
                        sp = (step % 2) * 3
                        f_ap = ga_sb[:, (sp + 0) * NT:(sp + 0) * NT + NT]
                        i_ap = ga_sb[:, (sp + 1) * NT:(sp + 1) * NT + NT]
                        g_ap = ga_sb[:, (sp + 2) * NT:(sp + 2) * NT + NT]
                        c_ap = c_sb[:, m * RC:m * RC + NT]
                        cb_ap = cb_sb[:, (step % 2) * MC * RC + m * RC:
                                         (step % 2) * MC * RC + m * RC + NT]
                        v.tensor_mul(c_ap, f_ap, c_ap)
                        v.tensor_mul(i_ap, i_ap, g_ap)
                        v.tensor_add(c_ap, c_ap, i_ap)
                        upd.inc(v.tensor_copy(cb_ap, c_ap))

        ph.done = [(stq, stq.n)]
    return ph


def ph_final(nc, es, pool, deps, cf_d, uoT_d, bo_d, xwo_d, s1p_d, x1_d, x2_d):
    ph = Ph(nc, pool, deps)
    MC = 8
    with ExitStack() as ees:
        E = ees.enter_context
        uo_sb = E(nc.sbuf_tensor("fno", [128, MC * D], BF16))
        cb_sb = E(nc.sbuf_tensor("fncb", [128, MC * ROWS], BF16))
        bo_sb = E(nc.sbuf_tensor("fnbo", [128, D], F32))
        cuo_sb = E(nc.sbuf_tensor("fncuo", [128, 8 * D], F32))
        xw_sb = E(nc.sbuf_tensor("fnxw", [128, 2 * D], BF16))
        s1_sb = E(nc.sbuf_tensor("fns1", [128, 2 * D], BF16))
        x1_sb = E(nc.sbuf_tensor("fnx1", [128, 2 * D], F32))
        t_sb = E(nc.sbuf_tensor("fnt", [128, 2 * D], F32))
        t2_sb = E(nc.sbuf_tensor("fnt2", [128, 2 * D], F32))
        o_sb = E(nc.sbuf_tensor("fnoo", [128, 2 * D], F32))
        ps = E(nc.psum_tensor("fnps", [128, 4 * 512], F32))
        wld = ph.sem(); eld = [ph.sem(), ph.sem()]
        mmd = ph.sem(); cud = ph.sem(); vad = ph.sem(); tad = ph.sem()
        oad = ph.sem(); stp = [ph.sem(), ph.sem()]
        bwld = wld.n; beld = [x.n for x in eld]; bmmd = mmd.n; bcud = cud.n
        bvad = vad.n; btad = tad.n; boad = oad.n; bstp = [x.n for x in stp]
        block = E(nc.Block())

        @block.gpsimd
        def _(g):
            ph.entry(g)
            for k in range(MC):
                wld.inc(g.dma_start(out=uo_sb[:, k * D:(k + 1) * D],
                                    in_=uoT_d[k * 128:(k + 1) * 128, :]), 16)
            wld.inc(g.dma_start(out=cb_sb[:].rearrange("p (m r) -> p m r", m=MC),
                                in_=cf_d.rearrange("(m p) r -> p m r", p=128)), 16)
            wld.inc(g.dma_start(out=bo_sb[:], in_=bo_d), 16)
            for mt in range(TOWN // 128):
                par = mt % 2
                g.wait_ge(oad.h, boad + mt + 1)
                stp[par].inc(g.dma_start(out=x2_d[mt * 128:(mt + 1) * 128, :],
                                         in_=o_sb[:, par * D:(par + 1) * D]), 16)
            g.wait_ge(stp[0].h, bstp[0] + 16 * 32)
            g.wait_ge(stp[1].h, bstp[1] + 16 * 32)

        @block.sync
        def _(sy):
            ph.entry(sy)
            for mt in range(TOWN // 128):
                par = mt % 2
                if mt >= 2:
                    sy.wait_ge(oad.h, boad + mt - 1)
                eld[par].inc(sy.dma_start(out=xw_sb[:, par * D:(par + 1) * D],
                                          in_=xwo_d[mt * 128:(mt + 1) * 128, :]), 16)
                eld[par].inc(sy.dma_start(out=s1_sb[:, par * D:(par + 1) * D],
                                          in_=s1p_d[mt * 128:(mt + 1) * 128, :]), 16)
                eld[par].inc(sy.dma_start(out=x1_sb[:, par * D:(par + 1) * D],
                                          in_=x1_d[mt * 128:(mt + 1) * 128, :]), 16)

        @block.tensor
        def _(t):
            ph.entry(t)
            t.wait_ge(wld.h, bwld + 16 * (MC + 2))
            for rc in range(8):
                for nt in range(2):
                    idx = rc * 2 + nt
                    bank = idx % 4
                    if idx >= 4:
                        t.wait_ge(cud.h, bcud + idx - 3)
                    for k in range(MC):
                        ins = t.matmul(ps[:, bank * 512:(bank + 1) * 512],
                                       cb_sb[:, k * ROWS + rc * 128:k * ROWS + rc * 128 + 128],
                                       uo_sb[:, k * D + nt * 512:k * D + (nt + 1) * 512],
                                       start=(k == 0), stop=(k == MC - 1))
                    mmd.inc(ins)

        @block.vector
        def _(v):
            ph.entry(v)
            for rc in range(8):
                for nt in range(2):
                    idx = rc * 2 + nt
                    bank = idx % 4
                    v.wait_ge(mmd.h, bmmd + idx + 1)
                    cud.inc(v.tensor_copy(cuo_sb[:, rc * D + nt * 512:rc * D + (nt + 1) * 512],
                                          ps[:, bank * 512:(bank + 1) * 512]))
            for mt in range(TOWN // 128):
                par = mt % 2
                rc = mt % 8
                v.wait_ge(eld[par].h, beld[par] + 48 * (mt // 2 + 1))
                if mt >= 2:
                    v.wait_ge(stp[par].h, bstp[par] + 16 * (mt // 2))
                if mt >= 2:
                    v.wait_ge(tad.h, btad + mt - 1)
                v.tensor_tensor(t_sb[:, par * D:(par + 1) * D],
                                xw_sb[:, par * D:(par + 1) * D],
                                cuo_sb[:, rc * D:(rc + 1) * D], ALU.add)
                vad.inc(v.tensor_add(t_sb[:, par * D:(par + 1) * D],
                                     t_sb[:, par * D:(par + 1) * D], bo_sb[:]))
                v.wait_ge(tad.h, btad + mt + 1)
                v.tensor_add(o_sb[:, par * D:(par + 1) * D],
                             t2_sb[:, par * D:(par + 1) * D],
                             x1_sb[:, par * D:(par + 1) * D])
                oad.inc(v.tensor_add(o_sb[:, par * D:(par + 1) * D],
                                     o_sb[:, par * D:(par + 1) * D],
                                     s1_sb[:, par * D:(par + 1) * D]))

        @block.scalar
        def _(s):
            ph.entry(s)
            for mt in range(TOWN // 128):
                par = mt % 2
                s.wait_ge(vad.h, bvad + mt + 1)
                if mt >= 2:
                    s.wait_ge(oad.h, boad + mt - 1)
                tad.inc(s.activation(t2_sb[:, par * D:(par + 1) * D],
                                     t_sb[:, par * D:(par + 1) * D], AF.Tanh))

        ph.done = [(stp[0], stp[0].n), (stp[1], stp[1].n)]
    return ph

def build_fused(dbg=()):
    nc = bass.Bass("TRN2", target_bir_lowering=False, debug=False)

    def dram(nm, shape, dt):
        kind = "ExternalOutput" if nm in dbg else "Internal"
        return nc.dram_tensor(nm, shape, dt, kind=kind).ap()

    def inp(nm, shape, dt=F32):
        return nc.dram_tensor(nm, shape, dt, kind="ExternalInput").ap()

    x_d = inp("x", [TOK, D])
    ln0g_d = inp("ln0g", [128, D]); ln0b_d = inp("ln0b", [128, D])
    win0_d = inp("win0", [D, 2 * D]); bin0_d = inp("bin0", [128, 2 * D])
    mask_d = inp("mask", [128, 1]); asc_d = inp("asc", [128, SC_F])
    wout0_d = inp("wout0", [D, D]); bout0_d = inp("bout0", [128, D])
    mur_d = inp("mur", [128, 8]); muk_d = inp("muk", [128, 8])
    wrT_d = inp("wrT", [D, D]); wkT_d = inp("wkT", [D, D]); wvT_d = inp("wvT", [D, D])
    ln1g_d = inp("ln1g", [128, D]); ln1b_d = inp("ln1b", [128, D])
    winT1_d = inp("winT1", [D, 2 * D]); bin1_d = inp("bin1", [128, 2 * D])
    wau_d = inp("wau", [2 * D, 64]); oh_d = inp("oh", [128, 8])
    cssm_d = inp("cssm", [2 * D, 16]); woT1_d = inp("woT1", [2 * D, D])
    bout1_d = inp("bout1", [128, D])
    wficT_d = inp("wficT", [D, 3 * D]); wfoT_d = inp("wfoT", [D, D])
    u3_d = inp("u3", [D, 3 * D]); b3_d = inp("b3", [128, 24])
    c0_d = inp("c0", [D, ROWS]); id_d = inp("ident", [128, 128])
    uoT_d = inp("uoT", [D, D]); bo_d = inp("bo", [128, D])

    xn0_d = dram("xn0", [TOK, D], BF16)
    z_d = dram("z", [TOK, D], BF16)
    pu_d = dram("pu", [SC_STEPS * 128, SC_F], BF16)
    pg_d = dram("pg", [SC_STEPS * 128, SC_F], BF16)
    hs_d = dram("hs", [TOWN, D], BF16)
    attn0_d = dram("attn0", [TOWN, D], BF16)
    xr_d = dram("xr", [D, TOWN], BF16)
    xk_d = dram("xk", [D, TOWN], BF16)
    r_d = dram("r", [TOWN, D], BF16)
    kact_d = dram("kact", [TOWN, D], BF16)
    x1_d = dram("x1", [TOWN, D], F32)
    xn1_d = dram("xn1", [TOWN, D], BF16)
    xln_d = dram("xln", [TOWN, D], BF16)
    h_d = dram("h", [TOWN, 2 * D], BF16)
    agin_d = dram("agin", [128, 2], F32)
    agout_d = dram("agout", [NCORES * 128, 2], F32)
    states_d = dram("states", [128, ROWS], F32)
    s1p_d = dram("s1p", [TOWN, D], BF16)
    xw_d = dram("xw", [3 * D, TOWN], BF16)
    xwo_d = dram("xwo", [TOWN, D], BF16)
    cf_d = dram("cf", [D, ROWS], F32)
    x2_d = nc.dram_tensor("x2", [TOWN, D], F32, kind="ExternalOutput").ap()

    es = ExitStack()
    with es:
        pool = Pool(nc, es, n=48)
        p1 = ph_ln(nc, es, pool, [], TOK, x_d, ln0g_d, ln0b_d, xn0_d, z_d, True)
        p2 = ph_ug(nc, es, pool, p1.done, z_d, win0_d, bin0_d, mask_d, pu_d, pg_d)
        p3 = ph_scan0(nc, es, pool, p2.done, pu_d, pg_d, asc_d, hs_d)
        p4 = ph_mmb(nc, es, pool, p3.done, "at0", TOWN, D, D, ("t2", hs_d),
                    wout0_d, attn0_d, bias_d=bout0_d, acts=[(0, D, AF.Copy)])
        p5 = ph_shift(nc, es, pool, p4.done, xn0_d, mur_d, muk_d, xr_d, xk_d)
        p6 = ph_mmb(nc, es, pool, p5.done, "rr", TOWN, D, D, ("ch", xr_d),
                    wrT_d, r_d, acts=[(0, D, AF.Sigmoid)])
        p7 = ph_mmb(nc, es, pool, p6.done, "kk", TOWN, D, D, ("ch", xk_d),
                    wkT_d, kact_d, acts=[(0, D, AF.Relu)], relu_sq=True)

        xn0o = xn0_d.rearrange("(b rr) d -> b rr d", b=B)
        x_o = x_d.rearrange("(b rr) d -> b rr d", b=B)

        def rm_r(mt):
            return r_d[mt * 128:(mt + 1) * 128, :]

        def rm_at0(mt):
            return attn0_d[mt * 128:(mt + 1) * 128, :]

        def rm_xn0(mt):
            b, rc = mt // 8, mt % 8
            return xn0o[b, PRE + rc * 128:PRE + (rc + 1) * 128, :]

        def rm_x(mt):
            b, rc = mt // 8, mt % 8
            return x_o[b, PRE + rc * 128:PRE + (rc + 1) * 128, :]

        p8 = ph_mmb(nc, es, pool, p7.done, "x1", TOWN, D, D, ("t2", kact_d),
                    wvT_d, x1_d, out_dtype=F32,
                    extras=[(rm_r, ALU.mult), (rm_at0, ALU.add),
                            (rm_xn0, ALU.add), (rm_x, ALU.add)])
        p9 = ph_ln(nc, es, pool, p8.done, TOWN, x1_d, ln1g_d, ln1b_d, xn1_d, xln_d, False)
        p10 = ph_mmb(nc, es, pool, p9.done, "h1", TOWN, D, 2 * D, ("t2", xln_d),
                     winT1_d, h_d, bias_d=bin1_d, acts=[(0, 2 * D, AF.Copy)])
        p11 = ph_au_states(nc, es, pool, p10.done, h_d, wau_d, oh_d,
                           agin_d, agout_d, states_d)
        p13 = ph_s1p(nc, es, pool, p11.done, states_d, cssm_d, woT1_d, bout1_d,
                     xn1_d, s1p_d)
        p14 = ph_mma(nc, es, pool, p13.done, "xwf", D, 3 * D, xn1_d, wficT_d, xw_d)
        p15 = ph_mmb(nc, es, pool, p14.done, "xwo", TOWN, D, D, ("t2", xn1_d),
                     wfoT_d, xwo_d, acts=[(0, D, AF.Copy)])
        p16 = ph_ffn1loop(nc, es, pool, p15.done, xw_d, u3_d, b3_d, c0_d, id_d, cf_d)
        p17 = ph_final(nc, es, pool, p16.done, cf_d, uoT_d, bo_d, xwo_d, s1p_d,
                       x1_d, x2_d)
    return nc




def _t(a):
    return np.ascontiguousarray(np.asarray(a, np.float32).T)


def _bcast(v):
    a = np.asarray(v, np.float32)
    return np.ascontiguousarray(np.broadcast_to(a, (128, a.shape[0])))


def _col(v):
    a = np.asarray(v, np.float32)
    return np.ascontiguousarray(a.reshape(-1, 128).T)

def host_maps(inp):
    x = np.asarray(inp["L0_attn_x"], np.float32)
    A = np.asarray(inp["L0_attn_A"], np.float32)
    asc = np.empty((128, SC_F), np.float32)
    asc[0::2, :] = A[:512][None, :]
    asc[1::2, :] = A[512:][None, :]
    wau = np.zeros((2 * D, 64), np.float32)
    wau[:, 0:16] = np.asarray(inp["L1_attn_A"], np.float32)
    wau[:, 32:48] = np.asarray(inp["L1_attn_B_ssm"], np.float32)
    wfic = np.concatenate([_t(np.asarray(inp["L1_ffn_wf"], np.float32)),
                           _t(np.asarray(inp["L1_ffn_wi"], np.float32)),
                           _t(np.asarray(inp["L1_ffn_wc"], np.float32))], axis=1)
    u3 = np.concatenate([_t(np.asarray(inp["L1_ffn_uf"], np.float32)),
                         _t(np.asarray(inp["L1_ffn_ui"], np.float32)),
                         _t(np.asarray(inp["L1_ffn_uc"], np.float32))], axis=1)
    b3 = _col(np.concatenate([np.asarray(inp["L1_ffn_bf"], np.float32),
                              np.asarray(inp["L1_ffn_bi"], np.float32),
                              np.asarray(inp["L1_ffn_bc"], np.float32)]))
    c0 = np.ascontiguousarray(
        np.broadcast_to(np.asarray(inp["L1_ffn_in_c"], np.float32)[:, None], (D, ROWS)))
    common = dict(
        ln0g=_bcast(inp["L0_attn_gamma"]), ln0b=_bcast(inp["L0_attn_beta"]),
        win0=np.asarray(inp["L0_attn_W_in"], np.float32),
        bin0=_bcast(inp["L0_attn_b_in"]), asc=asc,
        wout0=np.asarray(inp["L0_attn_W_out"], np.float32),
        bout0=_bcast(inp["L0_attn_b_out"]),
        mur=_col(inp["L0_ffn_mu_r"]), muk=_col(inp["L0_ffn_mu_k"]),
        wrT=_t(np.asarray(inp["L0_ffn_W_r"], np.float32)),
        wkT=_t(np.asarray(inp["L0_ffn_W_k"], np.float32)),
        wvT=_t(np.asarray(inp["L0_ffn_W_v"], np.float32)),
        ln1g=_bcast(inp["L1_attn_norm_w"]), ln1b=_bcast(inp["L1_attn_norm_b"]),
        winT1=_t(np.asarray(inp["L1_attn_W_in"], np.float32)),
        bin1=_bcast(inp["L1_attn_b_in"]), wau=wau,
        cssm=np.asarray(inp["L1_attn_C_ssm"], np.float32),
        woT1=_t(np.asarray(inp["L1_attn_W_out"], np.float32)),
        bout1=_bcast(inp["L1_attn_b_out"]),
        wficT=wfic, wfoT=_t(np.asarray(inp["L1_ffn_wo"], np.float32)),
        u3=u3, b3=b3, c0=c0, ident=np.eye(128, dtype=np.float32),
        uoT=_t(np.asarray(inp["L1_ffn_uo"], np.float32)),
        bo=_bcast(inp["L1_ffn_bo"]),
    )
    maps = []
    for k in range(NCORES):
        xs = np.zeros((B, RB, D), np.float32)
        r0 = k * ROWS
        pre = min(PRE, r0)
        if pre > 0:
            xs[:, PRE - pre:PRE, :] = x[:, r0 - pre:r0, :]
        xs[:, PRE:, :] = x[:, r0:r0 + ROWS, :]
        oh = np.zeros(8, np.float32)
        if k >= 1:
            oh[k - 1] = 1.0
        m = dict(common)
        m["x"] = np.ascontiguousarray(xs.reshape(TOK, D))
        m["mask"] = np.full((128, 1), 0.0 if k == 0 else 1.0, np.float32)
        m["oh"] = _bcast(oh)
        maps.append(m)
    return maps


def kernel(**inp):
    nc = build_fused()
    maps = host_maps(inp)
    res = run_bass_kernel_spmd(nc, maps, core_ids=list(range(NCORES)))
    out = np.empty((B, L, D), np.float32)
    for k in range(NCORES):
        x2 = res.results[k]["x2"]
        out[:, k * ROWS:(k + 1) * ROWS, :] = x2.reshape(B, ROWS, D)
    return out


# version 04
# speedup vs baseline: 1.0137x; 1.0137x over previous
import numpy as np
import ml_dtypes
from contextlib import ExitStack
import concourse.bass as bass
import concourse.mybir as mybir
from concourse.bass_utils import run_bass_kernel_spmd

F32 = mybir.dt.float32
BF16 = mybir.dt.bfloat16
AF = mybir.ActivationFunctionType
ALU = mybir.AluOpType
AX = mybir.AxisListType
BF = ml_dtypes.bfloat16

NCORES = 8
B, L, D = 8, 8192, 1024
ROWS = 1024
PRE = 128
RB = ROWS + PRE
TOK = B * RB
TOWN = B * ROWS
LN_EPS = 1e-5

SC_W = 128
SC_LC = 128
SC_STEPS = SC_W + SC_LC
SC_F = 512
SC_BLK = 8
NBLK = SC_STEPS // SC_BLK
WBLK = SC_W // SC_BLK
BLKC = SC_BLK * SC_F
SC_H = SC_F // 2


class S:

    def __init__(self, h):
        self.h = h
        self.n = 0

    def inc(self, ins, k=1):
        self.n += k
        ins.then_inc(self.h, k)
        return ins


class Pool:
    def __init__(self, nc, es, n=48):
        self.sems = [S(es.enter_context(nc.semaphore(f"sp{i}"))) for i in range(n)]
        self.i = 0

    def take(self):
        s = self.sems[self.i % len(self.sems)]
        self.i += 1
        return s


class Ph:
    def __init__(self, nc, pool, deps):
        self.nc = nc
        self.pool = pool
        self.deps = deps
        self.done = []

    def sem(self):
        return self.pool.take()

    def entry(self, eng):
        for (s, c) in self.deps:
            eng.wait_ge(s.h, c)


_LN_N = [0]


def ph_ln(nc, es, pool, deps, Tn, x_d, gam_d, bet_d, xn_d, z_d, double):
    ph = Ph(nc, pool, deps)
    NTIL = Tn // 128
    _LN_N[0] += 1
    pfx = f"lnp{_LN_N[0]}"
    with ExitStack() as ees:
        E = ees.enter_context
        x_sb = E(nc.sbuf_tensor(pfx + "x", [128, 2 * D], F32))
        xn_sb = E(nc.sbuf_tensor(pfx + "xn", [128, 2 * D], F32))
        z_sb = E(nc.sbuf_tensor(pfx + "z", [128, 2 * D], BF16))
        g_sb = E(nc.sbuf_tensor(pfx + "g", [128, D], F32))
        be_sb = E(nc.sbuf_tensor(pfx + "be", [128, D], F32))
        sq_sb = E(nc.sbuf_tensor(pfx + "sq", [128, D], F32))
        st_sb = E(nc.sbuf_tensor(pfx + "st", [128, 16], F32))
        eps_sb = E(nc.sbuf_tensor(pfx + "eps", [128, 1], F32))
        cld = ph.sem()
        lde, ldo = ph.sem(), ph.sem()
        ld = [lde, ldo]
        sq1 = ph.sem()
        sq2 = ph.sem()
        rst = ph.sem()
        xnd = ph.sem()
        zd = ph.sem()
        sx = [ph.sem(), ph.sem()]
        sz = [ph.sem(), ph.sem()]
        vv = ph.sem()
        b = [s.n for s in (cld, lde, ldo, sq1, sq2, rst, xnd, zd, sx[0], sx[1],
                           sz[0], sz[1], vv)]
        (bcld, blde, bldo, bsq1, bsq2, brst, bxnd, bzd, bsx0, bsx1,
         bsz0, bsz1, bvv) = b
        bld = [blde, bldo]
        bsx_ = [bsx0, bsx1]
        bsz_ = [bsz0, bsz1]
        block = E(nc.Block())

        @block.sync
        def _(sy):
            ph.entry(sy)
            cld.inc(sy.dma_start(out=g_sb[:], in_=gam_d), 16)
            cld.inc(sy.dma_start(out=be_sb[:], in_=bet_d), 16)
            for i in range(NTIL):
                par = i % 2
                if i >= 2:
                    sy.wait_ge(xnd.h, bxnd + i - 1)
                ld[par].inc(sy.dma_start(out=x_sb[:, par * D:(par + 1) * D],
                                         in_=x_d[i * 128:(i + 1) * 128, :]), 16)
                sy.wait_ge(zd.h, bzd + i)
                if i >= 1:
                    sz[(i - 1) % 2].inc(
                        sy.dma_start(out=z_d[(i - 1) * 128:i * 128, :],
                                     in_=z_sb[:, ((i - 1) % 2) * D:((i - 1) % 2) * D + D]),
                        16)
            sy.wait_ge(zd.h, bzd + NTIL)
            sz[(NTIL - 1) % 2].inc(
                sy.dma_start(out=z_d[(NTIL - 1) * 128:NTIL * 128, :],
                             in_=z_sb[:, ((NTIL - 1) % 2) * D:((NTIL - 1) % 2) * D + D]),
                16)

        @block.gpsimd
        def _(g):
            ph.entry(g)
            for i in range(NTIL):
                par = i % 2
                g.wait_ge(xnd.h, bxnd + i + 1)
                sx[par].inc(g.dma_start(out=xn_d[i * 128:(i + 1) * 128, :],
                                        in_=xn_sb[:, par * D:(par + 1) * D]), 16)

        @block.scalar
        def _(s):
            ph.entry(s)
            for i in range(NTIL):
                par = i % 2
                p0 = 8 * par
                s.wait_ge(ld[par].h, bld[par] + 16 * (i // 2 + 1))
                sq1.inc(s.activation(sq_sb[:, 0:D], x_sb[:, par * D:(par + 1) * D],
                                     AF.Square, accum_out=st_sb[:, p0 + 0:p0 + 1]))
                s.wait_ge(rst.h, brst + (4 * i if double else 2 * i) + 1)
                sq1.inc(s.activation(st_sb[:, p0 + 3:p0 + 4], st_sb[:, p0 + 3:p0 + 4],
                                     AF.Sqrt, bias=eps_sb[:, 0:1]))
                if double:
                    s.wait_ge(xnd.h, bxnd + i + 1)
                    sq2.inc(s.activation(sq_sb[:, 0:D], xn_sb[:, par * D:(par + 1) * D],
                                         AF.Square, accum_out=st_sb[:, p0 + 4:p0 + 5]))
                    s.wait_ge(rst.h, brst + 4 * i + 3)
                    sq2.inc(s.activation(st_sb[:, p0 + 7:p0 + 8], st_sb[:, p0 + 7:p0 + 8],
                                         AF.Sqrt, bias=eps_sb[:, 0:1]))

        @block.vector
        def _(v):
            ph.entry(v)
            v.memset(eps_sb[:], LN_EPS)
            vvc = [0]

            def tick(ins):
                vvc[0] += 1
                vv.inc(ins)

            def syn():
                v.wait_ge(vv.h, bvv + vvc[0])

            for i in range(NTIL):
                par = i % 2
                p0 = 8 * par
                xap = x_sb[:, par * D:(par + 1) * D]
                xnap = xn_sb[:, par * D:(par + 1) * D]
                zap = z_sb[:, par * D:(par + 1) * D]
                v.wait_ge(cld.h, bcld + 32)
                v.wait_ge(ld[par].h, bld[par] + 16 * (i // 2 + 1))
                tick(v.reduce_sum(st_sb[:, p0 + 1:p0 + 2], xap, axis=AX.X))
                syn()
                tick(v.tensor_scalar_mul(st_sb[:, p0 + 1:p0 + 2],
                                         st_sb[:, p0 + 1:p0 + 2], 1.0 / D))
                v.wait_ge(sq1.h, bsq1 + 2 * i + 1)
                syn()
                tick(v.tensor_mul(st_sb[:, p0 + 2:p0 + 3], st_sb[:, p0 + 1:p0 + 2],
                                  st_sb[:, p0 + 1:p0 + 2]))
                syn()
                rst.inc(v.scalar_tensor_tensor(st_sb[:, p0 + 3:p0 + 4],
                                               st_sb[:, p0 + 0:p0 + 1], 1.0 / D,
                                               st_sb[:, p0 + 2:p0 + 3],
                                               ALU.mult, ALU.subtract))
                v.wait_ge(sq1.h, bsq1 + 2 * i + 2)
                tick(v.reciprocal(st_sb[:, p0 + 3:p0 + 4], st_sb[:, p0 + 3:p0 + 4]))
                rst.inc(v.engine_nop())
                if i >= 2:
                    v.wait_ge(sx[par].h, bsx_[par] + 16 * (i // 2))
                syn()
                xnd.inc(v.tensor_scalar(xnap, xap, st_sb[:, p0 + 1:p0 + 2],
                                        st_sb[:, p0 + 3:p0 + 4], ALU.subtract, ALU.mult))
                if i >= 2:
                    v.wait_ge(sz[par].h, bsz_[par] + 16 * (i // 2))
                if double:
                    tick(v.reduce_sum(st_sb[:, p0 + 5:p0 + 6], xnap, axis=AX.X))
                    syn()
                    tick(v.tensor_scalar_mul(st_sb[:, p0 + 5:p0 + 6],
                                             st_sb[:, p0 + 5:p0 + 6], 1.0 / D))
                    v.wait_ge(sq2.h, bsq2 + 2 * i + 1)
                    syn()
                    tick(v.tensor_mul(st_sb[:, p0 + 6:p0 + 7], st_sb[:, p0 + 5:p0 + 6],
                                      st_sb[:, p0 + 5:p0 + 6]))
                    syn()
                    rst.inc(v.scalar_tensor_tensor(st_sb[:, p0 + 7:p0 + 8],
                                                   st_sb[:, p0 + 4:p0 + 5], 1.0 / D,
                                                   st_sb[:, p0 + 6:p0 + 7],
                                                   ALU.mult, ALU.subtract))
                    v.wait_ge(sq2.h, bsq2 + 2 * i + 2)
                    tick(v.reciprocal(st_sb[:, p0 + 7:p0 + 8], st_sb[:, p0 + 7:p0 + 8]))
                    rst.inc(v.engine_nop())
                    syn()
                    v.tensor_scalar(zap, xnap, st_sb[:, p0 + 5:p0 + 6],
                                    st_sb[:, p0 + 7:p0 + 8], ALU.subtract, ALU.mult)
                    v.tensor_mul(zap, zap, g_sb[:])
                else:
                    v.tensor_mul(zap, xnap, g_sb[:])
                zd.inc(v.tensor_add(zap, zap, be_sb[:]))

        ph.done = [(sx[0], sx[0].n), (sx[1], sx[1].n),
                   (sz[0], sz[0].n), (sz[1], sz[1].n)]
    return ph



def ph_mmb(nc, es, pool, deps, name, Ttok, K, Nout, lhs, w_d, out_d,
           out_dtype=BF16, bias_d=None, acts=None, relu_sq=False, extras=()):
    ph = Ph(nc, pool, deps)
    KC = K // 128
    TG = Ttok // 512
    NT2 = max(1, Nout // 512)
    NW = min(Nout, 512)
    MCH = Ttok // 128
    NE = len(extras)
    kind, src_d = lhs
    use_scalar = (acts is not None) or relu_sq
    assert not (use_scalar and NE)
    with ExitStack() as ees:
        E = ees.enter_context
        w_sb = E(nc.sbuf_tensor(name + "w", [128, KC * Nout], BF16))
        a_sb = E(nc.sbuf_tensor(name + "a", [128, 2 * KC * 512], BF16))
        bias_sb = E(nc.sbuf_tensor(name + "b", [128, Nout], F32)) if bias_d is not None else None
        e_sb = E(nc.sbuf_tensor(name + "e", [128, max(1, 2 * NE * Nout)], F32))
        t_sb = E(nc.sbuf_tensor(name + "t", [128, 2 * NW], F32))
        t2_sb = E(nc.sbuf_tensor(name + "t2", [128, 2 * NW], F32))
        o_sb = E(nc.sbuf_tensor(name + "o", [128, 2 * Nout], out_dtype))
        ps = E(nc.psum_tensor(name + "ps", [128, 8 * 512], F32))
        wld = ph.sem(); ald = [ph.sem(), ph.sem()]; eld = [ph.sem(), ph.sem()]
        mmd = ph.sem(); vps = ph.sem(); sev = ph.sem(); vev = ph.sem()
        stp = [ph.sem(), ph.sem()]
        bwld = wld.n; bald = [x.n for x in ald]; beld = [x.n for x in eld]
        bmmd = mmd.n; bvps = vps.n; bsev = sev.n; bvev = vev.n
        bstp = [x.n for x in stp]
        n_w_dma = KC + (1 if bias_d is not None else 0)
        fin, bfin, finpm = (sev, bsev, NT2) if use_scalar else (vev, bvev, 1)
        block = E(nc.Block())

        @block.gpsimd
        def _(g):
            ph.entry(g)
            for k in range(KC):
                wld.inc(g.dma_start(out=w_sb[:, k * Nout:(k + 1) * Nout],
                                    in_=w_d[k * 128:(k + 1) * 128, :]), 16)
            if bias_d is not None:
                wld.inc(g.dma_start(out=bias_sb[:], in_=bias_d), 16)
            for mt in range(MCH):
                par = mt % 2
                for ei, (rmap, op) in enumerate(extras):
                    eld[par].inc(g.dma_start(
                        out=e_sb[:, (par * NE + ei) * Nout:(par * NE + ei + 1) * Nout],
                        in_=rmap(mt)), 16)
                g.wait_ge(fin.h, bfin + finpm * (mt + 1))
                stp[par].inc(g.dma_start(out=out_d[mt * 128:(mt + 1) * 128, :],
                                         in_=o_sb[:, par * Nout:(par + 1) * Nout]), 16)

        @block.sync
        def _(sy):
            ph.entry(sy)
            for gi in range(TG):
                par = gi % 2
                if gi >= 2:
                    sy.wait_ge(mmd.h, bmmd + (gi - 1) * 4)
                if kind == "t2":
                    for k in range(KC):
                        ald[par].inc(sy.dma_start_transpose(
                            a_sb[:, par * KC * 512 + k * 512:par * KC * 512 + (k + 1) * 512],
                            src_d[gi * 512:(gi + 1) * 512, k * 128:(k + 1) * 128]), 16)
                else:
                    ald[par].inc(sy.dma_start(
                        out=a_sb[:, par * KC * 512:(par + 1) * KC * 512]
                            .rearrange("p (k n) -> p k n", k=KC),
                        in_=src_d.rearrange("(k p) t -> p k t", p=128)
                        [:, :, gi * 512:(gi + 1) * 512]), 16)


        TIF = max(1, 8 // NT2)

        @block.tensor
        def _(t):
            ph.entry(t)
            t.wait_ge(wld.h, bwld + 16 * n_w_dma)
            for gi in range(TG):
                par = gi % 2
                nld = KC if kind == "t2" else 1
                t.wait_ge(ald[par].h, bald[par] + 16 * nld * (gi // 2 + 1))
                for m in range(4):
                    idx = gi * 4 + m
                    if idx >= TIF:
                        t.wait_ge(vps.h, bvps + NT2 * (idx - TIF + 1))
                    for k in range(KC):
                        for nt in range(NT2):
                            bank = (idx % TIF) * NT2 + nt
                            ins = t.matmul(
                                ps[:, bank * 512:bank * 512 + NW],
                                a_sb[:, par * KC * 512 + k * 512 + m * 128:
                                        par * KC * 512 + k * 512 + (m + 1) * 128],
                                w_sb[:, k * Nout + nt * NW:k * Nout + nt * NW + NW],
                                start=(k == 0), stop=(k == KC - 1))
                    mmd.inc(ins)

        @block.vector
        def _(v):
            ph.entry(v)
            for mt in range(MCH):
                par = mt % 2
                if NE:
                    v.wait_ge(eld[par].h, beld[par] + 16 * NE * (mt // 2 + 1))
                if mt >= 2:
                    v.wait_ge(stp[par].h, bstp[par] + 16 * (mt // 2))
                for nt in range(NT2):
                    idx = mt * NT2 + nt
                    bank = (mt % TIF) * NT2 + nt
                    v.wait_ge(mmd.h, bmmd + mt + 1)
                    if use_scalar and idx >= 2:
                        v.wait_ge(sev.h, bsev + idx - 1)
                    tgt = (t_sb[:, (idx % 2) * NW:(idx % 2) * NW + NW]
                           if (use_scalar or NE)
                           else o_sb[:, par * Nout + nt * NW:par * Nout + nt * NW + NW])
                    if bias_d is not None:
                        ins = v.tensor_tensor(tgt, ps[:, bank * 512:bank * 512 + NW],
                                              bias_sb[:, nt * NW:nt * NW + NW], ALU.add)
                    else:
                        ins = v.tensor_copy(tgt, ps[:, bank * 512:bank * 512 + NW])
                    vps.inc(ins)
                    if NE:
                        src = t_sb[:, (idx % 2) * NW:(idx % 2) * NW + NW]
                        for ei, (rmap, op) in enumerate(extras):
                            eap = e_sb[:, (par * NE + ei) * Nout + nt * NW:
                                          (par * NE + ei) * Nout + nt * NW + NW]
                            last = (ei == NE - 1)
                            dst = (o_sb[:, par * Nout + nt * NW:par * Nout + nt * NW + NW]
                                   if last else src)
                            ins = v.tensor_tensor(dst, src, eap, op)
                    if not use_scalar and nt == NT2 - 1:
                        vev.inc(ins)

        @block.scalar
        def _(s):
            ph.entry(s)
            if not use_scalar:
                return
            for mt in range(MCH):
                par = mt % 2
                for nt in range(NT2):
                    idx = mt * NT2 + nt
                    s.wait_ge(vps.h, bvps + idx + 1)
                    if mt >= 2 and nt == 0:
                        s.wait_ge(stp[par].h, bstp[par] + 16 * (mt // 2))
                    ins = None
                    for (lo, hi, fn) in acts:
                        if hi <= nt * NW or lo >= (nt + 1) * NW:
                            continue
                        clo = max(lo - nt * NW, 0)
                        chi = min(hi - nt * NW, NW)
                        srcap = t_sb[:, (idx % 2) * NW + clo:(idx % 2) * NW + chi]
                        if relu_sq:
                            dst = t2_sb[:, (idx % 2) * NW + clo:(idx % 2) * NW + chi]
                        else:
                            dst = o_sb[:, par * Nout + nt * NW + clo:
                                          par * Nout + nt * NW + chi]
                        ins = s.activation(dst, srcap, fn)
                    if relu_sq:
                        ins = s.activation(o_sb[:, par * Nout + nt * NW:
                                                   par * Nout + nt * NW + NW],
                                           t2_sb[:, (idx % 2) * NW:(idx % 2) * NW + NW],
                                           AF.Square)
                    sev.inc(ins)

        ph.done = [(stp[0], stp[0].n), (stp[1], stp[1].n)]
    return ph


def ph_ug(nc, es, pool, deps, z_d, w_d, bias_d, mask_d, pu_d, pg_d):
    ph = Ph(nc, pool, deps)
    K, Nout = D, 2 * D
    KC, NT2, NW = 8, 4, 512
    TG = TOK // 512
    MCH = TOK // 128
    pu3 = pu_d.rearrange("(tl q) f -> tl q f", q=128)
    pg3 = pg_d.rearrange("(tl q) f -> tl q f", q=128)

    def targets(mt):
        b, cb = mt // 9, mt % 9
        tg = []
        if cb == 0:
            tg.append((0, 0))
        else:
            tg.append((cb - 1, SC_W))
            if cb <= 7:
                tg.append((cb, 0))
        return b, tg

    nst = [0, 0]
    st_after = []
    for mt in range(MCH):
        b, tg = targets(mt)
        nst[mt % 2] += 16 * 2 * len(tg)
        st_after.append((mt % 2, nst[0], nst[1]))

    with ExitStack() as ees:
        E = ees.enter_context
        w_sb = E(nc.sbuf_tensor("ugw", [128, KC * Nout], BF16))
        a_sb = E(nc.sbuf_tensor("uga", [128, 2 * KC * 512], BF16))
        bias_sb = E(nc.sbuf_tensor("ugb", [128, Nout], F32))
        t_sb = E(nc.sbuf_tensor("ugt", [128, 2 * NW], F32))
        o_sb = E(nc.sbuf_tensor("ugo", [128, 2 * Nout], BF16))
        m_sb = E(nc.sbuf_tensor("ugm", [128, 1], F32))
        ps = E(nc.psum_tensor("ugps", [128, 8 * 512], F32))
        wld = ph.sem(); ald = [ph.sem(), ph.sem()]
        mmd = ph.sem(); vps = ph.sem(); sev = ph.sem(); vev = ph.sem()
        stp = [ph.sem(), ph.sem()]
        bwld = wld.n; bald = [x.n for x in ald]; bmmd = mmd.n; bvps = vps.n
        bsev = sev.n; bvev = vev.n; bstp = [x.n for x in stp]
        block = E(nc.Block())

        @block.gpsimd
        def _(g):
            ph.entry(g)
            for k in range(KC):
                wld.inc(g.dma_start(out=w_sb[:, k * Nout:(k + 1) * Nout],
                                    in_=w_d[k * 128:(k + 1) * 128, :]), 16)
            wld.inc(g.dma_start(out=bias_sb[:], in_=bias_d), 16)
            wld.inc(g.dma_start(out=m_sb[:], in_=mask_d), 16)
            for mt in range(MCH):
                par = mt % 2
                b, tg = targets(mt)
                ispre = (mt % 9 == 0)
                if ispre:
                    g.wait_ge(vev.h, bvev + len([m for m in range(mt + 1) if m % 9 == 0]))
                else:
                    g.wait_ge(sev.h, bsev + NT2 * (mt + 1))
                for (k, tlb) in tg:
                    q0 = k * 16 + b * 2
                    srcu = o_sb[:, par * Nout:par * Nout + 1024] \
                        .rearrange("p (gb f) -> p gb f", gb=2)
                    srcg = o_sb[:, par * Nout + 1024:par * Nout + 2048] \
                        .rearrange("p (gb f) -> p gb f", gb=2)
                    stp[par].inc(g.dma_start(out=pu3[tlb:tlb + 128, q0:q0 + 2, :],
                                             in_=srcu), 16)
                    stp[par].inc(g.dma_start(out=pg3[tlb:tlb + 128, q0:q0 + 2, :],
                                             in_=srcg), 16)

        @block.sync
        def _(sy):
            ph.entry(sy)
            for gi in range(TG):
                par = gi % 2
                if gi >= 2:
                    sy.wait_ge(mmd.h, bmmd + (gi - 1) * 4)
                for k in range(KC):
                    ald[par].inc(sy.dma_start_transpose(
                        a_sb[:, par * KC * 512 + k * 512:par * KC * 512 + (k + 1) * 512],
                        z_d[gi * 512:(gi + 1) * 512, k * 128:(k + 1) * 128]), 16)

        TIF = 2

        @block.tensor
        def _(t):
            ph.entry(t)
            t.wait_ge(wld.h, bwld + 16 * (KC + 2))
            for gi in range(TG):
                par = gi % 2
                t.wait_ge(ald[par].h, bald[par] + 16 * KC * (gi // 2 + 1))
                for m in range(4):
                    idx = gi * 4 + m
                    if idx >= TIF:
                        t.wait_ge(vps.h, bvps + NT2 * (idx - TIF + 1))
                    for k in range(KC):
                        for nt in range(NT2):
                            bank = (idx % TIF) * NT2 + nt
                            ins = t.matmul(
                                ps[:, bank * 512:bank * 512 + NW],
                                a_sb[:, par * KC * 512 + k * 512 + m * 128:
                                        par * KC * 512 + k * 512 + (m + 1) * 128],
                                w_sb[:, k * Nout + nt * NW:k * Nout + nt * NW + NW],
                                start=(k == 0), stop=(k == KC - 1))
                    mmd.inc(ins)

        @block.vector
        def _(v):
            ph.entry(v)
            for mt in range(MCH):
                par = mt % 2
                ispre = (mt % 9 == 0)
                if mt >= 2:
                    pp, c0_, c1_ = st_after[mt - 2]
                    v.wait_ge(stp[par].h, bstp[par] + (c0_ if par == 0 else c1_))
                for nt in range(NT2):
                    idx = mt * NT2 + nt
                    bank = (mt % TIF) * NT2 + nt
                    v.wait_ge(mmd.h, bmmd + mt + 1)
                    if idx >= 2:
                        v.wait_ge(sev.h, bsev + idx - 1)
                    vps.inc(v.tensor_tensor(t_sb[:, (idx % 2) * NW:(idx % 2) * NW + NW],
                                            ps[:, bank * 512:bank * 512 + NW],
                                            bias_sb[:, nt * NW:nt * NW + NW], ALU.add))
                if ispre:
                    v.wait_ge(sev.h, bsev + NT2 * (mt + 1))
                    vev.inc(v.tensor_scalar_mul(o_sb[:, par * Nout:(par + 1) * Nout],
                                                o_sb[:, par * Nout:(par + 1) * Nout],
                                                m_sb[:, 0:1]))

        @block.scalar
        def _(s):
            ph.entry(s)
            for mt in range(MCH):
                par = mt % 2
                for nt in range(NT2):
                    idx = mt * NT2 + nt
                    s.wait_ge(vps.h, bvps + idx + 1)
                    if mt >= 2 and nt == 0:
                        pp, c0_, c1_ = st_after[mt - 2]
                        s.wait_ge(stp[par].h, bstp[par] + (c0_ if par == 0 else c1_))
                    fn = AF.Copy if nt < 2 else AF.Sigmoid
                    sev.inc(s.activation(o_sb[:, par * Nout + nt * NW:
                                                 par * Nout + nt * NW + NW],
                                         t_sb[:, (idx % 2) * NW:(idx % 2) * NW + NW], fn))

        ph.done = [(stp[0], stp[0].n), (stp[1], stp[1].n)]
    return ph


def ph_scan0(nc, es, pool, deps, pu_d, pg_d, a_d, hs_d):
    ph = Ph(nc, pool, deps)
    H = SC_H
    HW = 32
    NW_ = SC_STEPS // HW
    WOWN = SC_W // HW
    HWC = HW * SC_F
    puq = pu_d.rearrange("(tl q) f -> q tl f", q=128)
    pgq = pg_d.rearrange("(tl q) f -> q tl f", q=128)
    hs5 = hs_d.rearrange("(b k t) (gb f) -> b k gb t f", b=8, k=8, gb=2)
    with ExitStack() as ees:
        E = ees.enter_context
        u_sb = E(nc.sbuf_tensor("scu", [128, 2 * BLKC], BF16))
        g_sb = E(nc.sbuf_tensor("scg", [128, 2 * BLKC], BF16))
        h_sb = E(nc.sbuf_tensor("sch", [128, 2 * HWC], BF16))
        a_sb = E(nc.sbuf_tensor("sca", [128, SC_F], BF16))
        z_sb = E(nc.sbuf_tensor("scz", [128, SC_F], BF16))
        v_sb = E(nc.sbuf_tensor("scv", [128, SC_F], BF16))
        c_sb = E(nc.sbuf_tensor("scc", [128, SC_F], BF16))
        d_sb = E(nc.sbuf_tensor("scd", [128, SC_F], BF16))
        cld = ph.sem()
        ld = [ph.sem(), ph.sem()]
        vs = ph.sem(); cs = ph.sem()
        st3 = [ph.sem(), ph.sem(), ph.sem()]
        bcld = cld.n; bld = [x.n for x in ld]; bvs = vs.n; bcs = cs.n
        bst3 = [x.n for x in st3]
        shares = [list(range(0, 22)), list(range(22, 43)), list(range(43, 64))]
        block = E(nc.Block())

        def emit_spills(eng, qi, w):
            t0 = w * HW - SC_W
            par = w % 2
            eng.wait_ge(vs.h, bvs + 4 * HW * (w + 1))
            for pair in shares[qi]:
                k, b = pair // 8, pair % 8
                q0 = k * 16 + b * 2
                st3[qi].inc(eng.dma_start(
                    out=hs5[b, k, :, t0:t0 + HW, :],
                    in_=h_sb[par * HWC:par * HWC, 0:0] if False else
                        h_sb[q0:q0 + 2, par * HWC:(par + 1) * HWC]
                        .rearrange("p (t f) -> p t f", t=HW)), 16)

        @block.sync
        def _(sy):
            ph.entry(sy)
            for w in range(WOWN, NW_):
                emit_spills(sy, 1, w)

        @block.scalar
        def _(s_):
            ph.entry(s_)
            spw = iter(range(WOWN, NW_))
            nxt = next(spw, None)
            for s in range(SC_STEPS):
                for hf in range(2):
                    c0, c1 = hf * H, (hf + 1) * H
                    s_.wait_ge(vs.h, bvs + 4 * s + 1 + hf)
                    cs.inc(s_.activation(c_sb[:, c0:c1], v_sb[:, c0:c1], AF.Tanh))
                if nxt is not None and s == (nxt + 1) * HW - 1:
                    emit_spills(s_, 2, nxt)
                    nxt = next(spw, None)

        @block.gpsimd
        def _(g):
            ph.entry(g)
            cld.inc(g.dma_start(out=a_sb[:], in_=a_d), 16)
            for blk in range(NBLK):
                par = blk % 2
                if blk >= 2:
                    g.wait_ge(vs.h, bvs + 4 * SC_BLK * (blk - 1))
                ld[par].inc(g.dma_start(
                    out=u_sb[:, par * BLKC:(par + 1) * BLKC]
                        .rearrange("p (t f) -> p t f", t=SC_BLK),
                    in_=puq[:, blk * SC_BLK:(blk + 1) * SC_BLK, :]), 16)
                ld[par].inc(g.dma_start(
                    out=g_sb[:, par * BLKC:(par + 1) * BLKC]
                        .rearrange("p (t f) -> p t f", t=SC_BLK),
                    in_=pgq[:, blk * SC_BLK:(blk + 1) * SC_BLK, :]), 16)
                if blk % 4 == 0 and blk // 4 - 1 >= WOWN:
                    emit_spills(g, 0, blk // 4 - 1)
            for w in range(NBLK // 4 - 1, NW_):
                if w >= WOWN:
                    emit_spills(g, 0, w)

        @block.vector
        def _(v):
            ph.entry(v)
            v.memset(z_sb[:], 0.0)
            for s in range(SC_STEPS):
                blk = s // SC_BLK
                sl = s % SC_BLK
                par = blk % 2
                w = s // HW
                wl = s % HW
                wpar = w % 2
                if sl == 0:
                    if blk == 0:
                        v.wait_ge(cld.h, bcld + 16)
                    v.wait_ge(ld[par].h, bld[par] + 32 * (blk // 2 + 1))
                if wl == 0 and w >= 2 and w - 2 >= WOWN:
                    for qi in range(3):
                        nsp = len(shares[qi]) * 16 * (w - 2 - WOWN + 1)
                        v.wait_ge(st3[qi].h, bst3[qi] + nsp)
                if s == 0:
                    hprev = z_sb[:]
                elif wl == 0:
                    hprev = h_sb[:, (1 - wpar) * HWC + (HW - 1) * SC_F:
                                    (1 - wpar) * HWC + HW * SC_F]
                else:
                    hprev = h_sb[:, wpar * HWC + (wl - 1) * SC_F:wpar * HWC + wl * SC_F]
                uap = u_sb[:, par * BLKC + sl * SC_F:par * BLKC + (sl + 1) * SC_F]
                gap = g_sb[:, par * BLKC + sl * SC_F:par * BLKC + (sl + 1) * SC_F]
                hap = h_sb[:, wpar * HWC + wl * SC_F:wpar * HWC + (wl + 1) * SC_F]
                for hf in range(2):
                    c0, c1 = hf * H, (hf + 1) * H
                    if s >= 1:
                        v.wait_ge(cs.h, bcs + 2 * (s - 1) + hf + 1)
                    v.tensor_mul(v_sb[:, c0:c1], a_sb[:, c0:c1], hprev[:, c0:c1])
                    vs.inc(v.tensor_add(v_sb[:, c0:c1], v_sb[:, c0:c1], uap[:, c0:c1]))
                for hf in range(2):
                    c0, c1 = hf * H, (hf + 1) * H
                    v.wait_ge(cs.h, bcs + 2 * s + hf + 1)
                    v.tensor_sub(d_sb[:, c0:c1], hprev[:, c0:c1], c_sb[:, c0:c1])
                    v.tensor_mul(d_sb[:, c0:c1], gap[:, c0:c1], d_sb[:, c0:c1])
                    vs.inc(v.tensor_add(hap[:, c0:c1], d_sb[:, c0:c1], c_sb[:, c0:c1]))

        ph.done = [(st3[i], st3[i].n) for i in range(3)]
    return ph


def ph_shift(nc, es, pool, deps, xn_d, mur_d, muk_d, xr_d, xk_d):
    ph = Ph(nc, pool, deps)
    with ExitStack() as ees:
        E = ees.enter_context
        xc_sb = E(nc.sbuf_tensor("shx", [128, 2 * RB], BF16))
        d_sb = E(nc.sbuf_tensor("shd", [128, ROWS], F32))
        o_sb = E(nc.sbuf_tensor("sho", [128, 2 * 2 * ROWS], BF16))
        mu_sb = E(nc.sbuf_tensor("shmu", [128, 16], F32))
        cld = ph.sem(); tld = [ph.sem(), ph.sem()]
        dn = ph.sem(); stp = [ph.sem(), ph.sem()]
        bcld = cld.n; btld = [x.n for x in tld]; bdn = dn.n; bstp = [x.n for x in stp]
        block = E(nc.Block())

        @block.sync
        def _(sy):
            ph.entry(sy)
            cld.inc(sy.dma_start(out=mu_sb[:, 0:8], in_=mur_d), 16)
            cld.inc(sy.dma_start(out=mu_sb[:, 8:16], in_=muk_d), 16)
            for i in range(64):
                c, b = i // 8, i % 8
                par = i % 2
                if i >= 2:
                    sy.wait_ge(dn.h, bdn + i - 1)
                tld[par].inc(sy.dma_start_transpose(
                    xc_sb[:, par * RB:(par + 1) * RB],
                    xn_d[b * RB:(b + 1) * RB, c * 128:(c + 1) * 128]), 16)

        @block.vector
        def _(v):
            ph.entry(v)
            v.wait_ge(cld.h, bcld + 32)
            for i in range(64):
                c, b = i // 8, i % 8
                par = i % 2
                v.wait_ge(tld[par].h, btld[par] + 16 * (i // 2 + 1))
                if i >= 2:
                    v.wait_ge(stp[par].h, bstp[par] + 32 * (i // 2))
                xo = xc_sb[:, par * RB + PRE:par * RB + RB]
                xp = xc_sb[:, par * RB + PRE - 1:par * RB + RB - 1]
                v.tensor_sub(d_sb[:], xo, xp)
                v.scalar_tensor_tensor(o_sb[:, par * 2 * ROWS:par * 2 * ROWS + ROWS],
                                       d_sb[:], mu_sb[:, c:c + 1], xp,
                                       ALU.mult, ALU.add)
                dn.inc(v.scalar_tensor_tensor(
                    o_sb[:, par * 2 * ROWS + ROWS:par * 2 * ROWS + 2 * ROWS],
                    d_sb[:], mu_sb[:, 8 + c:8 + c + 1], xp, ALU.mult, ALU.add))

        @block.scalar
        def _(s):
            ph.entry(s)
            for i in range(64):
                c, b = i // 8, i % 8
                par = i % 2
                s.wait_ge(dn.h, bdn + i + 1)
                stp[par].inc(s.dma_start(out=xr_d[c * 128:(c + 1) * 128,
                                                  b * ROWS:(b + 1) * ROWS],
                                         in_=o_sb[:, par * 2 * ROWS:par * 2 * ROWS + ROWS]),
                             16)
                stp[par].inc(s.dma_start(out=xk_d[c * 128:(c + 1) * 128,
                                                  b * ROWS:(b + 1) * ROWS],
                                         in_=o_sb[:, par * 2 * ROWS + ROWS:
                                                     par * 2 * ROWS + 2 * ROWS]), 16)

        ph.done = [(stp[0], stp[0].n), (stp[1], stp[1].n)]
    return ph

def ph_au_states(nc, es, pool, deps, h_d, wau_d, oh_d, agin_d, agout_d, states_d):
    ph = Ph(nc, pool, deps)
    KC = 16
    TG = TOWN // 512
    with ExitStack() as ees:
        E = ees.enter_context
        w_sb = E(nc.sbuf_tensor("auw", [128, KC * 64], BF16))
        a_sb = E(nc.sbuf_tensor("aua", [128, 2 * KC * 512], BF16))
        au_sb = E(nc.sbuf_tensor("auau", [64, TOWN], F32))
        al_sb = E(nc.sbuf_tensor("aual", [128, ROWS], F32))
        ul_sb = E(nc.sbuf_tensor("auul", [128, ROWS], F32))
        ones_sb = E(nc.sbuf_tensor("auon", [128, ROWS], F32))
        sl_sb = E(nc.sbuf_tensor("ausl", [128, ROWS], F32))
        cl_sb = E(nc.sbuf_tensor("aucl", [128, ROWS], F32))
        agi_sb = E(nc.sbuf_tensor("auag", [128, 2], F32))
        aa_sb = E(nc.sbuf_tensor("auaa", [128, 8], F32))
        uu_sb = E(nc.sbuf_tensor("auuu", [128, 8], F32))
        ss_sb = E(nc.sbuf_tensor("auss", [128, 8], F32))
        oh_sb = E(nc.sbuf_tensor("auoh", [128, 8], F32))
        si_sb = E(nc.sbuf_tensor("ausi", [128, 1], F32))
        st_out = E(nc.sbuf_tensor("aust", [128, ROWS], F32))
        ps = E(nc.psum_tensor("aups", [64, 2 * 512], F32))
        wld = ph.sem(); ald = [ph.sem(), ph.sem()]
        mmd = ph.sem(); sev = ph.sem(); locd = ph.sem(); vv = ph.sem()
        cc = ph.sem(); agl = ph.sem(); std = ph.sem()
        bwld = wld.n; bald = [x.n for x in ald]; bmmd = mmd.n; bsev = sev.n
        blocd = locd.n; bvv = vv.n; bcc = cc.n; bagl = agl.n; bstd = std.n
        block = E(nc.Block())

        @block.sync
        def _(sy):
            ph.entry(sy)
            for gi in range(TG):
                par = gi % 2
                if gi >= 2:
                    sy.wait_ge(mmd.h, bmmd + gi - 1)
                for k in range(KC):
                    ald[par].inc(sy.dma_start_transpose(
                        a_sb[:, par * KC * 512 + k * 512:par * KC * 512 + (k + 1) * 512],
                        h_d[gi * 512:(gi + 1) * 512, k * 128:(k + 1) * 128]), 16)

        @block.tensor
        def _(t):
            ph.entry(t)
            t.wait_ge(wld.h, bwld + 16 * 2)
            for gi in range(TG):
                par = gi % 2
                t.wait_ge(ald[par].h, bald[par] + 16 * KC * (gi // 2 + 1))
                bank = gi % 2
                if gi >= 2:
                    t.wait_ge(sev.h, bsev + gi - 1)
                for k in range(KC):
                    ins = t.matmul(ps[:, bank * 512:(bank + 1) * 512],
                                   w_sb[:, k * 64:(k + 1) * 64],
                                   a_sb[:, par * KC * 512 + k * 512:
                                           par * KC * 512 + (k + 1) * 512],
                                   start=(k == 0), stop=(k == KC - 1))
                mmd.inc(ins)

        @block.scalar
        def _(s):
            ph.entry(s)
            for gi in range(TG):
                bank = gi % 2
                s.wait_ge(mmd.h, bmmd + gi + 1)
                s.activation(au_sb[0:16, gi * 512:(gi + 1) * 512],
                             ps[0:16, bank * 512:(bank + 1) * 512], AF.Sigmoid)
                sev.inc(s.activation(au_sb[32:48, gi * 512:(gi + 1) * 512],
                                     ps[32:48, bank * 512:(bank + 1) * 512], AF.Copy))

        @block.gpsimd
        def _(g):
            ph.entry(g)
            wld.inc(g.dma_start(out=w_sb[:, 0:KC * 64].rearrange("p (k n) -> p k n", k=KC),
                                in_=wau_d.rearrange("(k p) n -> p k n", p=128)), 16)
            wld.inc(g.dma_start(out=oh_sb[:], in_=oh_d), 16)
            g.wait_ge(sev.h, bsev + TG)
            for b in range(8):
                locd.inc(g.dma_start(out=al_sb[b * 16:(b + 1) * 16, :],
                                     in_=au_sb[0:16, b * ROWS:(b + 1) * ROWS]), 16)
                locd.inc(g.dma_start(out=ul_sb[b * 16:(b + 1) * 16, :],
                                     in_=au_sb[32:48, b * ROWS:(b + 1) * ROWS]), 16)
            g.wait_ge(vv.h, bvv + 2)
            agl.inc(g.dma_start(out=agin_d, in_=agi_sb[:]), 16)
            g.wait_ge(agl.h, bagl + 16)
            cc.inc(g.collective_compute(
                "AllGather", ALU.bypass, replica_groups=[list(range(NCORES))],
                ins=[agin_d.opt()], outs=[agout_d.opt()]))
            g.wait_ge(cc.h, bcc + 1)
            for j in range(8):
                agl.inc(g.dma_start(out=aa_sb[:, j:j + 1],
                                    in_=agout_d[j * 128:(j + 1) * 128, 0:1],
                                    allow_slow_non_contiguous=True), 16)
                agl.inc(g.dma_start(out=uu_sb[:, j:j + 1],
                                    in_=agout_d[j * 128:(j + 1) * 128, 1:2],
                                    allow_slow_non_contiguous=True), 16)
            g.wait_ge(vv.h, bvv + 6)
            std.inc(g.dma_start(out=states_d, in_=st_out[:]), 16)
            g.wait_ge(std.h, bstd + 16)

        @block.vector
        def _(v):
            ph.entry(v)
            v.memset(ones_sb[:], 1.0)
            v.wait_ge(locd.h, blocd + 16 * 16)
            v.tensor_tensor_scan(sl_sb[:], al_sb[:], ul_sb[:], 0.0, ALU.mult, ALU.add)
            vv.inc(v.tensor_tensor_scan(cl_sb[:], al_sb[:], ones_sb[:], 1.0,
                                        ALU.mult, ALU.mult))
            v.wait_ge(vv.h, bvv + 1)
            v.tensor_copy(agi_sb[:, 0:1], cl_sb[:, ROWS - 1:ROWS])
            vv.inc(v.tensor_copy(agi_sb[:, 1:2], sl_sb[:, ROWS - 1:ROWS]))
            v.wait_ge(agl.h, bagl + 16 * 17)
            vv.inc(v.tensor_tensor_scan(ss_sb[:], aa_sb[:], uu_sb[:], 0.0,
                                        ALU.mult, ALU.add))
            v.wait_ge(vv.h, bvv + 3)
            vv.inc(v.tensor_mul(ss_sb[:], ss_sb[:], oh_sb[:]))
            v.wait_ge(vv.h, bvv + 4)
            vv.inc(v.reduce_sum(si_sb[:], ss_sb[:], axis=AX.X))
            v.wait_ge(vv.h, bvv + 5)
            vv.inc(v.scalar_tensor_tensor(st_out[:], cl_sb[:], si_sb[:, 0:1],
                                          sl_sb[:], ALU.mult, ALU.add))

        ph.done = [(std, std.n)]
    return ph


def ph_s1p(nc, es, pool, deps, states_d, cssm_d, woT_d, bout_d, xn1_d, s1p_d):
    ph = Ph(nc, pool, deps)
    with ExitStack() as ees:
        E = ees.enter_context
        c_sb = E(nc.sbuf_tensor("s1c", [128, 16 * 16], BF16))
        wo_sb = E(nc.sbuf_tensor("s1wo", [128, 16 * D], BF16))
        gf_sb = E(nc.sbuf_tensor("s1gf", [16, D], F32))
        gb_sb = E(nc.sbuf_tensor("s1gb", [16, D], BF16))
        sts_sb = E(nc.sbuf_tensor("s1st", [16, TOWN], BF16))
        bo_sb = E(nc.sbuf_tensor("s1bo", [128, D], F32))
        e_sb = E(nc.sbuf_tensor("s1e", [128, 2 * D], BF16))
        t_sb = E(nc.sbuf_tensor("s1t", [128, 2 * 512], F32))
        o_sb = E(nc.sbuf_tensor("s1o", [128, 2 * D], BF16))
        psg = E(nc.psum_tensor("s1pg", [16, 1024], F32))
        ps = E(nc.psum_tensor("s1ps", [128, 4 * 512], F32))
        wld = ph.sem(); eld = [ph.sem(), ph.sem()]; gdn = ph.sem()
        mmd = ph.sem(); vev = ph.sem(); stp = [ph.sem(), ph.sem()]
        bwld = wld.n; beld = [x.n for x in eld]; bgdn = gdn.n
        bmmd = mmd.n; bvev = vev.n; bstp = [x.n for x in stp]
        block = E(nc.Block())

        @block.gpsimd
        def _(g):
            ph.entry(g)
            for k in range(16):
                wld.inc(g.dma_start(out=c_sb[:, k * 16:(k + 1) * 16],
                                    in_=cssm_d[k * 128:(k + 1) * 128, :]), 16)
            for k in range(16):
                wld.inc(g.dma_start(out=wo_sb[:, k * D:(k + 1) * D],
                                    in_=woT_d[k * 128:(k + 1) * 128, :]), 16)
            wld.inc(g.dma_start(out=bo_sb[:], in_=bout_d), 16)
            wld.inc(g.dma_start(out=sts_sb[:].rearrange("p (b r) -> p b r", b=8),
                                in_=states_d.rearrange("(b n) r -> n b r", n=16)), 16)
            for mt in range(TOWN // 128):
                par = mt % 2
                g.wait_ge(vev.h, bvev + mt + 1)
                stp[par].inc(g.dma_start(out=s1p_d[mt * 128:(mt + 1) * 128, :],
                                         in_=o_sb[:, par * D:(par + 1) * D]), 16)

        @block.sync
        def _(sy):
            ph.entry(sy)
            for mt in range(TOWN // 128):
                par = mt % 2
                if mt >= 2:
                    sy.wait_ge(vev.h, bvev + mt - 1)
                eld[par].inc(sy.dma_start(out=e_sb[:, par * D:(par + 1) * D],
                                          in_=xn1_d[mt * 128:(mt + 1) * 128, :]), 16)

        @block.tensor
        def _(t):
            ph.entry(t)
            t.wait_ge(wld.h, bwld + 16 * 34)
            for k in range(16):
                ins = t.matmul(psg[:, 0:512], c_sb[:, k * 16:(k + 1) * 16],
                               wo_sb[:, k * D:k * D + 512], start=(k == 0), stop=(k == 15))
            for k in range(16):
                ins = t.matmul(psg[:, 512:1024], c_sb[:, k * 16:(k + 1) * 16],
                               wo_sb[:, k * D + 512:(k + 1) * D],
                               start=(k == 0), stop=(k == 15))
            mmd.inc(ins)
            t.wait_ge(gdn.h, bgdn + 1)
            for mt in range(TOWN // 128):
                for nt in range(2):
                    idx = mt * 2 + nt
                    bank = idx % 4
                    if idx >= 4:
                        t.wait_ge(vev.h, bvev + (idx - 3 + 1) // 2)
                    mmd.inc(t.matmul(ps[:, bank * 512:bank * 512 + 512],
                                     sts_sb[:, mt * 128:(mt + 1) * 128],
                                     gb_sb[:, nt * 512:(nt + 1) * 512],
                                     start=True, stop=True))

        @block.vector
        def _(v):
            ph.entry(v)
            v.wait_ge(mmd.h, bmmd + 1)
            v.tensor_copy(gf_sb[:], psg[:, 0:1024])
            gdn.inc(v.tensor_copy(gb_sb[:], gf_sb[:]))
            for mt in range(TOWN // 128):
                par = mt % 2
                v.wait_ge(eld[par].h, beld[par] + 16 * (mt // 2 + 1))
                if mt >= 2:
                    v.wait_ge(stp[par].h, bstp[par] + 16 * (mt // 2))
                for nt in range(2):
                    idx = mt * 2 + nt
                    bank = idx % 4
                    v.wait_ge(mmd.h, bmmd + idx + 2)
                    v.tensor_tensor(t_sb[:, nt * 512:(nt + 1) * 512],
                                    ps[:, bank * 512:bank * 512 + 512],
                                    bo_sb[:, nt * 512:(nt + 1) * 512], ALU.add)
                ins = v.tensor_tensor(o_sb[:, par * D:(par + 1) * D], t_sb[:, 0:1024],
                                      e_sb[:, par * D:(par + 1) * D], ALU.add)
                vev.inc(ins)

        ph.done = [(stp[0], stp[0].n), (stp[1], stp[1].n)]
    return ph


def ph_mma(nc, es, pool, deps, name, K, Mout, src_d, w_d, out_d):
    ph = Ph(nc, pool, deps)
    KC = K // 128
    MC = Mout // 128
    TG = TOWN // 512
    with ExitStack() as ees:
        E = ees.enter_context
        w_sb = E(nc.sbuf_tensor(name + "w", [128, KC * Mout], BF16))
        a_sb = E(nc.sbuf_tensor(name + "a", [128, 2 * KC * 512], BF16))
        o_sb = E(nc.sbuf_tensor(name + "o", [128, 2 * 512], BF16))
        ps = E(nc.psum_tensor(name + "ps", [128, 4 * 512], F32))
        wld = ph.sem(); ald = [ph.sem(), ph.sem()]
        mmd = ph.sem(); vev = ph.sem(); stp = [ph.sem(), ph.sem()]
        bwld = wld.n; bald = [x.n for x in ald]; bmmd = mmd.n; bvev = vev.n
        bstp = [x.n for x in stp]
        block = E(nc.Block())

        @block.gpsimd
        def _(g):
            ph.entry(g)
            for k in range(KC):
                wld.inc(g.dma_start(out=w_sb[:, k * Mout:(k + 1) * Mout],
                                    in_=w_d[k * 128:(k + 1) * 128, :]), 16)
            for gi in range(TG):
                for m in range(MC):
                    idx = gi * MC + m
                    par = idx % 2
                    g.wait_ge(vev.h, bvev + idx + 1)
                    stp[par].inc(g.dma_start(
                        out=out_d[m * 128:(m + 1) * 128, gi * 512:(gi + 1) * 512],
                        in_=o_sb[:, par * 512:(par + 1) * 512]), 16)

        @block.sync
        def _(sy):
            ph.entry(sy)
            for gi in range(TG):
                par = gi % 2
                if gi >= 2:
                    sy.wait_ge(mmd.h, bmmd + (gi - 1) * MC)
                for k in range(KC):
                    ald[par].inc(sy.dma_start_transpose(
                        a_sb[:, par * KC * 512 + k * 512:par * KC * 512 + (k + 1) * 512],
                        src_d[gi * 512:(gi + 1) * 512, k * 128:(k + 1) * 128]), 16)

        @block.tensor
        def _(t):
            ph.entry(t)
            t.wait_ge(wld.h, bwld + 16 * KC)
            for gi in range(TG):
                par = gi % 2
                t.wait_ge(ald[par].h, bald[par] + 16 * KC * (gi // 2 + 1))
                for m in range(MC):
                    idx = gi * MC + m
                    bank = idx % 4
                    if idx >= 4:
                        t.wait_ge(vev.h, bvev + idx - 3)
                    for k in range(KC):
                        ins = t.matmul(ps[:, bank * 512:(bank + 1) * 512],
                                       w_sb[:, k * Mout + m * 128:k * Mout + (m + 1) * 128],
                                       a_sb[:, par * KC * 512 + k * 512:
                                               par * KC * 512 + (k + 1) * 512],
                                       start=(k == 0), stop=(k == KC - 1))
                    mmd.inc(ins)

        @block.vector
        def _(v):
            ph.entry(v)
            for gi in range(TG):
                for m in range(MC):
                    idx = gi * MC + m
                    par = idx % 2
                    bank = idx % 4
                    v.wait_ge(mmd.h, bmmd + idx + 1)
                    if idx >= 2:
                        v.wait_ge(stp[par].h, bstp[par] + 16 * (idx // 2))
                    vev.inc(v.tensor_copy(o_sb[:, par * 512:(par + 1) * 512],
                                          ps[:, bank * 512:(bank + 1) * 512]))

        ph.done = [(stp[0], stp[0].n), (stp[1], stp[1].n)]
    return ph

def ph_ffn1loop(nc, es, pool, deps, xw_d, u3_d, b3_d, c0_d, id_d, cf_d, RC=512):
    ph = Ph(nc, pool, deps)
    NPASS = 1024 // RC
    MC = D // 128
    NT = RC
    NB = B

    def mm_after(step, m, gi):
        return step * 3 * MC + m * 3 + gi + 1

    def upd_after(p, t, m):
        return p * (1 + NB * MC) + 1 + t * MC + m + 1

    def upd_init(p):
        return p * (1 + NB * MC) + 1

    with ExitStack() as ees:
        E = ees.enter_context
        u_sb = E(nc.sbuf_tensor("f1u", [128, MC * 3 * D], BF16))
        id_sb = E(nc.sbuf_tensor("f1id", [128, 128], BF16))
        b_sb = E(nc.sbuf_tensor("f1b", [128, 24], F32))
        c_sb = E(nc.sbuf_tensor("f1c", [128, MC * RC], F32))
        cb_sb = E(nc.sbuf_tensor("f1cb", [128, 2 * MC * RC], BF16))
        xw_sb = E(nc.sbuf_tensor("f1xw", [128, 2 * 3 * MC * RC], BF16))
        ga_sb = E(nc.sbuf_tensor("f1ga", [128, 2 * 3 * NT], F32))
        ps = E(nc.psum_tensor("f1ps", [128, 8 * 512], F32))
        wld = ph.sem(); xld = [ph.sem(), ph.sem()]
        mmd = ph.sem(); gev = ph.sem(); upd = ph.sem(); stq = ph.sem()
        bwld = wld.n; bxld = [x.n for x in xld]; bmmd = mmd.n
        bgev = gev.n; bupd = upd.n; bstq = stq.n
        block = E(nc.Block())

        @block.gpsimd
        def _(g):
            ph.entry(g)
            for k in range(MC):
                wld.inc(g.dma_start(out=u_sb[:, k * 3 * D:(k + 1) * 3 * D],
                                    in_=u3_d[k * 128:(k + 1) * 128, :]), 16)
            wld.inc(g.dma_start(out=id_sb[:], in_=id_d), 16)
            wld.inc(g.dma_start(out=b_sb[:], in_=b3_d), 16)
            for p in range(NPASS):
                if p >= 1:
                    g.wait_ge(stq.h, bstq + 16 * p)
                wld.inc(g.dma_start(out=c_sb[:].rearrange("p (m r) -> p m r", m=MC),
                                    in_=c0_d.rearrange("(m p) r -> p m r", p=128)
                                    [:, :, p * RC:(p + 1) * RC]), 16)
                for t in range(NB):
                    step = p * NB + t
                    if step >= 2:
                        g.wait_ge(gev.h, bgev + (step - 1) * 3 * MC)
                    for gi in range(3):
                        xld[t % 2].inc(g.dma_start(
                            out=xw_sb[:, ((t % 2) * 3 + gi) * MC * RC:
                                         ((t % 2) * 3 + gi + 1) * MC * RC]
                                .rearrange("p (m r) -> p m r", m=MC),
                            in_=xw_d.rearrange("(m p) t -> p m t", p=128)
                                [:, gi * MC:(gi + 1) * MC,
                                 t * 1024 + p * RC:t * 1024 + (p + 1) * RC]), 16)
                g.wait_ge(upd.h, bupd + upd_after(p, NB - 1, MC - 1))
                stq.inc(g.dma_start(out=cf_d.rearrange("(m p) r -> p m r", p=128)
                                    [:, :, p * RC:(p + 1) * RC],
                                    in_=c_sb[:].rearrange("p (m r) -> p m r", m=MC)), 16)
            g.wait_ge(stq.h, bstq + 16 * NPASS)

        @block.tensor
        def _(t_):
            ph.entry(t_)
            t_.wait_ge(wld.h, bwld + 16 * (MC + 2))
            for p in range(NPASS):
                t_.wait_ge(wld.h, bwld + 16 * (MC + 2 + p + 1))
                for t in range(NB):
                    step = p * NB + t
                    t_.wait_ge(xld[t % 2].h, bxld[t % 2] + 48 * (step // 2 + 1))
                    t_.wait_ge(upd.h, bupd + (upd_after(p, t - 1, MC - 1) if t > 0
                                              else upd_init(p)))
                    for m in range(MC):
                        for gi in range(3):
                            bidx = gi * 2 + (m % 2)
                            if m >= 2:
                                t_.wait_ge(gev.h, bgev + mm_after(step, m - 2, gi))
                            elif step >= 1:
                                t_.wait_ge(gev.h, bgev + mm_after(step - 1, MC - 2 + m, gi))
                            t_.matmul(ps[:, bidx * 512:bidx * 512 + NT], id_sb[:],
                                      xw_sb[:, ((t % 2) * 3 + gi) * MC * RC + m * RC:
                                               ((t % 2) * 3 + gi) * MC * RC + m * RC + NT],
                                      start=True, stop=False)
                            for k in range(MC):
                                ins = t_.matmul(
                                    ps[:, bidx * 512:bidx * 512 + NT],
                                    u_sb[:, k * 3 * D + gi * D + m * 128:
                                            k * 3 * D + gi * D + (m + 1) * 128],
                                    cb_sb[:, ((step - 1) % 2) * MC * RC + k * RC:
                                             ((step - 1) % 2) * MC * RC + k * RC + NT],
                                    start=False, stop=(k == MC - 1))
                            mmd.inc(ins)

        @block.scalar
        def _(s):
            ph.entry(s)
            AFS = [AF.Sigmoid, AF.Sigmoid, AF.Tanh]
            for p in range(NPASS):
                for t in range(NB):
                    step = p * NB + t
                    for m in range(MC):
                        for gi in range(3):
                            s.wait_ge(mmd.h, bmmd + mm_after(step, m, gi))
                            if m >= 1:
                                s.wait_ge(upd.h, bupd + upd_after(p, t, m - 1))
                            elif step >= 1:
                                pp, tt = (p, t - 1) if t > 0 else (p - 1, NB - 1)
                                s.wait_ge(upd.h, bupd + upd_after(pp, tt, MC - 1))
                            bidx = gi * 2 + (m % 2)
                            gev.inc(s.activation(
                                ga_sb[:, ((step % 2) * 3 + gi) * NT:
                                         ((step % 2) * 3 + gi) * NT + NT],
                                ps[:, bidx * 512:bidx * 512 + NT],
                                AFS[gi], bias=b_sb[:, gi * MC + m:gi * MC + m + 1]))

        @block.vector
        def _(v):
            ph.entry(v)
            for p in range(NPASS):
                v.wait_ge(wld.h, bwld + 16 * (MC + 2 + p + 1))
                for k in range(MC):
                    ins = v.tensor_copy(cb_sb[:, MC * RC + k * RC:MC * RC + (k + 1) * RC],
                                        c_sb[:, k * RC:(k + 1) * RC])
                upd.inc(ins)
                for t in range(NB):
                    step = p * NB + t
                    for m in range(MC):
                        v.wait_ge(gev.h, bgev + step * 3 * MC + m * 3 + 3)
                        sp = (step % 2) * 3
                        f_ap = ga_sb[:, (sp + 0) * NT:(sp + 0) * NT + NT]
                        i_ap = ga_sb[:, (sp + 1) * NT:(sp + 1) * NT + NT]
                        g_ap = ga_sb[:, (sp + 2) * NT:(sp + 2) * NT + NT]
                        c_ap = c_sb[:, m * RC:m * RC + NT]
                        cb_ap = cb_sb[:, (step % 2) * MC * RC + m * RC:
                                         (step % 2) * MC * RC + m * RC + NT]
                        v.tensor_mul(c_ap, f_ap, c_ap)
                        v.tensor_mul(i_ap, i_ap, g_ap)
                        v.tensor_add(c_ap, c_ap, i_ap)
                        upd.inc(v.tensor_copy(cb_ap, c_ap))

        ph.done = [(stq, stq.n)]
    return ph


def ph_final(nc, es, pool, deps, cf_d, uoT_d, bo_d, xwo_d, s1p_d, x1_d, x2_d):
    ph = Ph(nc, pool, deps)
    MC = 8
    with ExitStack() as ees:
        E = ees.enter_context
        uo_sb = E(nc.sbuf_tensor("fno", [128, MC * D], BF16))
        cb_sb = E(nc.sbuf_tensor("fncb", [128, MC * ROWS], BF16))
        bo_sb = E(nc.sbuf_tensor("fnbo", [128, D], F32))
        cuo_sb = E(nc.sbuf_tensor("fncuo", [128, 8 * D], F32))
        xw_sb = E(nc.sbuf_tensor("fnxw", [128, 2 * D], BF16))
        s1_sb = E(nc.sbuf_tensor("fns1", [128, 2 * D], BF16))
        x1_sb = E(nc.sbuf_tensor("fnx1", [128, 2 * D], F32))
        t_sb = E(nc.sbuf_tensor("fnt", [128, 2 * D], F32))
        t2_sb = E(nc.sbuf_tensor("fnt2", [128, 2 * D], F32))
        o_sb = E(nc.sbuf_tensor("fnoo", [128, 2 * D], F32))
        ps = E(nc.psum_tensor("fnps", [128, 4 * 512], F32))
        wld = ph.sem(); eld = [ph.sem(), ph.sem()]
        mmd = ph.sem(); cud = ph.sem(); vad = ph.sem(); tad = ph.sem()
        oad = ph.sem(); stp = [ph.sem(), ph.sem()]
        bwld = wld.n; beld = [x.n for x in eld]; bmmd = mmd.n; bcud = cud.n
        bvad = vad.n; btad = tad.n; boad = oad.n; bstp = [x.n for x in stp]
        block = E(nc.Block())

        @block.gpsimd
        def _(g):
            ph.entry(g)
            for k in range(MC):
                wld.inc(g.dma_start(out=uo_sb[:, k * D:(k + 1) * D],
                                    in_=uoT_d[k * 128:(k + 1) * 128, :]), 16)
            wld.inc(g.dma_start(out=cb_sb[:].rearrange("p (m r) -> p m r", m=MC),
                                in_=cf_d.rearrange("(m p) r -> p m r", p=128)), 16)
            wld.inc(g.dma_start(out=bo_sb[:], in_=bo_d), 16)
            for mt in range(TOWN // 128):
                par = mt % 2
                g.wait_ge(oad.h, boad + mt + 1)
                stp[par].inc(g.dma_start(out=x2_d[mt * 128:(mt + 1) * 128, :],
                                         in_=o_sb[:, par * D:(par + 1) * D]), 16)
            g.wait_ge(stp[0].h, bstp[0] + 16 * 32)
            g.wait_ge(stp[1].h, bstp[1] + 16 * 32)

        @block.sync
        def _(sy):
            ph.entry(sy)
            for mt in range(TOWN // 128):
                par = mt % 2
                if mt >= 2:
                    sy.wait_ge(oad.h, boad + mt - 1)
                eld[par].inc(sy.dma_start(out=xw_sb[:, par * D:(par + 1) * D],
                                          in_=xwo_d[mt * 128:(mt + 1) * 128, :]), 16)
                eld[par].inc(sy.dma_start(out=s1_sb[:, par * D:(par + 1) * D],
                                          in_=s1p_d[mt * 128:(mt + 1) * 128, :]), 16)
                eld[par].inc(sy.dma_start(out=x1_sb[:, par * D:(par + 1) * D],
                                          in_=x1_d[mt * 128:(mt + 1) * 128, :]), 16)

        @block.tensor
        def _(t):
            ph.entry(t)
            t.wait_ge(wld.h, bwld + 16 * (MC + 2))
            for rc in range(8):
                for nt in range(2):
                    idx = rc * 2 + nt
                    bank = idx % 4
                    if idx >= 4:
                        t.wait_ge(cud.h, bcud + idx - 3)
                    for k in range(MC):
                        ins = t.matmul(ps[:, bank * 512:(bank + 1) * 512],
                                       cb_sb[:, k * ROWS + rc * 128:k * ROWS + rc * 128 + 128],
                                       uo_sb[:, k * D + nt * 512:k * D + (nt + 1) * 512],
                                       start=(k == 0), stop=(k == MC - 1))
                    mmd.inc(ins)

        @block.vector
        def _(v):
            ph.entry(v)
            for rc in range(8):
                for nt in range(2):
                    idx = rc * 2 + nt
                    bank = idx % 4
                    v.wait_ge(mmd.h, bmmd + idx + 1)
                    cud.inc(v.tensor_copy(cuo_sb[:, rc * D + nt * 512:rc * D + (nt + 1) * 512],
                                          ps[:, bank * 512:(bank + 1) * 512]))
            for mt in range(TOWN // 128):
                par = mt % 2
                rc = mt % 8
                v.wait_ge(eld[par].h, beld[par] + 48 * (mt // 2 + 1))
                if mt >= 2:
                    v.wait_ge(stp[par].h, bstp[par] + 16 * (mt // 2))
                if mt >= 2:
                    v.wait_ge(tad.h, btad + mt - 1)
                v.tensor_tensor(t_sb[:, par * D:(par + 1) * D],
                                xw_sb[:, par * D:(par + 1) * D],
                                cuo_sb[:, rc * D:(rc + 1) * D], ALU.add)
                vad.inc(v.tensor_add(t_sb[:, par * D:(par + 1) * D],
                                     t_sb[:, par * D:(par + 1) * D], bo_sb[:]))
                v.wait_ge(tad.h, btad + mt + 1)
                v.tensor_add(o_sb[:, par * D:(par + 1) * D],
                             t2_sb[:, par * D:(par + 1) * D],
                             x1_sb[:, par * D:(par + 1) * D])
                oad.inc(v.tensor_add(o_sb[:, par * D:(par + 1) * D],
                                     o_sb[:, par * D:(par + 1) * D],
                                     s1_sb[:, par * D:(par + 1) * D]))

        @block.scalar
        def _(s):
            ph.entry(s)
            for mt in range(TOWN // 128):
                par = mt % 2
                s.wait_ge(vad.h, bvad + mt + 1)
                if mt >= 2:
                    s.wait_ge(oad.h, boad + mt - 1)
                tad.inc(s.activation(t2_sb[:, par * D:(par + 1) * D],
                                     t_sb[:, par * D:(par + 1) * D], AF.Tanh))

        ph.done = [(stp[0], stp[0].n), (stp[1], stp[1].n)]
    return ph

def build_fused(dbg=()):
    nc = bass.Bass("TRN2", target_bir_lowering=False, debug=False)

    def dram(nm, shape, dt):
        kind = "ExternalOutput" if nm in dbg else "Internal"
        return nc.dram_tensor(nm, shape, dt, kind=kind).ap()

    def inp(nm, shape, dt=F32):
        return nc.dram_tensor(nm, shape, dt, kind="ExternalInput").ap()

    x_d = inp("x", [TOK, D])
    ln0g_d = inp("ln0g", [128, D]); ln0b_d = inp("ln0b", [128, D])
    win0_d = inp("win0", [D, 2 * D]); bin0_d = inp("bin0", [128, 2 * D])
    mask_d = inp("mask", [128, 1]); asc_d = inp("asc", [128, SC_F])
    wout0_d = inp("wout0", [D, D]); bout0_d = inp("bout0", [128, D])
    mur_d = inp("mur", [128, 8]); muk_d = inp("muk", [128, 8])
    wrT_d = inp("wrT", [D, D]); wkT_d = inp("wkT", [D, D]); wvT_d = inp("wvT", [D, D])
    ln1g_d = inp("ln1g", [128, D]); ln1b_d = inp("ln1b", [128, D])
    winT1_d = inp("winT1", [D, 2 * D]); bin1_d = inp("bin1", [128, 2 * D])
    wau_d = inp("wau", [2 * D, 64]); oh_d = inp("oh", [128, 8])
    cssm_d = inp("cssm", [2 * D, 16]); woT1_d = inp("woT1", [2 * D, D])
    bout1_d = inp("bout1", [128, D])
    wficT_d = inp("wficT", [D, 3 * D]); wfoT_d = inp("wfoT", [D, D])
    u3_d = inp("u3", [D, 3 * D]); b3_d = inp("b3", [128, 24])
    c0_d = inp("c0", [D, ROWS]); id_d = inp("ident", [128, 128])
    uoT_d = inp("uoT", [D, D]); bo_d = inp("bo", [128, D])

    xn0_d = dram("xn0", [TOK, D], BF16)
    z_d = dram("z", [TOK, D], BF16)
    pu_d = dram("pu", [SC_STEPS * 128, SC_F], BF16)
    pg_d = dram("pg", [SC_STEPS * 128, SC_F], BF16)
    hs_d = dram("hs", [TOWN, D], BF16)
    attn0_d = dram("attn0", [TOWN, D], BF16)
    xr_d = dram("xr", [D, TOWN], BF16)
    xk_d = dram("xk", [D, TOWN], BF16)
    r_d = dram("r", [TOWN, D], BF16)
    kact_d = dram("kact", [TOWN, D], BF16)
    x1_d = dram("x1", [TOWN, D], F32)
    xn1_d = dram("xn1", [TOWN, D], BF16)
    xln_d = dram("xln", [TOWN, D], BF16)
    h_d = dram("h", [TOWN, 2 * D], BF16)
    agin_d = dram("agin", [128, 2], F32)
    agout_d = dram("agout", [NCORES * 128, 2], F32)
    states_d = dram("states", [128, ROWS], F32)
    s1p_d = dram("s1p", [TOWN, D], BF16)
    xw_d = dram("xw", [3 * D, TOWN], BF16)
    xwo_d = dram("xwo", [TOWN, D], BF16)
    cf_d = dram("cf", [D, ROWS], F32)
    x2_d = nc.dram_tensor("x2", [TOWN, D], F32, kind="ExternalOutput").ap()

    es = ExitStack()
    with es:
        pool = Pool(nc, es, n=48)
        p1 = ph_ln(nc, es, pool, [], TOK, x_d, ln0g_d, ln0b_d, xn0_d, z_d, True)
        p2 = ph_ug(nc, es, pool, p1.done, z_d, win0_d, bin0_d, mask_d, pu_d, pg_d)
        p3 = ph_scan0(nc, es, pool, p2.done, pu_d, pg_d, asc_d, hs_d)
        p4 = ph_mmb(nc, es, pool, p3.done, "at0", TOWN, D, D, ("t2", hs_d),
                    wout0_d, attn0_d, bias_d=bout0_d, acts=[(0, D, AF.Copy)])
        p5 = ph_shift(nc, es, pool, p4.done, xn0_d, mur_d, muk_d, xr_d, xk_d)
        p6 = ph_mmb(nc, es, pool, p5.done, "rr", TOWN, D, D, ("ch", xr_d),
                    wrT_d, r_d, acts=[(0, D, AF.Sigmoid)])
        p7 = ph_mmb(nc, es, pool, p6.done, "kk", TOWN, D, D, ("ch", xk_d),
                    wkT_d, kact_d, acts=[(0, D, AF.Relu)], relu_sq=True)

        xn0o = xn0_d.rearrange("(b rr) d -> b rr d", b=B)
        x_o = x_d.rearrange("(b rr) d -> b rr d", b=B)

        def rm_r(mt):
            return r_d[mt * 128:(mt + 1) * 128, :]

        def rm_at0(mt):
            return attn0_d[mt * 128:(mt + 1) * 128, :]

        def rm_xn0(mt):
            b, rc = mt // 8, mt % 8
            return xn0o[b, PRE + rc * 128:PRE + (rc + 1) * 128, :]

        def rm_x(mt):
            b, rc = mt // 8, mt % 8
            return x_o[b, PRE + rc * 128:PRE + (rc + 1) * 128, :]

        p8 = ph_mmb(nc, es, pool, p7.done, "x1", TOWN, D, D, ("t2", kact_d),
                    wvT_d, x1_d, out_dtype=F32,
                    extras=[(rm_r, ALU.mult), (rm_at0, ALU.add),
                            (rm_xn0, ALU.add), (rm_x, ALU.add)])
        p9 = ph_ln(nc, es, pool, p8.done, TOWN, x1_d, ln1g_d, ln1b_d, xn1_d, xln_d, False)
        p10 = ph_mmb(nc, es, pool, p9.done, "h1", TOWN, D, 2 * D, ("t2", xln_d),
                     winT1_d, h_d, bias_d=bin1_d, acts=[(0, 2 * D, AF.Copy)])
        p11 = ph_au_states(nc, es, pool, p10.done, h_d, wau_d, oh_d,
                           agin_d, agout_d, states_d)
        p13 = ph_s1p(nc, es, pool, p11.done, states_d, cssm_d, woT1_d, bout1_d,
                     xn1_d, s1p_d)
        p14 = ph_mma(nc, es, pool, p13.done, "xwf", D, 3 * D, xn1_d, wficT_d, xw_d)
        p15 = ph_mmb(nc, es, pool, p14.done, "xwo", TOWN, D, D, ("t2", xn1_d),
                     wfoT_d, xwo_d, acts=[(0, D, AF.Copy)])
        p16 = ph_ffn1loop(nc, es, pool, p15.done, xw_d, u3_d, b3_d, c0_d, id_d, cf_d)
        p17 = ph_final(nc, es, pool, p16.done, cf_d, uoT_d, bo_d, xwo_d, s1p_d,
                       x1_d, x2_d)
    return nc




def _t(a):
    return np.ascontiguousarray(np.asarray(a, np.float32).T)


def _bcast(v):
    a = np.asarray(v, np.float32)
    return np.ascontiguousarray(np.broadcast_to(a, (128, a.shape[0])))


def _col(v):
    a = np.asarray(v, np.float32)
    return np.ascontiguousarray(a.reshape(-1, 128).T)

def host_maps(inp):
    x = np.asarray(inp["L0_attn_x"], np.float32)
    A = np.asarray(inp["L0_attn_A"], np.float32)
    asc = np.empty((128, SC_F), np.float32)
    asc[0::2, :] = A[:512][None, :]
    asc[1::2, :] = A[512:][None, :]
    wau = np.zeros((2 * D, 64), np.float32)
    wau[:, 0:16] = np.asarray(inp["L1_attn_A"], np.float32)
    wau[:, 32:48] = np.asarray(inp["L1_attn_B_ssm"], np.float32)
    wfic = np.concatenate([_t(np.asarray(inp["L1_ffn_wf"], np.float32)),
                           _t(np.asarray(inp["L1_ffn_wi"], np.float32)),
                           _t(np.asarray(inp["L1_ffn_wc"], np.float32))], axis=1)
    u3 = np.concatenate([_t(np.asarray(inp["L1_ffn_uf"], np.float32)),
                         _t(np.asarray(inp["L1_ffn_ui"], np.float32)),
                         _t(np.asarray(inp["L1_ffn_uc"], np.float32))], axis=1)
    b3 = _col(np.concatenate([np.asarray(inp["L1_ffn_bf"], np.float32),
                              np.asarray(inp["L1_ffn_bi"], np.float32),
                              np.asarray(inp["L1_ffn_bc"], np.float32)]))
    c0 = np.ascontiguousarray(
        np.broadcast_to(np.asarray(inp["L1_ffn_in_c"], np.float32)[:, None], (D, ROWS)))
    common = dict(
        ln0g=_bcast(inp["L0_attn_gamma"]), ln0b=_bcast(inp["L0_attn_beta"]),
        win0=np.asarray(inp["L0_attn_W_in"], np.float32),
        bin0=_bcast(inp["L0_attn_b_in"]), asc=asc,
        wout0=np.asarray(inp["L0_attn_W_out"], np.float32),
        bout0=_bcast(inp["L0_attn_b_out"]),
        mur=_col(inp["L0_ffn_mu_r"]), muk=_col(inp["L0_ffn_mu_k"]),
        wrT=_t(np.asarray(inp["L0_ffn_W_r"], np.float32)),
        wkT=_t(np.asarray(inp["L0_ffn_W_k"], np.float32)),
        wvT=_t(np.asarray(inp["L0_ffn_W_v"], np.float32)),
        ln1g=_bcast(inp["L1_attn_norm_w"]), ln1b=_bcast(inp["L1_attn_norm_b"]),
        winT1=_t(np.asarray(inp["L1_attn_W_in"], np.float32)),
        bin1=_bcast(inp["L1_attn_b_in"]), wau=wau,
        cssm=np.asarray(inp["L1_attn_C_ssm"], np.float32),
        woT1=_t(np.asarray(inp["L1_attn_W_out"], np.float32)),
        bout1=_bcast(inp["L1_attn_b_out"]),
        wficT=wfic, wfoT=_t(np.asarray(inp["L1_ffn_wo"], np.float32)),
        u3=u3, b3=b3, c0=c0, ident=np.eye(128, dtype=np.float32),
        uoT=_t(np.asarray(inp["L1_ffn_uo"], np.float32)),
        bo=_bcast(inp["L1_ffn_bo"]),
    )
    maps = []
    for k in range(NCORES):
        xs = np.zeros((B, RB, D), np.float32)
        r0 = k * ROWS
        pre = min(PRE, r0)
        if pre > 0:
            xs[:, PRE - pre:PRE, :] = x[:, r0 - pre:r0, :]
        xs[:, PRE:, :] = x[:, r0:r0 + ROWS, :]
        oh = np.zeros(8, np.float32)
        if k >= 1:
            oh[k - 1] = 1.0
        m = dict(common)
        m["x"] = np.ascontiguousarray(xs.reshape(TOK, D))
        m["mask"] = np.full((128, 1), 0.0 if k == 0 else 1.0, np.float32)
        m["oh"] = _bcast(oh)
        maps.append(m)
    return maps


def kernel(**inp):
    nc = build_fused()
    maps = host_maps(inp)
    res = run_bass_kernel_spmd(nc, maps, core_ids=list(range(NCORES)))
    out = np.empty((B, L, D), np.float32)
    for k in range(NCORES):
        x2 = res.results[k]["x2"]
        out[:, k * ROWS:(k + 1) * ROWS, :] = x2.reshape(B, ROWS, D)
    return out


# version 08
# speedup vs baseline: 1.1071x; 1.0922x over previous
import numpy as np
import ml_dtypes
from contextlib import ExitStack
import concourse.bass as bass
import concourse.mybir as mybir
from concourse.bass_utils import run_bass_kernel_spmd

F32 = mybir.dt.float32
BF16 = mybir.dt.bfloat16
AF = mybir.ActivationFunctionType
ALU = mybir.AluOpType
AX = mybir.AxisListType
BF = ml_dtypes.bfloat16

NCORES = 8
B, L, D = 8, 8192, 1024
ROWS = 1024
PRE = 128
RB = ROWS + PRE
TOK = B * RB
TOWN = B * ROWS
LN_EPS = 1e-5

SC_W = 128
SC_LC = 128
SC_STEPS = SC_W + SC_LC
SC_F = 512
SC_BLK = 8
NBLK = SC_STEPS // SC_BLK
WBLK = SC_W // SC_BLK
BLKC = SC_BLK * SC_F
SC_H = SC_F // 2


class S:

    def __init__(self, h):
        self.h = h
        self.n = 0

    def inc(self, ins, k=1):
        self.n += k
        ins.then_inc(self.h, k)
        return ins


class Pool:
    def __init__(self, nc, es, n=48):
        self.sems = [S(es.enter_context(nc.semaphore(f"sp{i}"))) for i in range(n)]
        self.i = 0

    def take(self):
        s = self.sems[self.i % len(self.sems)]
        self.i += 1
        return s


class Ph:
    def __init__(self, nc, pool, deps):
        self.nc = nc
        self.pool = pool
        self.deps = deps
        self.done = []

    def sem(self):
        return self.pool.take()

    def entry(self, eng):
        for (s, c) in self.deps:
            eng.wait_ge(s.h, c)


_LN_N = [0]


def ph_ln(nc, es, pool, deps, Tn, x_d, gam_d, bet_d, xn_d, z_d, double):
    ph = Ph(nc, pool, deps)
    NTIL = Tn // 128
    _LN_N[0] += 1
    pfx = f"lnp{_LN_N[0]}"
    with ExitStack() as ees:
        E = ees.enter_context
        x_sb = E(nc.sbuf_tensor(pfx + "x", [128, 2 * D], F32))
        xn_sb = E(nc.sbuf_tensor(pfx + "xn", [128, 2 * D], F32))
        z_sb = E(nc.sbuf_tensor(pfx + "z", [128, 2 * D], BF16))
        g_sb = E(nc.sbuf_tensor(pfx + "g", [128, D], F32))
        be_sb = E(nc.sbuf_tensor(pfx + "be", [128, D], F32))
        sq_sb = E(nc.sbuf_tensor(pfx + "sq", [128, D], F32))
        st_sb = E(nc.sbuf_tensor(pfx + "st", [128, 16], F32))
        eps_sb = E(nc.sbuf_tensor(pfx + "eps", [128, 1], F32))
        cld = ph.sem()
        lde, ldo = ph.sem(), ph.sem()
        ld = [lde, ldo]
        sq1 = ph.sem()
        sq2 = ph.sem()
        rst = ph.sem()
        xnd = ph.sem()
        zd = ph.sem()
        sx = [ph.sem(), ph.sem()]
        sz = [ph.sem(), ph.sem()]
        vv = ph.sem()
        b = [s.n for s in (cld, lde, ldo, sq1, sq2, rst, xnd, zd, sx[0], sx[1],
                           sz[0], sz[1], vv)]
        (bcld, blde, bldo, bsq1, bsq2, brst, bxnd, bzd, bsx0, bsx1,
         bsz0, bsz1, bvv) = b
        bld = [blde, bldo]
        bsx_ = [bsx0, bsx1]
        bsz_ = [bsz0, bsz1]
        block = E(nc.Block())

        @block.sync
        def _(sy):
            ph.entry(sy)
            cld.inc(sy.dma_start(out=g_sb[:], in_=gam_d), 16)
            cld.inc(sy.dma_start(out=be_sb[:], in_=bet_d), 16)
            for i in range(NTIL):
                par = i % 2
                if i >= 2:
                    sy.wait_ge(xnd.h, bxnd + i - 1)
                ld[par].inc(sy.dma_start(out=x_sb[:, par * D:(par + 1) * D],
                                         in_=x_d[i * 128:(i + 1) * 128, :]), 16)
                sy.wait_ge(zd.h, bzd + i)
                if i >= 1:
                    sz[(i - 1) % 2].inc(
                        sy.dma_start(out=z_d[(i - 1) * 128:i * 128, :],
                                     in_=z_sb[:, ((i - 1) % 2) * D:((i - 1) % 2) * D + D]),
                        16)
            sy.wait_ge(zd.h, bzd + NTIL)
            sz[(NTIL - 1) % 2].inc(
                sy.dma_start(out=z_d[(NTIL - 1) * 128:NTIL * 128, :],
                             in_=z_sb[:, ((NTIL - 1) % 2) * D:((NTIL - 1) % 2) * D + D]),
                16)

        @block.gpsimd
        def _(g):
            ph.entry(g)
            for i in range(NTIL):
                par = i % 2
                g.wait_ge(xnd.h, bxnd + i + 1)
                sx[par].inc(g.dma_start(out=xn_d[i * 128:(i + 1) * 128, :],
                                        in_=xn_sb[:, par * D:(par + 1) * D]), 16)

        @block.scalar
        def _(s):
            ph.entry(s)
            for i in range(NTIL):
                par = i % 2
                p0 = 8 * par
                s.wait_ge(ld[par].h, bld[par] + 16 * (i // 2 + 1))
                sq1.inc(s.activation(sq_sb[:, 0:D], x_sb[:, par * D:(par + 1) * D],
                                     AF.Square, accum_out=st_sb[:, p0 + 0:p0 + 1]))
                s.wait_ge(rst.h, brst + (4 * i if double else 2 * i) + 1)
                sq1.inc(s.activation(st_sb[:, p0 + 3:p0 + 4], st_sb[:, p0 + 3:p0 + 4],
                                     AF.Sqrt, bias=eps_sb[:, 0:1]))
                if double:
                    s.wait_ge(xnd.h, bxnd + i + 1)
                    sq2.inc(s.activation(sq_sb[:, 0:D], xn_sb[:, par * D:(par + 1) * D],
                                         AF.Square, accum_out=st_sb[:, p0 + 4:p0 + 5]))
                    s.wait_ge(rst.h, brst + 4 * i + 3)
                    sq2.inc(s.activation(st_sb[:, p0 + 7:p0 + 8], st_sb[:, p0 + 7:p0 + 8],
                                         AF.Sqrt, bias=eps_sb[:, 0:1]))

        @block.vector
        def _(v):
            ph.entry(v)
            v.memset(eps_sb[:], LN_EPS)
            vvc = [0]

            def tick(ins):
                vvc[0] += 1
                vv.inc(ins)

            def syn():
                v.wait_ge(vv.h, bvv + vvc[0])

            for i in range(NTIL):
                par = i % 2
                p0 = 8 * par
                xap = x_sb[:, par * D:(par + 1) * D]
                xnap = xn_sb[:, par * D:(par + 1) * D]
                zap = z_sb[:, par * D:(par + 1) * D]
                v.wait_ge(cld.h, bcld + 32)
                v.wait_ge(ld[par].h, bld[par] + 16 * (i // 2 + 1))
                tick(v.reduce_sum(st_sb[:, p0 + 1:p0 + 2], xap, axis=AX.X))
                syn()
                tick(v.tensor_scalar_mul(st_sb[:, p0 + 1:p0 + 2],
                                         st_sb[:, p0 + 1:p0 + 2], 1.0 / D))
                v.wait_ge(sq1.h, bsq1 + 2 * i + 1)
                syn()
                tick(v.tensor_mul(st_sb[:, p0 + 2:p0 + 3], st_sb[:, p0 + 1:p0 + 2],
                                  st_sb[:, p0 + 1:p0 + 2]))
                syn()
                rst.inc(v.scalar_tensor_tensor(st_sb[:, p0 + 3:p0 + 4],
                                               st_sb[:, p0 + 0:p0 + 1], 1.0 / D,
                                               st_sb[:, p0 + 2:p0 + 3],
                                               ALU.mult, ALU.subtract))
                v.wait_ge(sq1.h, bsq1 + 2 * i + 2)
                tick(v.reciprocal(st_sb[:, p0 + 3:p0 + 4], st_sb[:, p0 + 3:p0 + 4]))
                rst.inc(v.engine_nop())
                if i >= 2:
                    v.wait_ge(sx[par].h, bsx_[par] + 16 * (i // 2))
                syn()
                xnd.inc(v.tensor_scalar(xnap, xap, st_sb[:, p0 + 1:p0 + 2],
                                        st_sb[:, p0 + 3:p0 + 4], ALU.subtract, ALU.mult))
                if i >= 2:
                    v.wait_ge(sz[par].h, bsz_[par] + 16 * (i // 2))
                if double:
                    tick(v.reduce_sum(st_sb[:, p0 + 5:p0 + 6], xnap, axis=AX.X))
                    syn()
                    tick(v.tensor_scalar_mul(st_sb[:, p0 + 5:p0 + 6],
                                             st_sb[:, p0 + 5:p0 + 6], 1.0 / D))
                    v.wait_ge(sq2.h, bsq2 + 2 * i + 1)
                    syn()
                    tick(v.tensor_mul(st_sb[:, p0 + 6:p0 + 7], st_sb[:, p0 + 5:p0 + 6],
                                      st_sb[:, p0 + 5:p0 + 6]))
                    syn()
                    rst.inc(v.scalar_tensor_tensor(st_sb[:, p0 + 7:p0 + 8],
                                                   st_sb[:, p0 + 4:p0 + 5], 1.0 / D,
                                                   st_sb[:, p0 + 6:p0 + 7],
                                                   ALU.mult, ALU.subtract))
                    v.wait_ge(sq2.h, bsq2 + 2 * i + 2)
                    tick(v.reciprocal(st_sb[:, p0 + 7:p0 + 8], st_sb[:, p0 + 7:p0 + 8]))
                    rst.inc(v.engine_nop())
                    syn()
                    v.tensor_scalar(zap, xnap, st_sb[:, p0 + 5:p0 + 6],
                                    st_sb[:, p0 + 7:p0 + 8], ALU.subtract, ALU.mult)
                    v.tensor_mul(zap, zap, g_sb[:])
                else:
                    v.tensor_mul(zap, xnap, g_sb[:])
                zd.inc(v.tensor_add(zap, zap, be_sb[:]))

        ph.done = [(sx[0], sx[0].n), (sx[1], sx[1].n),
                   (sz[0], sz[0].n), (sz[1], sz[1].n)]
    return ph



def ph_mmb(nc, es, pool, deps, name, Ttok, K, Nout, lhs, w_d, out_d,
           out_dtype=BF16, bias_d=None, acts=None, relu_sq=False, extras=()):
    ph = Ph(nc, pool, deps)
    KC = K // 128
    TG = Ttok // 512
    NT2 = max(1, Nout // 512)
    NW = min(Nout, 512)
    MCH = Ttok // 128
    NE = len(extras)
    kind, src_d = lhs
    use_scalar = (acts is not None) or relu_sq
    assert not (use_scalar and NE)
    with ExitStack() as ees:
        E = ees.enter_context
        w_sb = E(nc.sbuf_tensor(name + "w", [128, KC * Nout], BF16))
        a_sb = E(nc.sbuf_tensor(name + "a", [128, 2 * KC * 512], BF16))
        bias_sb = E(nc.sbuf_tensor(name + "b", [128, Nout], F32)) if bias_d is not None else None
        e_sb = E(nc.sbuf_tensor(name + "e", [128, max(1, 2 * NE * Nout)], F32))
        t_sb = E(nc.sbuf_tensor(name + "t", [128, 2 * NW], F32))
        t2_sb = E(nc.sbuf_tensor(name + "t2", [128, 2 * NW], F32))
        o_sb = E(nc.sbuf_tensor(name + "o", [128, 2 * Nout], out_dtype))
        ps = E(nc.psum_tensor(name + "ps", [128, 8 * 512], F32))
        wld = ph.sem(); ald = [ph.sem(), ph.sem()]; eld = [ph.sem(), ph.sem()]
        ald2 = [ph.sem(), ph.sem()]
        mmd = ph.sem(); vps = ph.sem(); sev = ph.sem(); vev = ph.sem()
        stp = [ph.sem(), ph.sem()]
        bald2 = [x.n for x in ald2]
        bwld = wld.n; bald = [x.n for x in ald]; beld = [x.n for x in eld]
        bmmd = mmd.n; bvps = vps.n; bsev = sev.n; bvev = vev.n
        bstp = [x.n for x in stp]
        n_w_dma = KC + (1 if bias_d is not None else 0)
        fin, bfin, finpm = (sev, bsev, NT2) if use_scalar else (vev, bvev, 1)
        block = E(nc.Block())

        @block.gpsimd
        def _(g):
            ph.entry(g)
            for k in range(KC):
                wld.inc(g.dma_start(out=w_sb[:, k * Nout:(k + 1) * Nout],
                                    in_=w_d[k * 128:(k + 1) * 128, :]), 16)
            if bias_d is not None:
                wld.inc(g.dma_start(out=bias_sb[:], in_=bias_d), 16)
            for mt in range(MCH):
                par = mt % 2
                for ei, (rmap, op) in enumerate(extras):
                    eld[par].inc(g.dma_start(
                        out=e_sb[:, (par * NE + ei) * Nout:(par * NE + ei + 1) * Nout],
                        in_=rmap(mt)), 16)
                g.wait_ge(fin.h, bfin + finpm * (mt + 1))
                stp[par].inc(g.dma_start(out=out_d[mt * 128:(mt + 1) * 128, :],
                                         in_=o_sb[:, par * Nout:(par + 1) * Nout]), 16)

        @block.sync
        def _(sy):
            ph.entry(sy)
            for gi in range(TG):
                par = gi % 2
                if gi >= 2:
                    sy.wait_ge(mmd.h, bmmd + (gi - 1) * 4)
                if kind == "t2":
                    for k in range(KC):
                        ald[par].inc(sy.dma_start_transpose(
                            a_sb[:, par * KC * 512 + k * 512:par * KC * 512 + (k + 1) * 512],
                            src_d[gi * 512:(gi + 1) * 512, k * 128:(k + 1) * 128]), 16)
                else:
                    ald[par].inc(sy.dma_start(
                        out=a_sb[:, par * KC * 512:(par + 1) * KC * 512]
                            .rearrange("p (k n) -> p k n", k=KC),
                        in_=src_d.rearrange("(k p) t -> p k t", p=128)
                        [:, :, gi * 512:(gi + 1) * 512]), 16)


        TIF = max(1, 8 // NT2)

        @block.tensor
        def _(t):
            ph.entry(t)
            t.wait_ge(wld.h, bwld + 16 * n_w_dma)
            for gi in range(TG):
                par = gi % 2
                nld = KC if kind == "t2" else 1
                t.wait_ge(ald[par].h, bald[par] + 16 * nld * (gi // 2 + 1))
                for m in range(4):
                    idx = gi * 4 + m
                    if idx >= TIF:
                        t.wait_ge(vps.h, bvps + NT2 * (idx - TIF + 1))
                    for k in range(KC):
                        for nt in range(NT2):
                            bank = (idx % TIF) * NT2 + nt
                            ins = t.matmul(
                                ps[:, bank * 512:bank * 512 + NW],
                                a_sb[:, par * KC * 512 + k * 512 + m * 128:
                                        par * KC * 512 + k * 512 + (m + 1) * 128],
                                w_sb[:, k * Nout + nt * NW:k * Nout + nt * NW + NW],
                                start=(k == 0), stop=(k == KC - 1))
                    mmd.inc(ins)

        @block.vector
        def _(v):
            ph.entry(v)
            for mt in range(MCH):
                par = mt % 2
                if NE:
                    v.wait_ge(eld[par].h, beld[par] + 16 * NE * (mt // 2 + 1))
                if mt >= 2:
                    v.wait_ge(stp[par].h, bstp[par] + 16 * (mt // 2))
                for nt in range(NT2):
                    idx = mt * NT2 + nt
                    bank = (mt % TIF) * NT2 + nt
                    v.wait_ge(mmd.h, bmmd + mt + 1)
                    if use_scalar and idx >= 2:
                        v.wait_ge(sev.h, bsev + idx - 1)
                    tgt = (t_sb[:, (idx % 2) * NW:(idx % 2) * NW + NW]
                           if (use_scalar or NE)
                           else o_sb[:, par * Nout + nt * NW:par * Nout + nt * NW + NW])
                    if bias_d is not None:
                        ins = v.tensor_tensor(tgt, ps[:, bank * 512:bank * 512 + NW],
                                              bias_sb[:, nt * NW:nt * NW + NW], ALU.add)
                    else:
                        ins = v.tensor_copy(tgt, ps[:, bank * 512:bank * 512 + NW])
                    vps.inc(ins)
                    if NE:
                        src = t_sb[:, (idx % 2) * NW:(idx % 2) * NW + NW]
                        for ei, (rmap, op) in enumerate(extras):
                            eap = e_sb[:, (par * NE + ei) * Nout + nt * NW:
                                          (par * NE + ei) * Nout + nt * NW + NW]
                            last = (ei == NE - 1)
                            dst = (o_sb[:, par * Nout + nt * NW:par * Nout + nt * NW + NW]
                                   if last else src)
                            ins = v.tensor_tensor(dst, src, eap, op)
                    if not use_scalar and nt == NT2 - 1:
                        vev.inc(ins)


        @block.scalar
        def _(s):
            ph.entry(s)
            if not use_scalar:
                return
            for mt in range(MCH):
                par = mt % 2
                for nt in range(NT2):
                    idx = mt * NT2 + nt
                    s.wait_ge(vps.h, bvps + idx + 1)
                    if mt >= 2 and nt == 0:
                        s.wait_ge(stp[par].h, bstp[par] + 16 * (mt // 2))
                    ins = None
                    for (lo, hi, fn) in acts:
                        if hi <= nt * NW or lo >= (nt + 1) * NW:
                            continue
                        clo = max(lo - nt * NW, 0)
                        chi = min(hi - nt * NW, NW)
                        srcap = t_sb[:, (idx % 2) * NW + clo:(idx % 2) * NW + chi]
                        if relu_sq:
                            dst = t2_sb[:, (idx % 2) * NW + clo:(idx % 2) * NW + chi]
                        else:
                            dst = o_sb[:, par * Nout + nt * NW + clo:
                                          par * Nout + nt * NW + chi]
                        ins = s.activation(dst, srcap, fn)
                    if relu_sq:
                        ins = s.activation(o_sb[:, par * Nout + nt * NW:
                                                   par * Nout + nt * NW + NW],
                                           t2_sb[:, (idx % 2) * NW:(idx % 2) * NW + NW],
                                           AF.Square)
                    sev.inc(ins)

        ph.done = [(stp[0], stp[0].n), (stp[1], stp[1].n)]
    return ph


def ph_ug(nc, es, pool, deps, z_d, w_d, bias_d, mask_d, pu_d, pg_d):
    ph = Ph(nc, pool, deps)
    K, Nout = D, 2 * D
    KC, NT2, NW = 8, 4, 512
    TG = TOK // 512
    MCH = TOK // 128
    pu3 = pu_d.rearrange("(tl q) f -> tl q f", q=128)
    pg3 = pg_d.rearrange("(tl q) f -> tl q f", q=128)

    def targets(mt):
        b, cb = mt // 9, mt % 9
        tg = []
        if cb == 0:
            tg.append((0, 0))
        else:
            tg.append((cb - 1, SC_W))
            if cb <= 7:
                tg.append((cb, 0))
        return b, tg

    nst = [0, 0]
    st_after = []
    for mt in range(MCH):
        b, tg = targets(mt)
        nst[mt % 2] += 16 * 2 * len(tg)
        st_after.append((mt % 2, nst[0], nst[1]))

    with ExitStack() as ees:
        E = ees.enter_context
        w_sb = E(nc.sbuf_tensor("ugw", [128, KC * Nout], BF16))
        a_sb = E(nc.sbuf_tensor("uga", [128, 2 * KC * 512], BF16))
        bias_sb = E(nc.sbuf_tensor("ugb", [128, Nout], F32))
        t_sb = E(nc.sbuf_tensor("ugt", [128, 2 * NW], F32))
        o_sb = E(nc.sbuf_tensor("ugo", [128, 2 * Nout], BF16))
        m_sb = E(nc.sbuf_tensor("ugm", [128, 1], F32))
        ps = E(nc.psum_tensor("ugps", [128, 8 * 512], F32))
        wld = ph.sem(); ald = [ph.sem(), ph.sem()]; ald2 = [ph.sem(), ph.sem()]
        mmd = ph.sem(); vps = ph.sem(); sev = ph.sem(); vev = ph.sem()
        stp = [ph.sem(), ph.sem()]
        bwld = wld.n; bald = [x.n for x in ald]; bald2 = [x.n for x in ald2]
        bmmd = mmd.n; bvps = vps.n
        bsev = sev.n; bvev = vev.n; bstp = [x.n for x in stp]
        block = E(nc.Block())

        @block.gpsimd
        def _(g):
            ph.entry(g)
            for k in range(KC):
                wld.inc(g.dma_start(out=w_sb[:, k * Nout:(k + 1) * Nout],
                                    in_=w_d[k * 128:(k + 1) * 128, :]), 16)
            wld.inc(g.dma_start(out=bias_sb[:], in_=bias_d), 16)
            wld.inc(g.dma_start(out=m_sb[:], in_=mask_d), 16)
            for mt in range(MCH):
                par = mt % 2
                b, tg = targets(mt)
                ispre = (mt % 9 == 0)
                if ispre:
                    g.wait_ge(vev.h, bvev + len([m for m in range(mt + 1) if m % 9 == 0]))
                else:
                    g.wait_ge(sev.h, bsev + NT2 * (mt + 1))
                for (k, tlb) in tg:
                    q0 = k * 16 + b * 2
                    srcu = o_sb[:, par * Nout:par * Nout + 1024] \
                        .rearrange("p (gb f) -> p gb f", gb=2)
                    srcg = o_sb[:, par * Nout + 1024:par * Nout + 2048] \
                        .rearrange("p (gb f) -> p gb f", gb=2)
                    stp[par].inc(g.dma_start(out=pu3[tlb:tlb + 128, q0:q0 + 2, :],
                                             in_=srcu), 16)
                    stp[par].inc(g.dma_start(out=pg3[tlb:tlb + 128, q0:q0 + 2, :],
                                             in_=srcg), 16)

        def emit_t2(eng, gi):
            par = gi % 2
            if gi >= 2:
                eng.wait_ge(mmd.h, bmmd + (gi - 1) * 4)
            for k in range(KC):
                ald[par].inc(eng.dma_start_transpose(
                    a_sb[:, par * KC * 512 + k * 512:par * KC * 512 + (k + 1) * 512],
                    z_d[gi * 512:(gi + 1) * 512, k * 128:(k + 1) * 128]), 16)

        @block.sync
        def _(sy):
            ph.entry(sy)
            for gi in range(TG):
                emit_t2(sy, gi)

        TIF = 2

        @block.tensor
        def _(t):
            ph.entry(t)
            t.wait_ge(wld.h, bwld + 16 * (KC + 2))
            for gi in range(TG):
                par = gi % 2
                t.wait_ge(ald[par].h, bald[par] + 16 * KC * (gi // 2 + 1))
                for m in range(4):
                    idx = gi * 4 + m
                    if idx >= TIF:
                        t.wait_ge(vps.h, bvps + NT2 * (idx - TIF + 1))
                    for k in range(KC):
                        for nt in range(NT2):
                            bank = (idx % TIF) * NT2 + nt
                            ins = t.matmul(
                                ps[:, bank * 512:bank * 512 + NW],
                                a_sb[:, par * KC * 512 + k * 512 + m * 128:
                                        par * KC * 512 + k * 512 + (m + 1) * 128],
                                w_sb[:, k * Nout + nt * NW:k * Nout + nt * NW + NW],
                                start=(k == 0), stop=(k == KC - 1))
                    mmd.inc(ins)

        @block.vector
        def _(v):
            ph.entry(v)
            for mt in range(MCH):
                par = mt % 2
                ispre = (mt % 9 == 0)
                if mt >= 2:
                    pp, c0_, c1_ = st_after[mt - 2]
                    v.wait_ge(stp[par].h, bstp[par] + (c0_ if par == 0 else c1_))
                for nt in range(NT2):
                    idx = mt * NT2 + nt
                    bank = (mt % TIF) * NT2 + nt
                    v.wait_ge(mmd.h, bmmd + mt + 1)
                    if idx >= 2:
                        v.wait_ge(sev.h, bsev + idx - 1)
                    vps.inc(v.tensor_tensor(t_sb[:, (idx % 2) * NW:(idx % 2) * NW + NW],
                                            ps[:, bank * 512:bank * 512 + NW],
                                            bias_sb[:, nt * NW:nt * NW + NW], ALU.add))
                if ispre:
                    v.wait_ge(sev.h, bsev + NT2 * (mt + 1))
                    vev.inc(v.tensor_scalar_mul(o_sb[:, par * Nout:(par + 1) * Nout],
                                                o_sb[:, par * Nout:(par + 1) * Nout],
                                                m_sb[:, 0:1]))

        @block.scalar
        def _(s):
            ph.entry(s)
            for mt in range(MCH):
                par = mt % 2
                for nt in range(NT2):
                    idx = mt * NT2 + nt
                    s.wait_ge(vps.h, bvps + idx + 1)
                    if mt >= 2 and nt == 0:
                        pp, c0_, c1_ = st_after[mt - 2]
                        s.wait_ge(stp[par].h, bstp[par] + (c0_ if par == 0 else c1_))
                    fn = AF.Copy if nt < 2 else AF.Sigmoid
                    sev.inc(s.activation(o_sb[:, par * Nout + nt * NW:
                                                 par * Nout + nt * NW + NW],
                                         t_sb[:, (idx % 2) * NW:(idx % 2) * NW + NW], fn))

        ph.done = [(stp[0], stp[0].n), (stp[1], stp[1].n)]
    return ph



def ph_lnug(nc, es, pool, deps, x_d, gam_d, bet_d, xn_d, z_d,
            w_d, bias_d, mask_d, pu_d, pg_d):
    ph = Ph(nc, pool, deps)
    LAG = 5
    NTIL = TOK // 128
    K, Nout = D, 2 * D
    KC, NT2, NW = 8, 4, 512
    TG = TOK // 512
    MCH = TOK // 128
    pu3 = pu_d.rearrange("(tl q) f -> tl q f", q=128)
    pg3 = pg_d.rearrange("(tl q) f -> tl q f", q=128)

    def targets(mt):
        b, cb = mt // 9, mt % 9
        tg = []
        if cb == 0:
            tg.append((0, 0))
        else:
            tg.append((cb - 1, SC_W))
            if cb <= 7:
                tg.append((cb, 0))
        return b, tg

    nst = [0, 0]
    st_after = []
    for mt in range(MCH):
        b, tg = targets(mt)
        nst[mt % 2] += 16 * 2 * len(tg)
        st_after.append((mt % 2, nst[0], nst[1]))

    with ExitStack() as ees:
        E = ees.enter_context
        x_sb = E(nc.sbuf_tensor("lux", [128, 2 * D], F32))
        xn_sb = E(nc.sbuf_tensor("luxn", [128, 2 * D], F32))
        z_sb = E(nc.sbuf_tensor("luz", [128, 2 * D], BF16))
        g_sb = E(nc.sbuf_tensor("lug", [128, D], F32))
        be_sb = E(nc.sbuf_tensor("lube", [128, D], F32))
        sq_sb = E(nc.sbuf_tensor("lusq", [128, D], F32))
        st_sb = E(nc.sbuf_tensor("lust", [128, 16], F32))
        eps_sb = E(nc.sbuf_tensor("lueps", [128, 1], F32))
        w_sb = E(nc.sbuf_tensor("luw", [128, KC * Nout], BF16))
        a_sb = E(nc.sbuf_tensor("lua", [128, 2 * KC * 512], BF16))
        bias_sb = E(nc.sbuf_tensor("lub", [128, Nout], F32))
        t_sb = E(nc.sbuf_tensor("lut", [128, 2 * NW], F32))
        o_sb = E(nc.sbuf_tensor("luo", [128, 2 * Nout], BF16))
        m_sb = E(nc.sbuf_tensor("lum", [128, 1], F32))
        ps = E(nc.psum_tensor("lups", [128, 8 * 512], F32))
        cld = ph.sem()
        lde, ldo = ph.sem(), ph.sem()
        ld = [lde, ldo]
        sq1 = ph.sem(); sq2 = ph.sem(); rst = ph.sem()
        xnd = ph.sem(); zd = ph.sem()
        sx = [ph.sem(), ph.sem()]
        sz = [ph.sem(), ph.sem()]
        vv = ph.sem()
        bcld = cld.n; bld = [lde.n, ldo.n]
        bsq1 = sq1.n; bsq2 = sq2.n; brst = rst.n
        bxnd = xnd.n; bzd = zd.n
        bsx_ = [s.n for s in sx]; bsz_ = [s.n for s in sz]
        bvv = vv.n
        wld = ph.sem(); ald = [ph.sem(), ph.sem()]
        mmd = ph.sem(); vps = ph.sem(); sev = ph.sem(); vev = ph.sem()
        stp = [ph.sem(), ph.sem()]
        bwld = wld.n; bald = [s.n for s in ald]
        bmmd = mmd.n; bvps = vps.n
        bsev = sev.n; bvev = vev.n; bstp = [s.n for s in stp]
        block = E(nc.Block())

        @block.sync
        def _(sy):
            ph.entry(sy)
            cld.inc(sy.dma_start(out=g_sb[:], in_=gam_d), 16)
            cld.inc(sy.dma_start(out=be_sb[:], in_=bet_d), 16)

            def t2(gi):
                par = gi % 2
                if gi >= 2:
                    sy.wait_ge(mmd.h, bmmd + (gi - 1) * 4)
                sy.wait_ge(sz[0].h, bsz_[0] + 16 * (2 * gi + 2))
                sy.wait_ge(sz[1].h, bsz_[1] + 16 * (2 * gi + 2))
                for k in range(KC):
                    ald[par].inc(sy.dma_start_transpose(
                        a_sb[:, par * KC * 512 + k * 512:par * KC * 512 + (k + 1) * 512],
                        z_d[gi * 512:(gi + 1) * 512, k * 128:(k + 1) * 128]), 16)

            for i in range(NTIL):
                par = i % 2
                if i >= 2:
                    sy.wait_ge(xnd.h, bxnd + i - 1)
                ld[par].inc(sy.dma_start(out=x_sb[:, par * D:(par + 1) * D],
                                         in_=x_d[i * 128:(i + 1) * 128, :]), 16)
                sy.wait_ge(zd.h, bzd + i)
                if i >= 1:
                    c = i - 1
                    sz[c % 2].inc(
                        sy.dma_start(out=z_d[c * 128:(c + 1) * 128, :],
                                     in_=z_sb[:, (c % 2) * D:(c % 2) * D + D]), 16)
                    if c % 4 == 3:
                        t2(c // 4)
            sy.wait_ge(zd.h, bzd + NTIL)
            c = NTIL - 1
            sz[c % 2].inc(
                sy.dma_start(out=z_d[c * 128:(c + 1) * 128, :],
                             in_=z_sb[:, (c % 2) * D:(c % 2) * D + D]), 16)
            t2(c // 4)

        TIF = 2

        @block.tensor
        def _(t):
            ph.entry(t)
            t.wait_ge(wld.h, bwld + 16 * (KC + 2))
            for gi in range(TG):
                par = gi % 2
                t.wait_ge(ald[par].h, bald[par] + 16 * KC * (gi // 2 + 1))
                for m in range(4):
                    idx = gi * 4 + m
                    if idx >= TIF:
                        t.wait_ge(vps.h, bvps + NT2 * (idx - TIF + 1))
                    for k in range(KC):
                        for nt in range(NT2):
                            bank = (idx % TIF) * NT2 + nt
                            ins = t.matmul(
                                ps[:, bank * 512:bank * 512 + NW],
                                a_sb[:, par * KC * 512 + k * 512 + m * 128:
                                        par * KC * 512 + k * 512 + (m + 1) * 128],
                                w_sb[:, k * Nout + nt * NW:k * Nout + nt * NW + NW],
                                start=(k == 0), stop=(k == KC - 1))
                    mmd.inc(ins)

        @block.gpsimd
        def _(g):
            ph.entry(g)
            for k in range(KC):
                wld.inc(g.dma_start(out=w_sb[:, k * Nout:(k + 1) * Nout],
                                    in_=w_d[k * 128:(k + 1) * 128, :]), 16)
            wld.inc(g.dma_start(out=bias_sb[:], in_=bias_d), 16)
            wld.inc(g.dma_start(out=m_sb[:], in_=mask_d), 16)
            for i in range(NTIL + LAG):
                if i < NTIL:
                    par = i % 2
                    g.wait_ge(xnd.h, bxnd + i + 1)
                    sx[par].inc(g.dma_start(out=xn_d[i * 128:(i + 1) * 128, :],
                                            in_=xn_sb[:, par * D:(par + 1) * D]), 16)
                if i >= LAG:
                    mt = i - LAG
                    par = mt % 2
                    b, tg = targets(mt)
                    ispre = (mt % 9 == 0)
                    if ispre:
                        g.wait_ge(vev.h, bvev + len(
                            [m for m in range(mt + 1) if m % 9 == 0]))
                    else:
                        g.wait_ge(sev.h, bsev + NT2 * (mt + 1))
                    for (k, tlb) in tg:
                        q0 = k * 16 + b * 2
                        srcu = o_sb[:, par * Nout:par * Nout + 1024] \
                            .rearrange("p (gb f) -> p gb f", gb=2)
                        srcg = o_sb[:, par * Nout + 1024:par * Nout + 2048] \
                            .rearrange("p (gb f) -> p gb f", gb=2)
                        stp[par].inc(g.dma_start(out=pu3[tlb:tlb + 128, q0:q0 + 2, :],
                                                 in_=srcu), 16)
                        stp[par].inc(g.dma_start(out=pg3[tlb:tlb + 128, q0:q0 + 2, :],
                                                 in_=srcg), 16)

        @block.scalar
        def _(s):
            ph.entry(s)
            for i in range(NTIL + LAG):
                if i < NTIL:
                    par = i % 2
                    p0 = 8 * par
                    s.wait_ge(ld[par].h, bld[par] + 16 * (i // 2 + 1))
                    sq1.inc(s.activation(sq_sb[:, 0:D], x_sb[:, par * D:(par + 1) * D],
                                         AF.Square, accum_out=st_sb[:, p0 + 0:p0 + 1]))
                    s.wait_ge(rst.h, brst + 4 * i + 1)
                    sq1.inc(s.activation(st_sb[:, p0 + 3:p0 + 4],
                                         st_sb[:, p0 + 3:p0 + 4],
                                         AF.Sqrt, bias=eps_sb[:, 0:1]))
                    s.wait_ge(xnd.h, bxnd + i + 1)
                    sq2.inc(s.activation(sq_sb[:, 0:D], xn_sb[:, par * D:(par + 1) * D],
                                         AF.Square, accum_out=st_sb[:, p0 + 4:p0 + 5]))
                    s.wait_ge(rst.h, brst + 4 * i + 3)
                    sq2.inc(s.activation(st_sb[:, p0 + 7:p0 + 8],
                                         st_sb[:, p0 + 7:p0 + 8],
                                         AF.Sqrt, bias=eps_sb[:, 0:1]))
                if i >= LAG:
                    mt = i - LAG
                    par = mt % 2
                    for nt in range(NT2):
                        idx = mt * NT2 + nt
                        s.wait_ge(vps.h, bvps + idx + 1)
                        if mt >= 2 and nt == 0:
                            pp, c0_, c1_ = st_after[mt - 2]
                            s.wait_ge(stp[par].h,
                                      bstp[par] + (c0_ if par == 0 else c1_))
                        fn = AF.Copy if nt < 2 else AF.Sigmoid
                        sev.inc(s.activation(o_sb[:, par * Nout + nt * NW:
                                                     par * Nout + nt * NW + NW],
                                             t_sb[:, (idx % 2) * NW:(idx % 2) * NW + NW],
                                             fn))

        @block.vector
        def _(v):
            ph.entry(v)
            v.memset(eps_sb[:], LN_EPS)
            vvc = [0]

            def tick(ins):
                vvc[0] += 1
                vv.inc(ins)

            def syn():
                v.wait_ge(vv.h, bvv + vvc[0])

            for i in range(NTIL + LAG):
                if i < NTIL:
                    par = i % 2
                    p0 = 8 * par
                    xap = x_sb[:, par * D:(par + 1) * D]
                    xnap = xn_sb[:, par * D:(par + 1) * D]
                    zap = z_sb[:, par * D:(par + 1) * D]
                    v.wait_ge(cld.h, bcld + 32)
                    v.wait_ge(ld[par].h, bld[par] + 16 * (i // 2 + 1))
                    tick(v.reduce_sum(st_sb[:, p0 + 1:p0 + 2], xap, axis=AX.X))
                    syn()
                    tick(v.tensor_scalar_mul(st_sb[:, p0 + 1:p0 + 2],
                                             st_sb[:, p0 + 1:p0 + 2], 1.0 / D))
                    v.wait_ge(sq1.h, bsq1 + 2 * i + 1)
                    syn()
                    tick(v.tensor_mul(st_sb[:, p0 + 2:p0 + 3], st_sb[:, p0 + 1:p0 + 2],
                                      st_sb[:, p0 + 1:p0 + 2]))
                    syn()
                    rst.inc(v.scalar_tensor_tensor(st_sb[:, p0 + 3:p0 + 4],
                                                   st_sb[:, p0 + 0:p0 + 1], 1.0 / D,
                                                   st_sb[:, p0 + 2:p0 + 3],
                                                   ALU.mult, ALU.subtract))
                    v.wait_ge(sq1.h, bsq1 + 2 * i + 2)
                    tick(v.reciprocal(st_sb[:, p0 + 3:p0 + 4], st_sb[:, p0 + 3:p0 + 4]))
                    rst.inc(v.engine_nop())
                    if i >= 2:
                        v.wait_ge(sx[par].h, bsx_[par] + 16 * (i // 2))
                    syn()
                    xnd.inc(v.tensor_scalar(xnap, xap, st_sb[:, p0 + 1:p0 + 2],
                                            st_sb[:, p0 + 3:p0 + 4],
                                            ALU.subtract, ALU.mult))
                    if i >= 2:
                        v.wait_ge(sz[par].h, bsz_[par] + 16 * (i // 2))
                    tick(v.reduce_sum(st_sb[:, p0 + 5:p0 + 6], xnap, axis=AX.X))
                    syn()
                    tick(v.tensor_scalar_mul(st_sb[:, p0 + 5:p0 + 6],
                                             st_sb[:, p0 + 5:p0 + 6], 1.0 / D))
                    v.wait_ge(sq2.h, bsq2 + 2 * i + 1)
                    syn()
                    tick(v.tensor_mul(st_sb[:, p0 + 6:p0 + 7], st_sb[:, p0 + 5:p0 + 6],
                                      st_sb[:, p0 + 5:p0 + 6]))
                    syn()
                    rst.inc(v.scalar_tensor_tensor(st_sb[:, p0 + 7:p0 + 8],
                                                   st_sb[:, p0 + 4:p0 + 5], 1.0 / D,
                                                   st_sb[:, p0 + 6:p0 + 7],
                                                   ALU.mult, ALU.subtract))
                    v.wait_ge(sq2.h, bsq2 + 2 * i + 2)
                    tick(v.reciprocal(st_sb[:, p0 + 7:p0 + 8], st_sb[:, p0 + 7:p0 + 8]))
                    rst.inc(v.engine_nop())
                    syn()
                    v.tensor_scalar(zap, xnap, st_sb[:, p0 + 5:p0 + 6],
                                    st_sb[:, p0 + 7:p0 + 8], ALU.subtract, ALU.mult)
                    v.tensor_mul(zap, zap, g_sb[:])
                    zd.inc(v.tensor_add(zap, zap, be_sb[:]))
                if i >= LAG:
                    mt = i - LAG
                    par = mt % 2
                    ispre = (mt % 9 == 0)
                    if mt >= 2:
                        pp, c0_, c1_ = st_after[mt - 2]
                        v.wait_ge(stp[par].h, bstp[par] + (c0_ if par == 0 else c1_))
                    for nt in range(NT2):
                        idx = mt * NT2 + nt
                        bank = (mt % TIF) * NT2 + nt
                        v.wait_ge(mmd.h, bmmd + mt + 1)
                        if idx >= 2:
                            v.wait_ge(sev.h, bsev + idx - 1)
                        vps.inc(v.tensor_tensor(
                            t_sb[:, (idx % 2) * NW:(idx % 2) * NW + NW],
                            ps[:, bank * 512:bank * 512 + NW],
                            bias_sb[:, nt * NW:nt * NW + NW], ALU.add))
                    if ispre:
                        v.wait_ge(sev.h, bsev + NT2 * (mt + 1))
                        vev.inc(v.tensor_scalar_mul(o_sb[:, par * Nout:(par + 1) * Nout],
                                                    o_sb[:, par * Nout:(par + 1) * Nout],
                                                    m_sb[:, 0:1]))

        ph.done = [(stp[0], stp[0].n), (stp[1], stp[1].n),
                   (sx[0], sx[0].n), (sx[1], sx[1].n),
                   (sz[0], sz[0].n), (sz[1], sz[1].n)]
    return ph


def ph_scan0(nc, es, pool, deps, pu_d, pg_d, a_d, hs_d):
    ph = Ph(nc, pool, deps)
    H = SC_H
    HW = 32
    NW_ = SC_STEPS // HW
    WOWN = SC_W // HW
    HWC = HW * SC_F
    puq = pu_d.rearrange("(tl q) f -> q tl f", q=128)
    pgq = pg_d.rearrange("(tl q) f -> q tl f", q=128)
    hs5 = hs_d.rearrange("(b k t) (gb f) -> b k gb t f", b=8, k=8, gb=2)
    with ExitStack() as ees:
        E = ees.enter_context
        u_sb = E(nc.sbuf_tensor("scu", [128, 2 * BLKC], BF16))
        g_sb = E(nc.sbuf_tensor("scg", [128, 2 * BLKC], BF16))
        h_sb = E(nc.sbuf_tensor("sch", [128, 2 * HWC], BF16))
        a_sb = E(nc.sbuf_tensor("sca", [128, SC_F], BF16))
        z_sb = E(nc.sbuf_tensor("scz", [128, SC_F], BF16))
        v_sb = E(nc.sbuf_tensor("scv", [128, SC_F], BF16))
        c_sb = E(nc.sbuf_tensor("scc", [128, SC_F], BF16))
        d_sb = E(nc.sbuf_tensor("scd", [128, SC_F], BF16))
        cld = ph.sem()
        ld = [ph.sem(), ph.sem()]
        vs = ph.sem(); cs = ph.sem()
        st3 = [ph.sem(), ph.sem(), ph.sem()]
        bcld = cld.n; bld = [x.n for x in ld]; bvs = vs.n; bcs = cs.n
        bst3 = [x.n for x in st3]
        shares = [list(range(0, 22)), list(range(22, 43)), list(range(43, 64))]
        block = E(nc.Block())

        def emit_spills(eng, qi, w):
            t0 = w * HW - SC_W
            par = w % 2
            eng.wait_ge(vs.h, bvs + 4 * HW * (w + 1))
            for pair in shares[qi]:
                k, b = pair // 8, pair % 8
                q0 = k * 16 + b * 2
                st3[qi].inc(eng.dma_start(
                    out=hs5[b, k, :, t0:t0 + HW, :],
                    in_=h_sb[par * HWC:par * HWC, 0:0] if False else
                        h_sb[q0:q0 + 2, par * HWC:(par + 1) * HWC]
                        .rearrange("p (t f) -> p t f", t=HW)), 16)

        @block.sync
        def _(sy):
            ph.entry(sy)
            for w in range(WOWN, NW_):
                emit_spills(sy, 1, w)

        @block.scalar
        def _(s_):
            ph.entry(s_)
            spw = iter(range(WOWN, NW_))
            nxt = next(spw, None)
            for s in range(SC_STEPS):
                for hf in range(2):
                    c0, c1 = hf * H, (hf + 1) * H
                    s_.wait_ge(vs.h, bvs + 4 * s + 1 + hf)
                    cs.inc(s_.activation(c_sb[:, c0:c1], v_sb[:, c0:c1], AF.Tanh))
                if nxt is not None and s == (nxt + 1) * HW - 1:
                    emit_spills(s_, 2, nxt)
                    nxt = next(spw, None)

        @block.gpsimd
        def _(g):
            ph.entry(g)
            cld.inc(g.dma_start(out=a_sb[:], in_=a_d), 16)
            for blk in range(NBLK):
                par = blk % 2
                if blk >= 2:
                    g.wait_ge(vs.h, bvs + 4 * SC_BLK * (blk - 1))
                ld[par].inc(g.dma_start(
                    out=u_sb[:, par * BLKC:(par + 1) * BLKC]
                        .rearrange("p (t f) -> p t f", t=SC_BLK),
                    in_=puq[:, blk * SC_BLK:(blk + 1) * SC_BLK, :]), 16)
                ld[par].inc(g.dma_start(
                    out=g_sb[:, par * BLKC:(par + 1) * BLKC]
                        .rearrange("p (t f) -> p t f", t=SC_BLK),
                    in_=pgq[:, blk * SC_BLK:(blk + 1) * SC_BLK, :]), 16)
                if blk % 4 == 0 and blk // 4 - 1 >= WOWN:
                    emit_spills(g, 0, blk // 4 - 1)
            for w in range(NBLK // 4 - 1, NW_):
                if w >= WOWN:
                    emit_spills(g, 0, w)

        @block.vector
        def _(v):
            ph.entry(v)
            v.memset(z_sb[:], 0.0)
            for s in range(SC_STEPS):
                blk = s // SC_BLK
                sl = s % SC_BLK
                par = blk % 2
                w = s // HW
                wl = s % HW
                wpar = w % 2
                if sl == 0:
                    if blk == 0:
                        v.wait_ge(cld.h, bcld + 16)
                    v.wait_ge(ld[par].h, bld[par] + 32 * (blk // 2 + 1))
                if wl == 0 and w >= 2 and w - 2 >= WOWN:
                    for qi in range(3):
                        nsp = len(shares[qi]) * 16 * (w - 2 - WOWN + 1)
                        v.wait_ge(st3[qi].h, bst3[qi] + nsp)
                if s == 0:
                    hprev = z_sb[:]
                elif wl == 0:
                    hprev = h_sb[:, (1 - wpar) * HWC + (HW - 1) * SC_F:
                                    (1 - wpar) * HWC + HW * SC_F]
                else:
                    hprev = h_sb[:, wpar * HWC + (wl - 1) * SC_F:wpar * HWC + wl * SC_F]
                uap = u_sb[:, par * BLKC + sl * SC_F:par * BLKC + (sl + 1) * SC_F]
                gap = g_sb[:, par * BLKC + sl * SC_F:par * BLKC + (sl + 1) * SC_F]
                hap = h_sb[:, wpar * HWC + wl * SC_F:wpar * HWC + (wl + 1) * SC_F]
                for hf in range(2):
                    c0, c1 = hf * H, (hf + 1) * H
                    if s >= 1:
                        v.wait_ge(cs.h, bcs + 2 * (s - 1) + hf + 1)
                    v.tensor_mul(v_sb[:, c0:c1], a_sb[:, c0:c1], hprev[:, c0:c1])
                    vs.inc(v.tensor_add(v_sb[:, c0:c1], v_sb[:, c0:c1], uap[:, c0:c1]))
                for hf in range(2):
                    c0, c1 = hf * H, (hf + 1) * H
                    v.wait_ge(cs.h, bcs + 2 * s + hf + 1)
                    v.tensor_sub(d_sb[:, c0:c1], hprev[:, c0:c1], c_sb[:, c0:c1])
                    v.tensor_mul(d_sb[:, c0:c1], gap[:, c0:c1], d_sb[:, c0:c1])
                    vs.inc(v.tensor_add(hap[:, c0:c1], d_sb[:, c0:c1], c_sb[:, c0:c1]))

        ph.done = [(st3[i], st3[i].n) for i in range(3)]
    return ph


def ph_shift(nc, es, pool, deps, xn_d, mur_d, muk_d, xr_d, xk_d):
    ph = Ph(nc, pool, deps)
    with ExitStack() as ees:
        E = ees.enter_context
        xc_sb = E(nc.sbuf_tensor("shx", [128, 2 * RB], BF16))
        d_sb = E(nc.sbuf_tensor("shd", [128, ROWS], F32))
        o_sb = E(nc.sbuf_tensor("sho", [128, 2 * 2 * ROWS], BF16))
        mu_sb = E(nc.sbuf_tensor("shmu", [128, 16], F32))
        cld = ph.sem(); tld = [ph.sem(), ph.sem()]
        dn = ph.sem(); stp = [ph.sem(), ph.sem()]
        bcld = cld.n; btld = [x.n for x in tld]; bdn = dn.n; bstp = [x.n for x in stp]
        block = E(nc.Block())

        @block.sync
        def _(sy):
            ph.entry(sy)
            cld.inc(sy.dma_start(out=mu_sb[:, 0:8], in_=mur_d), 16)
            cld.inc(sy.dma_start(out=mu_sb[:, 8:16], in_=muk_d), 16)
            for i in range(64):
                c, b = i // 8, i % 8
                par = i % 2
                if i >= 2:
                    sy.wait_ge(dn.h, bdn + i - 1)
                tld[par].inc(sy.dma_start_transpose(
                    xc_sb[:, par * RB:(par + 1) * RB],
                    xn_d[b * RB:(b + 1) * RB, c * 128:(c + 1) * 128]), 16)

        @block.vector
        def _(v):
            ph.entry(v)
            v.wait_ge(cld.h, bcld + 32)
            for i in range(64):
                c, b = i // 8, i % 8
                par = i % 2
                v.wait_ge(tld[par].h, btld[par] + 16 * (i // 2 + 1))
                if i >= 2:
                    v.wait_ge(stp[par].h, bstp[par] + 32 * (i // 2))
                xo = xc_sb[:, par * RB + PRE:par * RB + RB]
                xp = xc_sb[:, par * RB + PRE - 1:par * RB + RB - 1]
                v.tensor_sub(d_sb[:], xo, xp)
                v.scalar_tensor_tensor(o_sb[:, par * 2 * ROWS:par * 2 * ROWS + ROWS],
                                       d_sb[:], mu_sb[:, c:c + 1], xp,
                                       ALU.mult, ALU.add)
                dn.inc(v.scalar_tensor_tensor(
                    o_sb[:, par * 2 * ROWS + ROWS:par * 2 * ROWS + 2 * ROWS],
                    d_sb[:], mu_sb[:, 8 + c:8 + c + 1], xp, ALU.mult, ALU.add))

        @block.scalar
        def _(s):
            ph.entry(s)
            for i in range(64):
                c, b = i // 8, i % 8
                par = i % 2
                s.wait_ge(dn.h, bdn + i + 1)
                stp[par].inc(s.dma_start(out=xr_d[c * 128:(c + 1) * 128,
                                                  b * ROWS:(b + 1) * ROWS],
                                         in_=o_sb[:, par * 2 * ROWS:par * 2 * ROWS + ROWS]),
                             16)
                stp[par].inc(s.dma_start(out=xk_d[c * 128:(c + 1) * 128,
                                                  b * ROWS:(b + 1) * ROWS],
                                         in_=o_sb[:, par * 2 * ROWS + ROWS:
                                                     par * 2 * ROWS + 2 * ROWS]), 16)

        ph.done = [(stp[0], stp[0].n), (stp[1], stp[1].n)]
    return ph

def ph_au_states(nc, es, pool, deps, xln_d, win1n_d, binc_d, wau_d, oh_d,
                 agin_d, agout_d, states_d):
    ph = Ph(nc, pool, deps)
    KC = 8
    TG = TOWN // 512
    with ExitStack() as ees:
        E = ees.enter_context
        wn_sb = E(nc.sbuf_tensor("auwn", [128, 16 * 1024], BF16))
        wau_sb = E(nc.sbuf_tensor("auwu", [128, 16 * 64], BF16))
        bin_sb = E(nc.sbuf_tensor("aubi", [128, 16], BF16))
        wab_sb = E(nc.sbuf_tensor("auwb", [128, 8 * 64], BF16))
        ba_sb = E(nc.sbuf_tensor("auba", [64, 1], F32))
        a_sb = E(nc.sbuf_tensor("aua", [128, 2 * KC * 512], BF16))
        au_sb = E(nc.sbuf_tensor("auau", [64, TOWN], F32))
        al_sb = E(nc.sbuf_tensor("aual", [128, ROWS], F32))
        ul_sb = E(nc.sbuf_tensor("auul", [128, ROWS], F32))
        ones_sb = E(nc.sbuf_tensor("auon", [128, ROWS], F32))
        sl_sb = E(nc.sbuf_tensor("ausl", [128, ROWS], F32))
        cl_sb = E(nc.sbuf_tensor("aucl", [128, ROWS], F32))
        agi_sb = E(nc.sbuf_tensor("auag", [128, 2], F32))
        aa_sb = E(nc.sbuf_tensor("auaa", [128, 8], F32))
        uu_sb = E(nc.sbuf_tensor("auuu", [128, 8], F32))
        ss_sb = E(nc.sbuf_tensor("auss", [128, 8], F32))
        oh_sb = E(nc.sbuf_tensor("auoh", [128, 8], F32))
        si_sb = E(nc.sbuf_tensor("ausi", [128, 1], F32))
        st_out = E(nc.sbuf_tensor("aust", [128, ROWS], F32))
        psw = E(nc.psum_tensor("aupw", [128, 2 * 512], F32))
        psb = E(nc.psum_tensor("aupb", [64, 512], F32))
        ps = E(nc.psum_tensor("aups", [64, 2 * 512], F32))
        wld = ph.sem(); ald = [ph.sem(), ph.sem()]
        pmm = ph.sem(); pev = ph.sem()
        mmd = ph.sem(); sev = ph.sem(); locd = ph.sem(); vv = ph.sem()
        cc = ph.sem(); agl = ph.sem(); std = ph.sem()
        bwld = wld.n; bald = [x.n for x in ald]; bpmm = pmm.n; bpev = pev.n
        bmmd = mmd.n; bsev = sev.n
        blocd = locd.n; bvv = vv.n; bcc = cc.n; bagl = agl.n; bstd = std.n
        block = E(nc.Block())

        @block.sync
        def _(sy):
            ph.entry(sy)
            for gi in range(TG):
                par = gi % 2
                if gi >= 2:
                    sy.wait_ge(mmd.h, bmmd + gi - 1)
                for k in range(KC):
                    ald[par].inc(sy.dma_start_transpose(
                        a_sb[:, par * KC * 512 + k * 512:par * KC * 512 + (k + 1) * 512],
                        xln_d[gi * 512:(gi + 1) * 512, k * 128:(k + 1) * 128]), 16)

        @block.tensor
        def _(t):
            ph.entry(t)
            t.wait_ge(wld.h, bwld + 16 * 4)
            for m in range(8):
                bank = m % 2
                if m >= 2:
                    t.wait_ge(pev.h, bpev + m - 1)
                for k in range(16):
                    ins = t.matmul(psw[:, bank * 512:bank * 512 + 64],
                                   wn_sb[:, k * 1024 + m * 128:k * 1024 + (m + 1) * 128],
                                   wau_sb[:, k * 64:(k + 1) * 64],
                                   start=(k == 0), stop=(k == 15))
                pmm.inc(ins)
            for k in range(16):
                ins = t.matmul(psb[:, 0:1], wau_sb[:, k * 64:(k + 1) * 64],
                               bin_sb[:, k:k + 1], start=(k == 0), stop=(k == 15))
            pmm.inc(ins)
            t.wait_ge(pev.h, bpev + 9)
            for gi in range(TG):
                par = gi % 2
                t.wait_ge(ald[par].h, bald[par] + 16 * KC * (gi // 2 + 1))
                bank = gi % 2
                if gi >= 2:
                    t.wait_ge(sev.h, bsev + gi - 1)
                for k in range(KC):
                    ins = t.matmul(ps[:, bank * 512:(bank + 1) * 512],
                                   wab_sb[:, k * 64:(k + 1) * 64],
                                   a_sb[:, par * KC * 512 + k * 512:
                                           par * KC * 512 + (k + 1) * 512],
                                   start=(k == 0), stop=(k == KC - 1))
                mmd.inc(ins)

        @block.scalar
        def _(s):
            ph.entry(s)
            for gi in range(TG):
                bank = gi % 2
                s.wait_ge(mmd.h, bmmd + gi + 1)
                s.activation(au_sb[0:16, gi * 512:(gi + 1) * 512],
                             ps[0:16, bank * 512:(bank + 1) * 512], AF.Sigmoid,
                             bias=ba_sb[0:16, 0:1])
                sev.inc(s.activation(au_sb[32:48, gi * 512:(gi + 1) * 512],
                                     ps[32:48, bank * 512:(bank + 1) * 512], AF.Identity,
                                     bias=ba_sb[32:48, 0:1]))

        @block.gpsimd
        def _(g):
            ph.entry(g)
            wld.inc(g.dma_start(out=wn_sb[:].rearrange("p (k d) -> p k d", k=16),
                                in_=win1n_d.rearrange("(k p) d -> p k d", p=128)), 16)
            wld.inc(g.dma_start(out=wau_sb[:].rearrange("p (k n) -> p k n", k=16),
                                in_=wau_d.rearrange("(k p) n -> p k n", p=128)), 16)
            wld.inc(g.dma_start(out=bin_sb[:], in_=binc_d), 16)
            wld.inc(g.dma_start(out=oh_sb[:], in_=oh_d), 16)
            g.wait_ge(sev.h, bsev + TG)
            for b in range(8):
                locd.inc(g.dma_start(out=al_sb[b * 16:(b + 1) * 16, :],
                                     in_=au_sb[0:16, b * ROWS:(b + 1) * ROWS]), 16)
                locd.inc(g.dma_start(out=ul_sb[b * 16:(b + 1) * 16, :],
                                     in_=au_sb[32:48, b * ROWS:(b + 1) * ROWS]), 16)
            g.wait_ge(vv.h, bvv + 2)
            agl.inc(g.dma_start(out=agin_d, in_=agi_sb[:]), 16)
            g.wait_ge(agl.h, bagl + 16)
            cc.inc(g.collective_compute(
                "AllGather", ALU.bypass, replica_groups=[list(range(NCORES))],
                ins=[agin_d.opt()], outs=[agout_d.opt()]))
            g.wait_ge(cc.h, bcc + 1)
            for j in range(8):
                agl.inc(g.dma_start(out=aa_sb[:, j:j + 1],
                                    in_=agout_d[j * 128:(j + 1) * 128, 0:1],
                                    allow_slow_non_contiguous=True), 16)
                agl.inc(g.dma_start(out=uu_sb[:, j:j + 1],
                                    in_=agout_d[j * 128:(j + 1) * 128, 1:2],
                                    allow_slow_non_contiguous=True), 16)
            g.wait_ge(vv.h, bvv + 6)
            std.inc(g.dma_start(out=states_d, in_=st_out[:]), 16)
            g.wait_ge(std.h, bstd + 16)

        @block.vector
        def _(v):
            ph.entry(v)
            v.memset(ones_sb[:], 1.0)
            for m in range(8):
                bank = m % 2
                v.wait_ge(pmm.h, bpmm + m + 1)
                pev.inc(v.tensor_copy(wab_sb[:, m * 64:(m + 1) * 64],
                                      psw[:, bank * 512:bank * 512 + 64]))
            v.wait_ge(pmm.h, bpmm + 9)
            pev.inc(v.tensor_copy(ba_sb[:], psb[:, 0:1]))
            v.wait_ge(locd.h, blocd + 16 * 16)
            v.tensor_tensor_scan(sl_sb[:], al_sb[:], ul_sb[:], 0.0, ALU.mult, ALU.add)
            vv.inc(v.tensor_tensor_scan(cl_sb[:], al_sb[:], ones_sb[:], 1.0,
                                        ALU.mult, ALU.mult))
            v.wait_ge(vv.h, bvv + 1)
            v.tensor_copy(agi_sb[:, 0:1], cl_sb[:, ROWS - 1:ROWS])
            vv.inc(v.tensor_copy(agi_sb[:, 1:2], sl_sb[:, ROWS - 1:ROWS]))
            v.wait_ge(agl.h, bagl + 16 * 17)
            vv.inc(v.tensor_tensor_scan(ss_sb[:], aa_sb[:], uu_sb[:], 0.0,
                                        ALU.mult, ALU.add))
            v.wait_ge(vv.h, bvv + 3)
            vv.inc(v.tensor_mul(ss_sb[:], ss_sb[:], oh_sb[:]))
            v.wait_ge(vv.h, bvv + 4)
            vv.inc(v.reduce_sum(si_sb[:], ss_sb[:], axis=AX.X))
            v.wait_ge(vv.h, bvv + 5)
            vv.inc(v.scalar_tensor_tensor(st_out[:], cl_sb[:], si_sb[:, 0:1],
                                          sl_sb[:], ALU.mult, ALU.add))

        ph.done = [(std, std.n)]
    return ph
def ph_s1p(nc, es, pool, deps, states_d, cssm_d, woT_d, bout_d, xn1_d, s1p_d):
    ph = Ph(nc, pool, deps)
    with ExitStack() as ees:
        E = ees.enter_context
        c_sb = E(nc.sbuf_tensor("s1c", [128, 16 * 16], BF16))
        wo_sb = E(nc.sbuf_tensor("s1wo", [128, 16 * D], BF16))
        gf_sb = E(nc.sbuf_tensor("s1gf", [16, D], F32))
        gb_sb = E(nc.sbuf_tensor("s1gb", [16, D], BF16))
        sts_sb = E(nc.sbuf_tensor("s1st", [16, TOWN], BF16))
        bo_sb = E(nc.sbuf_tensor("s1bo", [128, D], F32))
        e_sb = E(nc.sbuf_tensor("s1e", [128, 2 * D], BF16))
        t_sb = E(nc.sbuf_tensor("s1t", [128, 2 * 512], F32))
        o_sb = E(nc.sbuf_tensor("s1o", [128, 2 * D], BF16))
        psg = E(nc.psum_tensor("s1pg", [16, 1024], F32))
        ps = E(nc.psum_tensor("s1ps", [128, 4 * 512], F32))
        wld = ph.sem(); eld = [ph.sem(), ph.sem()]; gdn = ph.sem()
        mmd = ph.sem(); vev = ph.sem(); stp = [ph.sem(), ph.sem()]
        bwld = wld.n; beld = [x.n for x in eld]; bgdn = gdn.n
        bmmd = mmd.n; bvev = vev.n; bstp = [x.n for x in stp]
        block = E(nc.Block())

        @block.gpsimd
        def _(g):
            ph.entry(g)
            for k in range(16):
                wld.inc(g.dma_start(out=c_sb[:, k * 16:(k + 1) * 16],
                                    in_=cssm_d[k * 128:(k + 1) * 128, :]), 16)
            for k in range(16):
                wld.inc(g.dma_start(out=wo_sb[:, k * D:(k + 1) * D],
                                    in_=woT_d[k * 128:(k + 1) * 128, :]), 16)
            wld.inc(g.dma_start(out=bo_sb[:], in_=bout_d), 16)
            wld.inc(g.dma_start(out=sts_sb[:].rearrange("p (b r) -> p b r", b=8),
                                in_=states_d.rearrange("(b n) r -> n b r", n=16)), 16)
            for mt in range(TOWN // 128):
                par = mt % 2
                g.wait_ge(vev.h, bvev + mt + 1)
                stp[par].inc(g.dma_start(out=s1p_d[mt * 128:(mt + 1) * 128, :],
                                         in_=o_sb[:, par * D:(par + 1) * D]), 16)

        @block.sync
        def _(sy):
            ph.entry(sy)
            for mt in range(TOWN // 128):
                par = mt % 2
                if mt >= 2:
                    sy.wait_ge(vev.h, bvev + mt - 1)
                eld[par].inc(sy.dma_start(out=e_sb[:, par * D:(par + 1) * D],
                                          in_=xn1_d[mt * 128:(mt + 1) * 128, :]), 16)

        @block.tensor
        def _(t):
            ph.entry(t)
            t.wait_ge(wld.h, bwld + 16 * 34)
            for k in range(16):
                ins = t.matmul(psg[:, 0:512], c_sb[:, k * 16:(k + 1) * 16],
                               wo_sb[:, k * D:k * D + 512], start=(k == 0), stop=(k == 15))
            for k in range(16):
                ins = t.matmul(psg[:, 512:1024], c_sb[:, k * 16:(k + 1) * 16],
                               wo_sb[:, k * D + 512:(k + 1) * D],
                               start=(k == 0), stop=(k == 15))
            mmd.inc(ins)
            t.wait_ge(gdn.h, bgdn + 1)
            for mt in range(TOWN // 128):
                for nt in range(2):
                    idx = mt * 2 + nt
                    bank = idx % 4
                    if idx >= 4:
                        t.wait_ge(vev.h, bvev + (idx - 3 + 1) // 2)
                    mmd.inc(t.matmul(ps[:, bank * 512:bank * 512 + 512],
                                     sts_sb[:, mt * 128:(mt + 1) * 128],
                                     gb_sb[:, nt * 512:(nt + 1) * 512],
                                     start=True, stop=True))

        @block.vector
        def _(v):
            ph.entry(v)
            v.wait_ge(mmd.h, bmmd + 1)
            v.tensor_copy(gf_sb[:], psg[:, 0:1024])
            gdn.inc(v.tensor_copy(gb_sb[:], gf_sb[:]))
            for mt in range(TOWN // 128):
                par = mt % 2
                v.wait_ge(eld[par].h, beld[par] + 16 * (mt // 2 + 1))
                if mt >= 2:
                    v.wait_ge(stp[par].h, bstp[par] + 16 * (mt // 2))
                for nt in range(2):
                    idx = mt * 2 + nt
                    bank = idx % 4
                    v.wait_ge(mmd.h, bmmd + idx + 2)
                    v.tensor_tensor(t_sb[:, nt * 512:(nt + 1) * 512],
                                    ps[:, bank * 512:bank * 512 + 512],
                                    bo_sb[:, nt * 512:(nt + 1) * 512], ALU.add)
                ins = v.tensor_tensor(o_sb[:, par * D:(par + 1) * D], t_sb[:, 0:1024],
                                      e_sb[:, par * D:(par + 1) * D], ALU.add)
                vev.inc(ins)

        ph.done = [(stp[0], stp[0].n), (stp[1], stp[1].n)]
    return ph


def ph_mma(nc, es, pool, deps, name, K, Mout, src_d, w_d, out_d):
    ph = Ph(nc, pool, deps)
    KC = K // 128
    MC = Mout // 128
    TG = TOWN // 512
    with ExitStack() as ees:
        E = ees.enter_context
        w_sb = E(nc.sbuf_tensor(name + "w", [128, KC * Mout], BF16))
        a_sb = E(nc.sbuf_tensor(name + "a", [128, 2 * KC * 512], BF16))
        o_sb = E(nc.sbuf_tensor(name + "o", [128, 2 * 512], BF16))
        ps = E(nc.psum_tensor(name + "ps", [128, 4 * 512], F32))
        wld = ph.sem(); ald = [ph.sem(), ph.sem()]; ald2 = [ph.sem(), ph.sem()]
        mmd = ph.sem(); vev = ph.sem(); stp = [ph.sem(), ph.sem()]
        bwld = wld.n; bald = [x.n for x in ald]; bald2 = [x.n for x in ald2]
        bmmd = mmd.n; bvev = vev.n
        bstp = [x.n for x in stp]
        block = E(nc.Block())

        @block.gpsimd
        def _(g):
            ph.entry(g)
            for k in range(KC):
                wld.inc(g.dma_start(out=w_sb[:, k * Mout:(k + 1) * Mout],
                                    in_=w_d[k * 128:(k + 1) * 128, :]), 16)
            for gi in range(TG):
                for m in range(MC):
                    idx = gi * MC + m
                    par = idx % 2
                    g.wait_ge(vev.h, bvev + idx + 1)
                    stp[par].inc(g.dma_start(
                        out=out_d[m * 128:(m + 1) * 128, gi * 512:(gi + 1) * 512],
                        in_=o_sb[:, par * 512:(par + 1) * 512]), 16)

        def emit_t2(eng, gi):
            par = gi % 2
            if gi >= 2:
                eng.wait_ge(mmd.h, bmmd + (gi - 1) * MC)
            for k in range(KC):
                ald[par].inc(eng.dma_start_transpose(
                    a_sb[:, par * KC * 512 + k * 512:par * KC * 512 + (k + 1) * 512],
                    src_d[gi * 512:(gi + 1) * 512, k * 128:(k + 1) * 128]), 16)

        @block.sync
        def _(sy):
            ph.entry(sy)
            for gi in range(TG):
                emit_t2(sy, gi)

        @block.tensor
        def _(t):
            ph.entry(t)
            t.wait_ge(wld.h, bwld + 16 * KC)
            for gi in range(TG):
                par = gi % 2
                t.wait_ge(ald[par].h, bald[par] + 16 * KC * (gi // 2 + 1))
                for m in range(MC):
                    idx = gi * MC + m
                    bank = idx % 4
                    if idx >= 4:
                        t.wait_ge(vev.h, bvev + idx - 3)
                    for k in range(KC):
                        ins = t.matmul(ps[:, bank * 512:(bank + 1) * 512],
                                       w_sb[:, k * Mout + m * 128:k * Mout + (m + 1) * 128],
                                       a_sb[:, par * KC * 512 + k * 512:
                                               par * KC * 512 + (k + 1) * 512],
                                       start=(k == 0), stop=(k == KC - 1))
                    mmd.inc(ins)

        @block.vector
        def _(v):
            ph.entry(v)
            for gi in range(TG):
                for m in range(MC):
                    idx = gi * MC + m
                    par = idx % 2
                    bank = idx % 4
                    v.wait_ge(mmd.h, bmmd + idx + 1)
                    if idx >= 2:
                        v.wait_ge(stp[par].h, bstp[par] + 16 * (idx // 2))
                    vev.inc(v.tensor_copy(o_sb[:, par * 512:(par + 1) * 512],
                                          ps[:, bank * 512:(bank + 1) * 512]))

        ph.done = [(stp[0], stp[0].n), (stp[1], stp[1].n)]
    return ph

def ph_ffn1loop(nc, es, pool, deps, xw_d, u3_d, b3_d, c0_d, id_d, cf_d, RC=512):
    ph = Ph(nc, pool, deps)
    NPASS = 1024 // RC
    MC = D // 128
    NT = RC
    NB = B

    def mm_after(step, m, gi):
        return step * 3 * MC + m * 3 + gi + 1

    def upd_after(p, t, m):
        return p * (1 + NB * MC) + 1 + t * MC + m + 1

    def upd_init(p):
        return p * (1 + NB * MC) + 1

    with ExitStack() as ees:
        E = ees.enter_context
        u_sb = E(nc.sbuf_tensor("f1u", [128, MC * 3 * D], BF16))
        id_sb = E(nc.sbuf_tensor("f1id", [128, 128], BF16))
        b_sb = E(nc.sbuf_tensor("f1b", [128, 24], F32))
        c_sb = E(nc.sbuf_tensor("f1c", [128, MC * RC], F32))
        cb_sb = E(nc.sbuf_tensor("f1cb", [128, 2 * MC * RC], BF16))
        xw_sb = E(nc.sbuf_tensor("f1xw", [128, 2 * 3 * MC * RC], BF16))
        ga_sb = E(nc.sbuf_tensor("f1ga", [128, 2 * 3 * NT], F32))
        ps = E(nc.psum_tensor("f1ps", [128, 8 * 512], F32))
        wld = ph.sem(); xld = [ph.sem(), ph.sem()]
        mmd = ph.sem(); gev = ph.sem(); upd = ph.sem(); stq = ph.sem()
        bwld = wld.n; bxld = [x.n for x in xld]; bmmd = mmd.n
        bgev = gev.n; bupd = upd.n; bstq = stq.n
        block = E(nc.Block())

        @block.gpsimd
        def _(g):
            ph.entry(g)
            for k in range(MC):
                wld.inc(g.dma_start(out=u_sb[:, k * 3 * D:(k + 1) * 3 * D],
                                    in_=u3_d[k * 128:(k + 1) * 128, :]), 16)
            wld.inc(g.dma_start(out=id_sb[:], in_=id_d), 16)
            wld.inc(g.dma_start(out=b_sb[:], in_=b3_d), 16)
            for p in range(NPASS):
                if p >= 1:
                    g.wait_ge(stq.h, bstq + 16 * p)
                wld.inc(g.dma_start(out=c_sb[:].rearrange("p (m r) -> p m r", m=MC),
                                    in_=c0_d.rearrange("(m p) r -> p m r", p=128)
                                    [:, :, p * RC:(p + 1) * RC]), 16)
                for t in range(NB):
                    step = p * NB + t
                    if step >= 2:
                        g.wait_ge(gev.h, bgev + (step - 1) * 3 * MC)
                    for gi in range(3):
                        xld[t % 2].inc(g.dma_start(
                            out=xw_sb[:, ((t % 2) * 3 + gi) * MC * RC:
                                         ((t % 2) * 3 + gi + 1) * MC * RC]
                                .rearrange("p (m r) -> p m r", m=MC),
                            in_=xw_d.rearrange("(m p) t -> p m t", p=128)
                                [:, gi * MC:(gi + 1) * MC,
                                 t * 1024 + p * RC:t * 1024 + (p + 1) * RC]), 16)
                g.wait_ge(upd.h, bupd + upd_after(p, NB - 1, MC - 1))
                stq.inc(g.dma_start(out=cf_d.rearrange("(m p) r -> p m r", p=128)
                                    [:, :, p * RC:(p + 1) * RC],
                                    in_=c_sb[:].rearrange("p (m r) -> p m r", m=MC)), 16)
            g.wait_ge(stq.h, bstq + 16 * NPASS)

        @block.tensor
        def _(t_):
            ph.entry(t_)
            t_.wait_ge(wld.h, bwld + 16 * (MC + 2))
            for p in range(NPASS):
                t_.wait_ge(wld.h, bwld + 16 * (MC + 2 + p + 1))
                for t in range(NB):
                    step = p * NB + t
                    t_.wait_ge(xld[t % 2].h, bxld[t % 2] + 48 * (step // 2 + 1))
                    t_.wait_ge(upd.h, bupd + (upd_after(p, t - 1, MC - 1) if t > 0
                                              else upd_init(p)))
                    for m in range(MC):
                        for gi in range(3):
                            bidx = gi * 2 + (m % 2)
                            if m >= 2:
                                t_.wait_ge(gev.h, bgev + mm_after(step, m - 2, gi))
                            elif step >= 1:
                                t_.wait_ge(gev.h, bgev + mm_after(step - 1, MC - 2 + m, gi))
                            t_.matmul(ps[:, bidx * 512:bidx * 512 + NT], id_sb[:],
                                      xw_sb[:, ((t % 2) * 3 + gi) * MC * RC + m * RC:
                                               ((t % 2) * 3 + gi) * MC * RC + m * RC + NT],
                                      start=True, stop=False)
                            for k in range(MC):
                                ins = t_.matmul(
                                    ps[:, bidx * 512:bidx * 512 + NT],
                                    u_sb[:, k * 3 * D + gi * D + m * 128:
                                            k * 3 * D + gi * D + (m + 1) * 128],
                                    cb_sb[:, ((step - 1) % 2) * MC * RC + k * RC:
                                             ((step - 1) % 2) * MC * RC + k * RC + NT],
                                    start=False, stop=(k == MC - 1))
                            mmd.inc(ins)

        @block.scalar
        def _(s):
            ph.entry(s)
            AFS = [AF.Sigmoid, AF.Sigmoid, AF.Tanh]
            for p in range(NPASS):
                for t in range(NB):
                    step = p * NB + t
                    for m in range(MC):
                        for gi in range(3):
                            s.wait_ge(mmd.h, bmmd + mm_after(step, m, gi))
                            if m >= 1:
                                s.wait_ge(upd.h, bupd + upd_after(p, t, m - 1))
                            elif step >= 1:
                                pp, tt = (p, t - 1) if t > 0 else (p - 1, NB - 1)
                                s.wait_ge(upd.h, bupd + upd_after(pp, tt, MC - 1))
                            bidx = gi * 2 + (m % 2)
                            gev.inc(s.activation(
                                ga_sb[:, ((step % 2) * 3 + gi) * NT:
                                         ((step % 2) * 3 + gi) * NT + NT],
                                ps[:, bidx * 512:bidx * 512 + NT],
                                AFS[gi], bias=b_sb[:, gi * MC + m:gi * MC + m + 1]))

        @block.vector
        def _(v):
            ph.entry(v)
            for p in range(NPASS):
                v.wait_ge(wld.h, bwld + 16 * (MC + 2 + p + 1))
                for k in range(MC):
                    ins = v.tensor_copy(cb_sb[:, MC * RC + k * RC:MC * RC + (k + 1) * RC],
                                        c_sb[:, k * RC:(k + 1) * RC])
                upd.inc(ins)
                for t in range(NB):
                    step = p * NB + t
                    for m in range(MC):
                        v.wait_ge(gev.h, bgev + step * 3 * MC + m * 3 + 3)
                        sp = (step % 2) * 3
                        f_ap = ga_sb[:, (sp + 0) * NT:(sp + 0) * NT + NT]
                        i_ap = ga_sb[:, (sp + 1) * NT:(sp + 1) * NT + NT]
                        g_ap = ga_sb[:, (sp + 2) * NT:(sp + 2) * NT + NT]
                        c_ap = c_sb[:, m * RC:m * RC + NT]
                        cb_ap = cb_sb[:, (step % 2) * MC * RC + m * RC:
                                         (step % 2) * MC * RC + m * RC + NT]
                        v.tensor_mul(c_ap, f_ap, c_ap)
                        v.tensor_mul(i_ap, i_ap, g_ap)
                        v.tensor_add(c_ap, c_ap, i_ap)
                        upd.inc(v.tensor_copy(cb_ap, c_ap))

        ph.done = [(stq, stq.n)]
    return ph


def ph_final(nc, es, pool, deps, cf_d, uoT_d, bo_d, xwo_d, s1p_d, x1_d, x2_d):
    ph = Ph(nc, pool, deps)
    MC = 8
    with ExitStack() as ees:
        E = ees.enter_context
        uo_sb = E(nc.sbuf_tensor("fno", [128, MC * D], BF16))
        cb_sb = E(nc.sbuf_tensor("fncb", [128, MC * ROWS], BF16))
        bo_sb = E(nc.sbuf_tensor("fnbo", [128, D], F32))
        cuo_sb = E(nc.sbuf_tensor("fncuo", [128, 8 * D], F32))
        xw_sb = E(nc.sbuf_tensor("fnxw", [128, 2 * D], BF16))
        s1_sb = E(nc.sbuf_tensor("fns1", [128, 2 * D], BF16))
        x1_sb = E(nc.sbuf_tensor("fnx1", [128, 2 * D], F32))
        t_sb = E(nc.sbuf_tensor("fnt", [128, 2 * D], F32))
        t2_sb = E(nc.sbuf_tensor("fnt2", [128, 2 * D], F32))
        o_sb = E(nc.sbuf_tensor("fnoo", [128, 2 * D], F32))
        ps = E(nc.psum_tensor("fnps", [128, 4 * 512], F32))
        wld = ph.sem(); eld = [ph.sem(), ph.sem()]
        mmd = ph.sem(); cud = ph.sem(); vad = ph.sem(); tad = ph.sem()
        oad = ph.sem(); stp = [ph.sem(), ph.sem()]
        bwld = wld.n; beld = [x.n for x in eld]; bmmd = mmd.n; bcud = cud.n
        bvad = vad.n; btad = tad.n; boad = oad.n; bstp = [x.n for x in stp]
        block = E(nc.Block())

        @block.gpsimd
        def _(g):
            ph.entry(g)
            for k in range(MC):
                wld.inc(g.dma_start(out=uo_sb[:, k * D:(k + 1) * D],
                                    in_=uoT_d[k * 128:(k + 1) * 128, :]), 16)
            wld.inc(g.dma_start(out=cb_sb[:].rearrange("p (m r) -> p m r", m=MC),
                                in_=cf_d.rearrange("(m p) r -> p m r", p=128)), 16)
            wld.inc(g.dma_start(out=bo_sb[:], in_=bo_d), 16)
            for mt in range(TOWN // 128):
                par = mt % 2
                g.wait_ge(oad.h, boad + mt + 1)
                stp[par].inc(g.dma_start(out=x2_d[mt * 128:(mt + 1) * 128, :],
                                         in_=o_sb[:, par * D:(par + 1) * D]), 16)
            g.wait_ge(stp[0].h, bstp[0] + 16 * 32)
            g.wait_ge(stp[1].h, bstp[1] + 16 * 32)

        @block.sync
        def _(sy):
            ph.entry(sy)
            for mt in range(TOWN // 128):
                par = mt % 2
                if mt >= 2:
                    sy.wait_ge(oad.h, boad + mt - 1)
                eld[par].inc(sy.dma_start(out=xw_sb[:, par * D:(par + 1) * D],
                                          in_=xwo_d[mt * 128:(mt + 1) * 128, :]), 16)
                eld[par].inc(sy.dma_start(out=s1_sb[:, par * D:(par + 1) * D],
                                          in_=s1p_d[mt * 128:(mt + 1) * 128, :]), 16)
                eld[par].inc(sy.dma_start(out=x1_sb[:, par * D:(par + 1) * D],
                                          in_=x1_d[mt * 128:(mt + 1) * 128, :]), 16)

        @block.tensor
        def _(t):
            ph.entry(t)
            t.wait_ge(wld.h, bwld + 16 * (MC + 2))
            for rc in range(8):
                for nt in range(2):
                    idx = rc * 2 + nt
                    bank = idx % 4
                    if idx >= 4:
                        t.wait_ge(cud.h, bcud + idx - 3)
                    for k in range(MC):
                        ins = t.matmul(ps[:, bank * 512:(bank + 1) * 512],
                                       cb_sb[:, k * ROWS + rc * 128:k * ROWS + rc * 128 + 128],
                                       uo_sb[:, k * D + nt * 512:k * D + (nt + 1) * 512],
                                       start=(k == 0), stop=(k == MC - 1))
                    mmd.inc(ins)

        @block.vector
        def _(v):
            ph.entry(v)
            for rc in range(8):
                for nt in range(2):
                    idx = rc * 2 + nt
                    bank = idx % 4
                    v.wait_ge(mmd.h, bmmd + idx + 1)
                    cud.inc(v.tensor_copy(cuo_sb[:, rc * D + nt * 512:rc * D + (nt + 1) * 512],
                                          ps[:, bank * 512:(bank + 1) * 512]))
            for mt in range(TOWN // 128):
                par = mt % 2
                rc = mt % 8
                v.wait_ge(eld[par].h, beld[par] + 48 * (mt // 2 + 1))
                if mt >= 2:
                    v.wait_ge(stp[par].h, bstp[par] + 16 * (mt // 2))
                if mt >= 2:
                    v.wait_ge(tad.h, btad + mt - 1)
                v.tensor_tensor(t_sb[:, par * D:(par + 1) * D],
                                xw_sb[:, par * D:(par + 1) * D],
                                cuo_sb[:, rc * D:(rc + 1) * D], ALU.add)
                vad.inc(v.tensor_add(t_sb[:, par * D:(par + 1) * D],
                                     t_sb[:, par * D:(par + 1) * D], bo_sb[:]))
                v.wait_ge(tad.h, btad + mt + 1)
                v.tensor_add(o_sb[:, par * D:(par + 1) * D],
                             t2_sb[:, par * D:(par + 1) * D],
                             x1_sb[:, par * D:(par + 1) * D])
                oad.inc(v.tensor_add(o_sb[:, par * D:(par + 1) * D],
                                     o_sb[:, par * D:(par + 1) * D],
                                     s1_sb[:, par * D:(par + 1) * D]))

        @block.scalar
        def _(s):
            ph.entry(s)
            for mt in range(TOWN // 128):
                par = mt % 2
                s.wait_ge(vad.h, bvad + mt + 1)
                if mt >= 2:
                    s.wait_ge(oad.h, boad + mt - 1)
                tad.inc(s.activation(t2_sb[:, par * D:(par + 1) * D],
                                     t_sb[:, par * D:(par + 1) * D], AF.Tanh))

        ph.done = [(stp[0], stp[0].n), (stp[1], stp[1].n)]
    return ph

def build_fused(dbg=()):
    nc = bass.Bass("TRN2", target_bir_lowering=False, debug=False)

    def dram(nm, shape, dt):
        kind = "ExternalOutput" if nm in dbg else "Internal"
        return nc.dram_tensor(nm, shape, dt, kind=kind).ap()

    def inp(nm, shape, dt=F32):
        return nc.dram_tensor(nm, shape, dt, kind="ExternalInput").ap()

    x_d = inp("x", [TOK, D])
    ln0g_d = inp("ln0g", [128, D]); ln0b_d = inp("ln0b", [128, D])
    win0_d = inp("win0", [D, 2 * D]); bin0_d = inp("bin0", [128, 2 * D])
    mask_d = inp("mask", [128, 1]); asc_d = inp("asc", [128, SC_F])
    wout0_d = inp("wout0", [D, D]); bout0_d = inp("bout0", [128, D])
    mur_d = inp("mur", [128, 8]); muk_d = inp("muk", [128, 8])
    wrT_d = inp("wrT", [D, D]); wkT_d = inp("wkT", [D, D]); wvT_d = inp("wvT", [D, D])
    ln1g_d = inp("ln1g", [128, D]); ln1b_d = inp("ln1b", [128, D])
    win1n_d = inp("win1n", [2 * D, D]); binc_d = inp("binc", [128, 16])
    wau_d = inp("wau", [2 * D, 64]); oh_d = inp("oh", [128, 8])
    cssm_d = inp("cssm", [2 * D, 16]); woT1_d = inp("woT1", [2 * D, D])
    bout1_d = inp("bout1", [128, D])
    wficT_d = inp("wficT", [D, 3 * D]); wfoT_d = inp("wfoT", [D, D])
    u3_d = inp("u3", [D, 3 * D]); b3_d = inp("b3", [128, 24])
    c0_d = inp("c0", [D, ROWS]); id_d = inp("ident", [128, 128])
    uoT_d = inp("uoT", [D, D]); bo_d = inp("bo", [128, D])

    xn0_d = dram("xn0", [TOK, D], BF16)
    z_d = dram("z", [TOK, D], BF16)
    pu_d = dram("pu", [SC_STEPS * 128, SC_F], BF16)
    pg_d = dram("pg", [SC_STEPS * 128, SC_F], BF16)
    hs_d = dram("hs", [TOWN, D], BF16)
    attn0_d = dram("attn0", [TOWN, D], BF16)
    xr_d = dram("xr", [D, TOWN], BF16)
    xk_d = dram("xk", [D, TOWN], BF16)
    r_d = dram("r", [TOWN, D], BF16)
    kact_d = dram("kact", [TOWN, D], BF16)
    x1_d = dram("x1", [TOWN, D], F32)
    xn1_d = dram("xn1", [TOWN, D], BF16)
    xln_d = dram("xln", [TOWN, D], BF16)
    agin_d = dram("agin", [128, 2], F32)
    agout_d = dram("agout", [NCORES * 128, 2], F32)
    states_d = dram("states", [128, ROWS], F32)
    s1p_d = dram("s1p", [TOWN, D], BF16)
    xw_d = dram("xw", [3 * D, TOWN], BF16)
    xwo_d = dram("xwo", [TOWN, D], BF16)
    cf_d = dram("cf", [D, ROWS], F32)
    x2_d = nc.dram_tensor("x2", [TOWN, D], F32, kind="ExternalOutput").ap()

    es = ExitStack()
    with es:
        pool = Pool(nc, es, n=48)
        p2 = ph_lnug(nc, es, pool, [], x_d, ln0g_d, ln0b_d, xn0_d, z_d,
                     win0_d, bin0_d, mask_d, pu_d, pg_d)
        p3 = ph_scan0(nc, es, pool, p2.done, pu_d, pg_d, asc_d, hs_d)
        p4 = ph_mmb(nc, es, pool, p3.done, "at0", TOWN, D, D, ("t2", hs_d),
                    wout0_d, attn0_d, bias_d=bout0_d, acts=[(0, D, AF.Copy)])
        p5 = ph_shift(nc, es, pool, p4.done, xn0_d, mur_d, muk_d, xr_d, xk_d)
        p6 = ph_mmb(nc, es, pool, p5.done, "rr", TOWN, D, D, ("ch", xr_d),
                    wrT_d, r_d, acts=[(0, D, AF.Sigmoid)])
        p7 = ph_mmb(nc, es, pool, p6.done, "kk", TOWN, D, D, ("ch", xk_d),
                    wkT_d, kact_d, acts=[(0, D, AF.Relu)], relu_sq=True)

        xn0o = xn0_d.rearrange("(b rr) d -> b rr d", b=B)
        x_o = x_d.rearrange("(b rr) d -> b rr d", b=B)

        def rm_r(mt):
            return r_d[mt * 128:(mt + 1) * 128, :]

        def rm_at0(mt):
            return attn0_d[mt * 128:(mt + 1) * 128, :]

        def rm_xn0(mt):
            b, rc = mt // 8, mt % 8
            return xn0o[b, PRE + rc * 128:PRE + (rc + 1) * 128, :]

        def rm_x(mt):
            b, rc = mt // 8, mt % 8
            return x_o[b, PRE + rc * 128:PRE + (rc + 1) * 128, :]

        p8 = ph_mmb(nc, es, pool, p7.done, "x1", TOWN, D, D, ("t2", kact_d),
                    wvT_d, x1_d, out_dtype=F32,
                    extras=[(rm_r, ALU.mult), (rm_at0, ALU.add),
                            (rm_xn0, ALU.add), (rm_x, ALU.add)])
        p9 = ph_ln(nc, es, pool, p8.done, TOWN, x1_d, ln1g_d, ln1b_d, xn1_d, xln_d, False)
        p11 = ph_au_states(nc, es, pool, p9.done, xln_d, win1n_d, binc_d, wau_d,
                           oh_d, agin_d, agout_d, states_d)
        p13 = ph_s1p(nc, es, pool, p11.done, states_d, cssm_d, woT1_d, bout1_d,
                     xn1_d, s1p_d)
        p14 = ph_mma(nc, es, pool, p13.done, "xwf", D, 3 * D, xn1_d, wficT_d, xw_d)
        p15 = ph_mmb(nc, es, pool, p14.done, "xwo", TOWN, D, D, ("t2", xn1_d),
                     wfoT_d, xwo_d, acts=[(0, D, AF.Copy)])
        p16 = ph_ffn1loop(nc, es, pool, p15.done, xw_d, u3_d, b3_d, c0_d, id_d, cf_d)
        p17 = ph_final(nc, es, pool, p16.done, cf_d, uoT_d, bo_d, xwo_d, s1p_d,
                       x1_d, x2_d)
    return nc




def _t(a):
    return np.ascontiguousarray(np.asarray(a, np.float32).T)


def _bcast(v):
    a = np.asarray(v, np.float32)
    return np.ascontiguousarray(np.broadcast_to(a, (128, a.shape[0])))


def _col(v):
    a = np.asarray(v, np.float32)
    return np.ascontiguousarray(a.reshape(-1, 128).T)

def host_maps(inp):
    x = np.asarray(inp["L0_attn_x"], np.float32)
    A = np.asarray(inp["L0_attn_A"], np.float32)
    asc = np.empty((128, SC_F), np.float32)
    asc[0::2, :] = A[:512][None, :]
    asc[1::2, :] = A[512:][None, :]
    wau = np.zeros((2 * D, 64), np.float32)
    wau[:, 0:16] = np.asarray(inp["L1_attn_A"], np.float32)
    wau[:, 32:48] = np.asarray(inp["L1_attn_B_ssm"], np.float32)
    wfic = np.concatenate([_t(np.asarray(inp["L1_ffn_wf"], np.float32)),
                           _t(np.asarray(inp["L1_ffn_wi"], np.float32)),
                           _t(np.asarray(inp["L1_ffn_wc"], np.float32))], axis=1)
    u3 = np.concatenate([_t(np.asarray(inp["L1_ffn_uf"], np.float32)),
                         _t(np.asarray(inp["L1_ffn_ui"], np.float32)),
                         _t(np.asarray(inp["L1_ffn_uc"], np.float32))], axis=1)
    b3 = _col(np.concatenate([np.asarray(inp["L1_ffn_bf"], np.float32),
                              np.asarray(inp["L1_ffn_bi"], np.float32),
                              np.asarray(inp["L1_ffn_bc"], np.float32)]))
    c0 = np.ascontiguousarray(
        np.broadcast_to(np.asarray(inp["L1_ffn_in_c"], np.float32)[:, None], (D, ROWS)))
    common = dict(
        ln0g=_bcast(inp["L0_attn_gamma"]), ln0b=_bcast(inp["L0_attn_beta"]),
        win0=np.asarray(inp["L0_attn_W_in"], np.float32),
        bin0=_bcast(inp["L0_attn_b_in"]), asc=asc,
        wout0=np.asarray(inp["L0_attn_W_out"], np.float32),
        bout0=_bcast(inp["L0_attn_b_out"]),
        mur=_col(inp["L0_ffn_mu_r"]), muk=_col(inp["L0_ffn_mu_k"]),
        wrT=_t(np.asarray(inp["L0_ffn_W_r"], np.float32)),
        wkT=_t(np.asarray(inp["L0_ffn_W_k"], np.float32)),
        wvT=_t(np.asarray(inp["L0_ffn_W_v"], np.float32)),
        ln1g=_bcast(inp["L1_attn_norm_w"]), ln1b=_bcast(inp["L1_attn_norm_b"]),
        win1n=np.asarray(inp["L1_attn_W_in"], np.float32),
        binc=_col(inp["L1_attn_b_in"]), wau=wau,
        cssm=np.asarray(inp["L1_attn_C_ssm"], np.float32),
        woT1=_t(np.asarray(inp["L1_attn_W_out"], np.float32)),
        bout1=_bcast(inp["L1_attn_b_out"]),
        wficT=wfic, wfoT=_t(np.asarray(inp["L1_ffn_wo"], np.float32)),
        u3=u3, b3=b3, c0=c0, ident=np.eye(128, dtype=np.float32),
        uoT=_t(np.asarray(inp["L1_ffn_uo"], np.float32)),
        bo=_bcast(inp["L1_ffn_bo"]),
    )
    maps = []
    for k in range(NCORES):
        xs = np.zeros((B, RB, D), np.float32)
        r0 = k * ROWS
        pre = min(PRE, r0)
        if pre > 0:
            xs[:, PRE - pre:PRE, :] = x[:, r0 - pre:r0, :]
        xs[:, PRE:, :] = x[:, r0:r0 + ROWS, :]
        oh = np.zeros(8, np.float32)
        if k >= 1:
            oh[k - 1] = 1.0
        m = dict(common)
        m["x"] = np.ascontiguousarray(xs.reshape(TOK, D))
        m["mask"] = np.full((128, 1), 0.0 if k == 0 else 1.0, np.float32)
        m["oh"] = _bcast(oh)
        maps.append(m)
    return maps


def kernel(**inp):
    nc = build_fused()
    maps = host_maps(inp)
    res = run_bass_kernel_spmd(nc, maps, core_ids=list(range(NCORES)))
    out = np.empty((B, L, D), np.float32)
    for k in range(NCORES):
        x2 = res.results[k]["x2"]
        out[:, k * ROWS:(k + 1) * ROWS, :] = x2.reshape(B, ROWS, D)
    return out


# version 10
# speedup vs baseline: 1.1158x; 1.0078x over previous
import numpy as np
import ml_dtypes
from contextlib import ExitStack
import concourse.bass as bass
import concourse.mybir as mybir
from concourse.bass_utils import run_bass_kernel_spmd

F32 = mybir.dt.float32
BF16 = mybir.dt.bfloat16
AF = mybir.ActivationFunctionType
ALU = mybir.AluOpType
AX = mybir.AxisListType
BF = ml_dtypes.bfloat16

NCORES = 8
B, L, D = 8, 8192, 1024
ROWS = 1024
PRE = 128
RB = ROWS + PRE
TOK = B * RB
TOWN = B * ROWS
LN_EPS = 1e-5

SC_W = 128
SC_LC = 128
SC_STEPS = SC_W + SC_LC
SC_F = 512
SC_BLK = 8
NBLK = SC_STEPS // SC_BLK
WBLK = SC_W // SC_BLK
BLKC = SC_BLK * SC_F
SC_H = SC_F // 2


class S:

    def __init__(self, h):
        self.h = h
        self.n = 0

    def inc(self, ins, k=1):
        self.n += k
        ins.then_inc(self.h, k)
        return ins


class Pool:
    def __init__(self, nc, es, n=48):
        self.sems = [S(es.enter_context(nc.semaphore(f"sp{i}"))) for i in range(n)]
        self.i = 0

    def take(self):
        s = self.sems[self.i % len(self.sems)]
        self.i += 1
        return s


class Ph:
    def __init__(self, nc, pool, deps):
        self.nc = nc
        self.pool = pool
        self.deps = deps
        self.done = []

    def sem(self):
        return self.pool.take()

    def entry(self, eng):
        for (s, c) in self.deps:
            eng.wait_ge(s.h, c)


_LN_N = [0]


def ph_ln(nc, es, pool, deps, Tn, x_d, gam_d, bet_d, xn_d, z_d, double):
    ph = Ph(nc, pool, deps)
    NTIL = Tn // 128
    _LN_N[0] += 1
    pfx = f"lnp{_LN_N[0]}"
    with ExitStack() as ees:
        E = ees.enter_context
        x_sb = E(nc.sbuf_tensor(pfx + "x", [128, 2 * D], F32))
        xn_sb = E(nc.sbuf_tensor(pfx + "xn", [128, 2 * D], F32))
        z_sb = E(nc.sbuf_tensor(pfx + "z", [128, 2 * D], BF16))
        g_sb = E(nc.sbuf_tensor(pfx + "g", [128, D], F32))
        be_sb = E(nc.sbuf_tensor(pfx + "be", [128, D], F32))
        sq_sb = E(nc.sbuf_tensor(pfx + "sq", [128, D], F32))
        st_sb = E(nc.sbuf_tensor(pfx + "st", [128, 16], F32))
        eps_sb = E(nc.sbuf_tensor(pfx + "eps", [128, 1], F32))
        cld = ph.sem()
        lde, ldo = ph.sem(), ph.sem()
        ld = [lde, ldo]
        sq1 = ph.sem()
        sq2 = ph.sem()
        rst = ph.sem()
        xnd = ph.sem()
        zd = ph.sem()
        sx = [ph.sem(), ph.sem()]
        sz = [ph.sem(), ph.sem()]
        vv = ph.sem()
        b = [s.n for s in (cld, lde, ldo, sq1, sq2, rst, xnd, zd, sx[0], sx[1],
                           sz[0], sz[1], vv)]
        (bcld, blde, bldo, bsq1, bsq2, brst, bxnd, bzd, bsx0, bsx1,
         bsz0, bsz1, bvv) = b
        bld = [blde, bldo]
        bsx_ = [bsx0, bsx1]
        bsz_ = [bsz0, bsz1]
        block = E(nc.Block())

        @block.sync
        def _(sy):
            ph.entry(sy)
            cld.inc(sy.dma_start(out=g_sb[:], in_=gam_d), 16)
            cld.inc(sy.dma_start(out=be_sb[:], in_=bet_d), 16)
            for i in range(NTIL):
                par = i % 2
                if i >= 2:
                    sy.wait_ge(xnd.h, bxnd + i - 1)
                ld[par].inc(sy.dma_start(out=x_sb[:, par * D:(par + 1) * D],
                                         in_=x_d[i * 128:(i + 1) * 128, :]), 16)
                sy.wait_ge(zd.h, bzd + i)
                if i >= 1:
                    sz[(i - 1) % 2].inc(
                        sy.dma_start(out=z_d[(i - 1) * 128:i * 128, :],
                                     in_=z_sb[:, ((i - 1) % 2) * D:((i - 1) % 2) * D + D]),
                        16)
            sy.wait_ge(zd.h, bzd + NTIL)
            sz[(NTIL - 1) % 2].inc(
                sy.dma_start(out=z_d[(NTIL - 1) * 128:NTIL * 128, :],
                             in_=z_sb[:, ((NTIL - 1) % 2) * D:((NTIL - 1) % 2) * D + D]),
                16)

        @block.gpsimd
        def _(g):
            ph.entry(g)
            for i in range(NTIL):
                par = i % 2
                g.wait_ge(xnd.h, bxnd + i + 1)
                sx[par].inc(g.dma_start(out=xn_d[i * 128:(i + 1) * 128, :],
                                        in_=xn_sb[:, par * D:(par + 1) * D]), 16)

        @block.scalar
        def _(s):
            ph.entry(s)
            for i in range(NTIL):
                par = i % 2
                p0 = 8 * par
                s.wait_ge(ld[par].h, bld[par] + 16 * (i // 2 + 1))
                sq1.inc(s.activation(sq_sb[:, 0:D], x_sb[:, par * D:(par + 1) * D],
                                     AF.Square, accum_out=st_sb[:, p0 + 0:p0 + 1]))
                s.wait_ge(rst.h, brst + (4 * i if double else 2 * i) + 1)
                sq1.inc(s.activation(st_sb[:, p0 + 3:p0 + 4], st_sb[:, p0 + 3:p0 + 4],
                                     AF.Sqrt, bias=eps_sb[:, 0:1]))
                if double:
                    s.wait_ge(xnd.h, bxnd + i + 1)
                    sq2.inc(s.activation(sq_sb[:, 0:D], xn_sb[:, par * D:(par + 1) * D],
                                         AF.Square, accum_out=st_sb[:, p0 + 4:p0 + 5]))
                    s.wait_ge(rst.h, brst + 4 * i + 3)
                    sq2.inc(s.activation(st_sb[:, p0 + 7:p0 + 8], st_sb[:, p0 + 7:p0 + 8],
                                         AF.Sqrt, bias=eps_sb[:, 0:1]))

        @block.vector
        def _(v):
            ph.entry(v)
            v.memset(eps_sb[:], LN_EPS)
            vvc = [0]

            def tick(ins):
                vvc[0] += 1
                vv.inc(ins)

            def syn():
                v.wait_ge(vv.h, bvv + vvc[0])

            for i in range(NTIL):
                par = i % 2
                p0 = 8 * par
                xap = x_sb[:, par * D:(par + 1) * D]
                xnap = xn_sb[:, par * D:(par + 1) * D]
                zap = z_sb[:, par * D:(par + 1) * D]
                v.wait_ge(cld.h, bcld + 32)
                v.wait_ge(ld[par].h, bld[par] + 16 * (i // 2 + 1))
                tick(v.reduce_sum(st_sb[:, p0 + 1:p0 + 2], xap, axis=AX.X))
                syn()
                tick(v.tensor_scalar_mul(st_sb[:, p0 + 1:p0 + 2],
                                         st_sb[:, p0 + 1:p0 + 2], 1.0 / D))
                v.wait_ge(sq1.h, bsq1 + 2 * i + 1)
                syn()
                tick(v.tensor_mul(st_sb[:, p0 + 2:p0 + 3], st_sb[:, p0 + 1:p0 + 2],
                                  st_sb[:, p0 + 1:p0 + 2]))
                syn()
                rst.inc(v.scalar_tensor_tensor(st_sb[:, p0 + 3:p0 + 4],
                                               st_sb[:, p0 + 0:p0 + 1], 1.0 / D,
                                               st_sb[:, p0 + 2:p0 + 3],
                                               ALU.mult, ALU.subtract))
                v.wait_ge(sq1.h, bsq1 + 2 * i + 2)
                tick(v.reciprocal(st_sb[:, p0 + 3:p0 + 4], st_sb[:, p0 + 3:p0 + 4]))
                rst.inc(v.engine_nop())
                if i >= 2:
                    v.wait_ge(sx[par].h, bsx_[par] + 16 * (i // 2))
                syn()
                xnd.inc(v.tensor_scalar(xnap, xap, st_sb[:, p0 + 1:p0 + 2],
                                        st_sb[:, p0 + 3:p0 + 4], ALU.subtract, ALU.mult))
                if i >= 2:
                    v.wait_ge(sz[par].h, bsz_[par] + 16 * (i // 2))
                if double:
                    tick(v.reduce_sum(st_sb[:, p0 + 5:p0 + 6], xnap, axis=AX.X))
                    syn()
                    tick(v.tensor_scalar_mul(st_sb[:, p0 + 5:p0 + 6],
                                             st_sb[:, p0 + 5:p0 + 6], 1.0 / D))
                    v.wait_ge(sq2.h, bsq2 + 2 * i + 1)
                    syn()
                    tick(v.tensor_mul(st_sb[:, p0 + 6:p0 + 7], st_sb[:, p0 + 5:p0 + 6],
                                      st_sb[:, p0 + 5:p0 + 6]))
                    syn()
                    rst.inc(v.scalar_tensor_tensor(st_sb[:, p0 + 7:p0 + 8],
                                                   st_sb[:, p0 + 4:p0 + 5], 1.0 / D,
                                                   st_sb[:, p0 + 6:p0 + 7],
                                                   ALU.mult, ALU.subtract))
                    v.wait_ge(sq2.h, bsq2 + 2 * i + 2)
                    tick(v.reciprocal(st_sb[:, p0 + 7:p0 + 8], st_sb[:, p0 + 7:p0 + 8]))
                    rst.inc(v.engine_nop())
                    syn()
                    v.tensor_scalar(zap, xnap, st_sb[:, p0 + 5:p0 + 6],
                                    st_sb[:, p0 + 7:p0 + 8], ALU.subtract, ALU.mult)
                    v.tensor_mul(zap, zap, g_sb[:])
                else:
                    v.tensor_mul(zap, xnap, g_sb[:])
                zd.inc(v.tensor_add(zap, zap, be_sb[:]))

        ph.done = [(sx[0], sx[0].n), (sx[1], sx[1].n),
                   (sz[0], sz[0].n), (sz[1], sz[1].n)]
    return ph



def ph_mmb(nc, es, pool, deps, name, Ttok, K, Nout, lhs, w_d, out_d,
           out_dtype=BF16, bias_d=None, acts=None, relu_sq=False, extras=()):
    ph = Ph(nc, pool, deps)
    KC = K // 128
    TG = Ttok // 512
    NT2 = max(1, Nout // 512)
    NW = min(Nout, 512)
    MCH = Ttok // 128
    NE = len(extras)
    kind, src_d = lhs
    use_scalar = (acts is not None) or relu_sq
    assert not (use_scalar and NE)
    with ExitStack() as ees:
        E = ees.enter_context
        w_sb = E(nc.sbuf_tensor(name + "w", [128, KC * Nout], BF16))
        a_sb = E(nc.sbuf_tensor(name + "a", [128, 2 * KC * 512], BF16))
        bias_sb = E(nc.sbuf_tensor(name + "b", [128, Nout], F32)) if bias_d is not None else None
        e_sb = E(nc.sbuf_tensor(name + "e", [128, max(1, 2 * NE * Nout)], F32))
        t_sb = E(nc.sbuf_tensor(name + "t", [128, 2 * NW], F32))
        t2_sb = E(nc.sbuf_tensor(name + "t2", [128, 2 * NW], F32))
        o_sb = E(nc.sbuf_tensor(name + "o", [128, 2 * Nout], out_dtype))
        ps = E(nc.psum_tensor(name + "ps", [128, 8 * 512], F32))
        wld = ph.sem(); ald = [ph.sem(), ph.sem()]; eld = [ph.sem(), ph.sem()]
        ald2 = [ph.sem(), ph.sem()]
        mmd = ph.sem(); vps = ph.sem(); sev = ph.sem(); vev = ph.sem()
        stp = [ph.sem(), ph.sem()]
        bald2 = [x.n for x in ald2]
        bwld = wld.n; bald = [x.n for x in ald]; beld = [x.n for x in eld]
        bmmd = mmd.n; bvps = vps.n; bsev = sev.n; bvev = vev.n
        bstp = [x.n for x in stp]
        n_w_dma = KC + (1 if bias_d is not None else 0)
        fin, bfin, finpm = (sev, bsev, NT2) if use_scalar else (vev, bvev, 1)
        block = E(nc.Block())

        @block.gpsimd
        def _(g):
            ph.entry(g)
            for k in range(KC):
                wld.inc(g.dma_start(out=w_sb[:, k * Nout:(k + 1) * Nout],
                                    in_=w_d[k * 128:(k + 1) * 128, :]), 16)
            if bias_d is not None:
                wld.inc(g.dma_start(out=bias_sb[:], in_=bias_d), 16)
            for mt in range(MCH):
                par = mt % 2
                for ei, (rmap, op) in enumerate(extras):
                    eld[par].inc(g.dma_start(
                        out=e_sb[:, (par * NE + ei) * Nout:(par * NE + ei + 1) * Nout],
                        in_=rmap(mt)), 16)
                g.wait_ge(fin.h, bfin + finpm * (mt + 1))
                stp[par].inc(g.dma_start(out=out_d[mt * 128:(mt + 1) * 128, :],
                                         in_=o_sb[:, par * Nout:(par + 1) * Nout]), 16)

        @block.sync
        def _(sy):
            ph.entry(sy)
            for gi in range(TG):
                par = gi % 2
                if gi >= 2:
                    sy.wait_ge(mmd.h, bmmd + (gi - 1) * 4)
                if kind == "t2":
                    for k in range(KC):
                        ald[par].inc(sy.dma_start_transpose(
                            a_sb[:, par * KC * 512 + k * 512:par * KC * 512 + (k + 1) * 512],
                            src_d[gi * 512:(gi + 1) * 512, k * 128:(k + 1) * 128]), 16)
                else:
                    ald[par].inc(sy.dma_start(
                        out=a_sb[:, par * KC * 512:(par + 1) * KC * 512]
                            .rearrange("p (k n) -> p k n", k=KC),
                        in_=src_d.rearrange("(k p) t -> p k t", p=128)
                        [:, :, gi * 512:(gi + 1) * 512]), 16)


        TIF = max(1, 8 // NT2)

        @block.tensor
        def _(t):
            ph.entry(t)
            t.wait_ge(wld.h, bwld + 16 * n_w_dma)
            for gi in range(TG):
                par = gi % 2
                nld = KC if kind == "t2" else 1
                t.wait_ge(ald[par].h, bald[par] + 16 * nld * (gi // 2 + 1))
                for m in range(4):
                    idx = gi * 4 + m
                    if idx >= TIF:
                        t.wait_ge(vps.h, bvps + NT2 * (idx - TIF + 1))
                    for k in range(KC):
                        for nt in range(NT2):
                            bank = (idx % TIF) * NT2 + nt
                            ins = t.matmul(
                                ps[:, bank * 512:bank * 512 + NW],
                                a_sb[:, par * KC * 512 + k * 512 + m * 128:
                                        par * KC * 512 + k * 512 + (m + 1) * 128],
                                w_sb[:, k * Nout + nt * NW:k * Nout + nt * NW + NW],
                                start=(k == 0), stop=(k == KC - 1))
                    mmd.inc(ins)

        @block.vector
        def _(v):
            ph.entry(v)
            for mt in range(MCH):
                par = mt % 2
                if NE:
                    v.wait_ge(eld[par].h, beld[par] + 16 * NE * (mt // 2 + 1))
                if mt >= 2:
                    v.wait_ge(stp[par].h, bstp[par] + 16 * (mt // 2))
                for nt in range(NT2):
                    idx = mt * NT2 + nt
                    bank = (mt % TIF) * NT2 + nt
                    v.wait_ge(mmd.h, bmmd + mt + 1)
                    if use_scalar and idx >= 2:
                        v.wait_ge(sev.h, bsev + idx - 1)
                    tgt = (t_sb[:, (idx % 2) * NW:(idx % 2) * NW + NW]
                           if (use_scalar or NE)
                           else o_sb[:, par * Nout + nt * NW:par * Nout + nt * NW + NW])
                    if bias_d is not None:
                        ins = v.tensor_tensor(tgt, ps[:, bank * 512:bank * 512 + NW],
                                              bias_sb[:, nt * NW:nt * NW + NW], ALU.add)
                    else:
                        ins = v.tensor_copy(tgt, ps[:, bank * 512:bank * 512 + NW])
                    vps.inc(ins)
                    if NE:
                        src = t_sb[:, (idx % 2) * NW:(idx % 2) * NW + NW]
                        for ei, (rmap, op) in enumerate(extras):
                            eap = e_sb[:, (par * NE + ei) * Nout + nt * NW:
                                          (par * NE + ei) * Nout + nt * NW + NW]
                            last = (ei == NE - 1)
                            dst = (o_sb[:, par * Nout + nt * NW:par * Nout + nt * NW + NW]
                                   if last else src)
                            ins = v.tensor_tensor(dst, src, eap, op)
                    if not use_scalar and nt == NT2 - 1:
                        vev.inc(ins)


        @block.scalar
        def _(s):
            ph.entry(s)
            if not use_scalar:
                return
            for mt in range(MCH):
                par = mt % 2
                for nt in range(NT2):
                    idx = mt * NT2 + nt
                    s.wait_ge(vps.h, bvps + idx + 1)
                    if mt >= 2 and nt == 0:
                        s.wait_ge(stp[par].h, bstp[par] + 16 * (mt // 2))
                    ins = None
                    for (lo, hi, fn) in acts:
                        if hi <= nt * NW or lo >= (nt + 1) * NW:
                            continue
                        clo = max(lo - nt * NW, 0)
                        chi = min(hi - nt * NW, NW)
                        srcap = t_sb[:, (idx % 2) * NW + clo:(idx % 2) * NW + chi]
                        if relu_sq:
                            dst = t2_sb[:, (idx % 2) * NW + clo:(idx % 2) * NW + chi]
                        else:
                            dst = o_sb[:, par * Nout + nt * NW + clo:
                                          par * Nout + nt * NW + chi]
                        ins = s.activation(dst, srcap, fn)
                    if relu_sq:
                        ins = s.activation(o_sb[:, par * Nout + nt * NW:
                                                   par * Nout + nt * NW + NW],
                                           t2_sb[:, (idx % 2) * NW:(idx % 2) * NW + NW],
                                           AF.Square)
                    sev.inc(ins)

        ph.done = [(stp[0], stp[0].n), (stp[1], stp[1].n)]
    return ph


def ph_ug(nc, es, pool, deps, z_d, w_d, bias_d, mask_d, pu_d, pg_d):
    ph = Ph(nc, pool, deps)
    K, Nout = D, 2 * D
    KC, NT2, NW = 8, 4, 512
    TG = TOK // 512
    MCH = TOK // 128
    pu3 = pu_d.rearrange("(tl q) f -> tl q f", q=128)
    pg3 = pg_d.rearrange("(tl q) f -> tl q f", q=128)

    def targets(mt):
        b, cb = mt // 9, mt % 9
        tg = []
        if cb == 0:
            tg.append((0, 0))
        else:
            tg.append((cb - 1, SC_W))
            if cb <= 7:
                tg.append((cb, 0))
        return b, tg

    nst = [0, 0]
    st_after = []
    for mt in range(MCH):
        b, tg = targets(mt)
        nst[mt % 2] += 16 * 2 * len(tg)
        st_after.append((mt % 2, nst[0], nst[1]))

    with ExitStack() as ees:
        E = ees.enter_context
        w_sb = E(nc.sbuf_tensor("ugw", [128, KC * Nout], BF16))
        a_sb = E(nc.sbuf_tensor("uga", [128, 2 * KC * 512], BF16))
        bias_sb = E(nc.sbuf_tensor("ugb", [128, Nout], F32))
        t_sb = E(nc.sbuf_tensor("ugt", [128, 2 * NW], F32))
        o_sb = E(nc.sbuf_tensor("ugo", [128, 2 * Nout], BF16))
        m_sb = E(nc.sbuf_tensor("ugm", [128, 1], F32))
        ps = E(nc.psum_tensor("ugps", [128, 8 * 512], F32))
        wld = ph.sem(); ald = [ph.sem(), ph.sem()]; ald2 = [ph.sem(), ph.sem()]
        mmd = ph.sem(); vps = ph.sem(); sev = ph.sem(); vev = ph.sem()
        stp = [ph.sem(), ph.sem()]
        bwld = wld.n; bald = [x.n for x in ald]; bald2 = [x.n for x in ald2]
        bmmd = mmd.n; bvps = vps.n
        bsev = sev.n; bvev = vev.n; bstp = [x.n for x in stp]
        block = E(nc.Block())

        @block.gpsimd
        def _(g):
            ph.entry(g)
            for k in range(KC):
                wld.inc(g.dma_start(out=w_sb[:, k * Nout:(k + 1) * Nout],
                                    in_=w_d[k * 128:(k + 1) * 128, :]), 16)
            wld.inc(g.dma_start(out=bias_sb[:], in_=bias_d), 16)
            wld.inc(g.dma_start(out=m_sb[:], in_=mask_d), 16)
            for mt in range(MCH):
                par = mt % 2
                b, tg = targets(mt)
                ispre = (mt % 9 == 0)
                if ispre:
                    g.wait_ge(vev.h, bvev + len([m for m in range(mt + 1) if m % 9 == 0]))
                else:
                    g.wait_ge(sev.h, bsev + NT2 * (mt + 1))
                for (k, tlb) in tg:
                    q0 = k * 16 + b * 2
                    srcu = o_sb[:, par * Nout:par * Nout + 1024] \
                        .rearrange("p (gb f) -> p gb f", gb=2)
                    srcg = o_sb[:, par * Nout + 1024:par * Nout + 2048] \
                        .rearrange("p (gb f) -> p gb f", gb=2)
                    stp[par].inc(g.dma_start(out=pu3[tlb:tlb + 128, q0:q0 + 2, :],
                                             in_=srcu), 16)
                    stp[par].inc(g.dma_start(out=pg3[tlb:tlb + 128, q0:q0 + 2, :],
                                             in_=srcg), 16)

        def emit_t2(eng, gi):
            par = gi % 2
            if gi >= 2:
                eng.wait_ge(mmd.h, bmmd + (gi - 1) * 4)
            for k in range(KC):
                ald[par].inc(eng.dma_start_transpose(
                    a_sb[:, par * KC * 512 + k * 512:par * KC * 512 + (k + 1) * 512],
                    z_d[gi * 512:(gi + 1) * 512, k * 128:(k + 1) * 128]), 16)

        @block.sync
        def _(sy):
            ph.entry(sy)
            for gi in range(TG):
                emit_t2(sy, gi)

        TIF = 2

        @block.tensor
        def _(t):
            ph.entry(t)
            t.wait_ge(wld.h, bwld + 16 * (KC + 2))
            for gi in range(TG):
                par = gi % 2
                t.wait_ge(ald[par].h, bald[par] + 16 * KC * (gi // 2 + 1))
                for m in range(4):
                    idx = gi * 4 + m
                    if idx >= TIF:
                        t.wait_ge(vps.h, bvps + NT2 * (idx - TIF + 1))
                    for k in range(KC):
                        for nt in range(NT2):
                            bank = (idx % TIF) * NT2 + nt
                            ins = t.matmul(
                                ps[:, bank * 512:bank * 512 + NW],
                                a_sb[:, par * KC * 512 + k * 512 + m * 128:
                                        par * KC * 512 + k * 512 + (m + 1) * 128],
                                w_sb[:, k * Nout + nt * NW:k * Nout + nt * NW + NW],
                                start=(k == 0), stop=(k == KC - 1))
                    mmd.inc(ins)

        @block.vector
        def _(v):
            ph.entry(v)
            for mt in range(MCH):
                par = mt % 2
                ispre = (mt % 9 == 0)
                if mt >= 2:
                    pp, c0_, c1_ = st_after[mt - 2]
                    v.wait_ge(stp[par].h, bstp[par] + (c0_ if par == 0 else c1_))
                for nt in range(NT2):
                    idx = mt * NT2 + nt
                    bank = (mt % TIF) * NT2 + nt
                    v.wait_ge(mmd.h, bmmd + mt + 1)
                    if idx >= 2:
                        v.wait_ge(sev.h, bsev + idx - 1)
                    vps.inc(v.tensor_tensor(t_sb[:, (idx % 2) * NW:(idx % 2) * NW + NW],
                                            ps[:, bank * 512:bank * 512 + NW],
                                            bias_sb[:, nt * NW:nt * NW + NW], ALU.add))
                if ispre:
                    v.wait_ge(sev.h, bsev + NT2 * (mt + 1))
                    vev.inc(v.tensor_scalar_mul(o_sb[:, par * Nout:(par + 1) * Nout],
                                                o_sb[:, par * Nout:(par + 1) * Nout],
                                                m_sb[:, 0:1]))

        @block.scalar
        def _(s):
            ph.entry(s)
            for mt in range(MCH):
                par = mt % 2
                for nt in range(NT2):
                    idx = mt * NT2 + nt
                    s.wait_ge(vps.h, bvps + idx + 1)
                    if mt >= 2 and nt == 0:
                        pp, c0_, c1_ = st_after[mt - 2]
                        s.wait_ge(stp[par].h, bstp[par] + (c0_ if par == 0 else c1_))
                    fn = AF.Copy if nt < 2 else AF.Sigmoid
                    sev.inc(s.activation(o_sb[:, par * Nout + nt * NW:
                                                 par * Nout + nt * NW + NW],
                                         t_sb[:, (idx % 2) * NW:(idx % 2) * NW + NW], fn))

        ph.done = [(stp[0], stp[0].n), (stp[1], stp[1].n)]
    return ph



def ph_lnug(nc, es, pool, deps, x_d, gam_d, bet_d, xn_d, z_d,
            w_d, bias_d, mask_d, pu_d, pg_d):
    ph = Ph(nc, pool, deps)
    LAG = 5
    NTIL = TOK // 128
    K, Nout = D, 2 * D
    KC, NT2, NW = 8, 4, 512
    TG = TOK // 512
    MCH = TOK // 128
    pu3 = pu_d.rearrange("(tl q) f -> tl q f", q=128)
    pg3 = pg_d.rearrange("(tl q) f -> tl q f", q=128)

    def targets(mt):
        b, cb = mt // 9, mt % 9
        tg = []
        if cb == 0:
            tg.append((0, 0))
        else:
            tg.append((cb - 1, SC_W))
            if cb <= 7:
                tg.append((cb, 0))
        return b, tg

    nst = [0, 0]
    st_after = []
    for mt in range(MCH):
        b, tg = targets(mt)
        nst[mt % 2] += 16 * 2 * len(tg)
        st_after.append((mt % 2, nst[0], nst[1]))

    with ExitStack() as ees:
        E = ees.enter_context
        x_sb = E(nc.sbuf_tensor("lux", [128, 2 * D], F32))
        xn_sb = E(nc.sbuf_tensor("luxn", [128, 2 * D], F32))
        z_sb = E(nc.sbuf_tensor("luz", [128, 2 * D], BF16))
        g_sb = E(nc.sbuf_tensor("lug", [128, D], F32))
        be_sb = E(nc.sbuf_tensor("lube", [128, D], F32))
        sq_sb = E(nc.sbuf_tensor("lusq", [128, D], F32))
        st_sb = E(nc.sbuf_tensor("lust", [128, 16], F32))
        eps_sb = E(nc.sbuf_tensor("lueps", [128, 1], F32))
        w_sb = E(nc.sbuf_tensor("luw", [128, KC * Nout], BF16))
        a_sb = E(nc.sbuf_tensor("lua", [128, 2 * KC * 512], BF16))
        bias_sb = E(nc.sbuf_tensor("lub", [128, Nout], F32))
        t_sb = E(nc.sbuf_tensor("lut", [128, 2 * NW], F32))
        o_sb = E(nc.sbuf_tensor("luo", [128, 2 * Nout], BF16))
        m_sb = E(nc.sbuf_tensor("lum", [128, 1], F32))
        ps = E(nc.psum_tensor("lups", [128, 8 * 512], F32))
        cld = ph.sem()
        lde, ldo = ph.sem(), ph.sem()
        ld = [lde, ldo]
        sq1 = ph.sem(); sq2 = ph.sem(); rst = ph.sem()
        xnd = ph.sem(); zd = ph.sem()
        sx = [ph.sem(), ph.sem()]
        sz = [ph.sem(), ph.sem()]
        vv = ph.sem()
        bcld = cld.n; bld = [lde.n, ldo.n]
        bsq1 = sq1.n; bsq2 = sq2.n; brst = rst.n
        bxnd = xnd.n; bzd = zd.n
        bsx_ = [s.n for s in sx]; bsz_ = [s.n for s in sz]
        bvv = vv.n
        wld = ph.sem(); ald = [ph.sem(), ph.sem()]
        mmd = ph.sem(); vps = ph.sem(); sev = ph.sem(); vev = ph.sem()
        stp = [ph.sem(), ph.sem()]
        bwld = wld.n; bald = [s.n for s in ald]
        bmmd = mmd.n; bvps = vps.n
        bsev = sev.n; bvev = vev.n; bstp = [s.n for s in stp]
        block = E(nc.Block())

        @block.sync
        def _(sy):
            ph.entry(sy)
            cld.inc(sy.dma_start(out=g_sb[:], in_=gam_d), 16)
            cld.inc(sy.dma_start(out=be_sb[:], in_=bet_d), 16)

            def t2(gi):
                par = gi % 2
                if gi >= 2:
                    sy.wait_ge(mmd.h, bmmd + (gi - 1) * 4)
                sy.wait_ge(sz[0].h, bsz_[0] + 16 * (2 * gi + 2))
                sy.wait_ge(sz[1].h, bsz_[1] + 16 * (2 * gi + 2))
                for k in range(KC):
                    ald[par].inc(sy.dma_start_transpose(
                        a_sb[:, par * KC * 512 + k * 512:par * KC * 512 + (k + 1) * 512],
                        z_d[gi * 512:(gi + 1) * 512, k * 128:(k + 1) * 128]), 16)

            for i in range(NTIL):
                sy.wait_ge(zd.h, bzd + i)
                if i >= 1:
                    c = i - 1
                    sz[c % 2].inc(
                        sy.dma_start(out=z_d[c * 128:(c + 1) * 128, :],
                                     in_=z_sb[:, (c % 2) * D:(c % 2) * D + D]), 16)
                    if c % 4 == 3:
                        t2(c // 4)
            sy.wait_ge(zd.h, bzd + NTIL)
            c = NTIL - 1
            sz[c % 2].inc(
                sy.dma_start(out=z_d[c * 128:(c + 1) * 128, :],
                             in_=z_sb[:, (c % 2) * D:(c % 2) * D + D]), 16)
            t2(c // 4)

        TIF = 2

        @block.tensor
        def _(t):
            ph.entry(t)
            t.wait_ge(wld.h, bwld + 16 * (KC + 2))
            for gi in range(TG):
                par = gi % 2
                t.wait_ge(ald[par].h, bald[par] + 16 * KC * (gi // 2 + 1))
                for m in range(4):
                    idx = gi * 4 + m
                    if idx >= TIF:
                        t.wait_ge(vps.h, bvps + NT2 * (idx - TIF + 1))
                    for k in range(KC):
                        for nt in range(NT2):
                            bank = (idx % TIF) * NT2 + nt
                            ins = t.matmul(
                                ps[:, bank * 512:bank * 512 + NW],
                                a_sb[:, par * KC * 512 + k * 512 + m * 128:
                                        par * KC * 512 + k * 512 + (m + 1) * 128],
                                w_sb[:, k * Nout + nt * NW:k * Nout + nt * NW + NW],
                                start=(k == 0), stop=(k == KC - 1))
                    mmd.inc(ins)

        @block.gpsimd
        def _(g):
            ph.entry(g)

            def xload(c):
                ld[c % 2].inc(g.dma_start(out=x_sb[:, (c % 2) * D:(c % 2 + 1) * D],
                                          in_=x_d[c * 128:(c + 1) * 128, :]), 16)

            xload(0)
            xload(1)
            for k in range(KC):
                wld.inc(g.dma_start(out=w_sb[:, k * Nout:(k + 1) * Nout],
                                    in_=w_d[k * 128:(k + 1) * 128, :]), 16)
            wld.inc(g.dma_start(out=bias_sb[:], in_=bias_d), 16)
            wld.inc(g.dma_start(out=m_sb[:], in_=mask_d), 16)
            for i in range(NTIL + LAG):
                if i < NTIL:
                    par = i % 2
                    g.wait_ge(xnd.h, bxnd + i + 1)
                    if i + 2 < NTIL:
                        xload(i + 2)
                    sx[par].inc(g.dma_start(out=xn_d[i * 128:(i + 1) * 128, :],
                                            in_=xn_sb[:, par * D:(par + 1) * D]), 16)
                if i >= LAG:
                    mt = i - LAG
                    par = mt % 2
                    b, tg = targets(mt)
                    ispre = (mt % 9 == 0)
                    if ispre:
                        g.wait_ge(vev.h, bvev + len(
                            [m for m in range(mt + 1) if m % 9 == 0]))
                    else:
                        g.wait_ge(sev.h, bsev + NT2 * (mt + 1))
                    for (k, tlb) in tg:
                        q0 = k * 16 + b * 2
                        srcu = o_sb[:, par * Nout:par * Nout + 1024] \
                            .rearrange("p (gb f) -> p gb f", gb=2)
                        srcg = o_sb[:, par * Nout + 1024:par * Nout + 2048] \
                            .rearrange("p (gb f) -> p gb f", gb=2)
                        stp[par].inc(g.dma_start(out=pu3[tlb:tlb + 128, q0:q0 + 2, :],
                                                 in_=srcu), 16)
                        stp[par].inc(g.dma_start(out=pg3[tlb:tlb + 128, q0:q0 + 2, :],
                                                 in_=srcg), 16)

        @block.scalar
        def _(s):
            ph.entry(s)
            for i in range(NTIL + LAG):
                if i < NTIL:
                    par = i % 2
                    p0 = 8 * par
                    s.wait_ge(ld[par].h, bld[par] + 16 * (i // 2 + 1))
                    sq1.inc(s.activation(sq_sb[:, 0:D], x_sb[:, par * D:(par + 1) * D],
                                         AF.Square, accum_out=st_sb[:, p0 + 0:p0 + 1]))
                    s.wait_ge(rst.h, brst + 4 * i + 1)
                    sq1.inc(s.activation(st_sb[:, p0 + 3:p0 + 4],
                                         st_sb[:, p0 + 3:p0 + 4],
                                         AF.Sqrt, bias=eps_sb[:, 0:1]))
                    s.wait_ge(xnd.h, bxnd + i + 1)
                    sq2.inc(s.activation(sq_sb[:, 0:D], xn_sb[:, par * D:(par + 1) * D],
                                         AF.Square, accum_out=st_sb[:, p0 + 4:p0 + 5]))
                    s.wait_ge(rst.h, brst + 4 * i + 3)
                    sq2.inc(s.activation(st_sb[:, p0 + 7:p0 + 8],
                                         st_sb[:, p0 + 7:p0 + 8],
                                         AF.Sqrt, bias=eps_sb[:, 0:1]))
                if i >= LAG:
                    mt = i - LAG
                    par = mt % 2
                    for nt in range(NT2):
                        idx = mt * NT2 + nt
                        s.wait_ge(vps.h, bvps + idx + 1)
                        if mt >= 2 and nt == 0:
                            pp, c0_, c1_ = st_after[mt - 2]
                            s.wait_ge(stp[par].h,
                                      bstp[par] + (c0_ if par == 0 else c1_))
                        fn = AF.Copy if nt < 2 else AF.Sigmoid
                        sev.inc(s.activation(o_sb[:, par * Nout + nt * NW:
                                                     par * Nout + nt * NW + NW],
                                             t_sb[:, (idx % 2) * NW:(idx % 2) * NW + NW],
                                             fn))

        @block.vector
        def _(v):
            ph.entry(v)
            v.memset(eps_sb[:], LN_EPS)
            vvc = [0]

            def tick(ins):
                vvc[0] += 1
                vv.inc(ins)

            def syn():
                v.wait_ge(vv.h, bvv + vvc[0])

            for i in range(NTIL + LAG):
                if i < NTIL:
                    par = i % 2
                    p0 = 8 * par
                    xap = x_sb[:, par * D:(par + 1) * D]
                    xnap = xn_sb[:, par * D:(par + 1) * D]
                    zap = z_sb[:, par * D:(par + 1) * D]
                    v.wait_ge(cld.h, bcld + 32)
                    v.wait_ge(ld[par].h, bld[par] + 16 * (i // 2 + 1))
                    tick(v.reduce_sum(st_sb[:, p0 + 1:p0 + 2], xap, axis=AX.X))
                    syn()
                    tick(v.tensor_scalar_mul(st_sb[:, p0 + 1:p0 + 2],
                                             st_sb[:, p0 + 1:p0 + 2], 1.0 / D))
                    v.wait_ge(sq1.h, bsq1 + 2 * i + 1)
                    syn()
                    tick(v.tensor_mul(st_sb[:, p0 + 2:p0 + 3], st_sb[:, p0 + 1:p0 + 2],
                                      st_sb[:, p0 + 1:p0 + 2]))
                    syn()
                    rst.inc(v.scalar_tensor_tensor(st_sb[:, p0 + 3:p0 + 4],
                                                   st_sb[:, p0 + 0:p0 + 1], 1.0 / D,
                                                   st_sb[:, p0 + 2:p0 + 3],
                                                   ALU.mult, ALU.subtract))
                    v.wait_ge(sq1.h, bsq1 + 2 * i + 2)
                    tick(v.reciprocal(st_sb[:, p0 + 3:p0 + 4], st_sb[:, p0 + 3:p0 + 4]))
                    rst.inc(v.engine_nop())
                    if i >= 2:
                        v.wait_ge(sx[par].h, bsx_[par] + 16 * (i // 2))
                    syn()
                    xnd.inc(v.tensor_scalar(xnap, xap, st_sb[:, p0 + 1:p0 + 2],
                                            st_sb[:, p0 + 3:p0 + 4],
                                            ALU.subtract, ALU.mult))
                    if i >= 2:
                        v.wait_ge(sz[par].h, bsz_[par] + 16 * (i // 2))
                    tick(v.reduce_sum(st_sb[:, p0 + 5:p0 + 6], xnap, axis=AX.X))
                    syn()
                    tick(v.tensor_scalar_mul(st_sb[:, p0 + 5:p0 + 6],
                                             st_sb[:, p0 + 5:p0 + 6], 1.0 / D))
                    v.wait_ge(sq2.h, bsq2 + 2 * i + 1)
                    syn()
                    tick(v.tensor_mul(st_sb[:, p0 + 6:p0 + 7], st_sb[:, p0 + 5:p0 + 6],
                                      st_sb[:, p0 + 5:p0 + 6]))
                    syn()
                    rst.inc(v.scalar_tensor_tensor(st_sb[:, p0 + 7:p0 + 8],
                                                   st_sb[:, p0 + 4:p0 + 5], 1.0 / D,
                                                   st_sb[:, p0 + 6:p0 + 7],
                                                   ALU.mult, ALU.subtract))
                    v.wait_ge(sq2.h, bsq2 + 2 * i + 2)
                    tick(v.reciprocal(st_sb[:, p0 + 7:p0 + 8], st_sb[:, p0 + 7:p0 + 8]))
                    rst.inc(v.engine_nop())
                    syn()
                    v.tensor_scalar(zap, xnap, st_sb[:, p0 + 5:p0 + 6],
                                    st_sb[:, p0 + 7:p0 + 8], ALU.subtract, ALU.mult)
                    v.tensor_mul(zap, zap, g_sb[:])
                    zd.inc(v.tensor_add(zap, zap, be_sb[:]))
                if i >= LAG:
                    mt = i - LAG
                    par = mt % 2
                    ispre = (mt % 9 == 0)
                    if mt >= 2:
                        pp, c0_, c1_ = st_after[mt - 2]
                        v.wait_ge(stp[par].h, bstp[par] + (c0_ if par == 0 else c1_))
                    for nt in range(NT2):
                        idx = mt * NT2 + nt
                        bank = (mt % TIF) * NT2 + nt
                        v.wait_ge(mmd.h, bmmd + mt + 1)
                        if idx >= 2:
                            v.wait_ge(sev.h, bsev + idx - 1)
                        vps.inc(v.tensor_tensor(
                            t_sb[:, (idx % 2) * NW:(idx % 2) * NW + NW],
                            ps[:, bank * 512:bank * 512 + NW],
                            bias_sb[:, nt * NW:nt * NW + NW], ALU.add))
                    if ispre:
                        v.wait_ge(sev.h, bsev + NT2 * (mt + 1))
                        vev.inc(v.tensor_scalar_mul(o_sb[:, par * Nout:(par + 1) * Nout],
                                                    o_sb[:, par * Nout:(par + 1) * Nout],
                                                    m_sb[:, 0:1]))

        ph.done = [(stp[0], stp[0].n), (stp[1], stp[1].n),
                   (sx[0], sx[0].n), (sx[1], sx[1].n),
                   (sz[0], sz[0].n), (sz[1], sz[1].n)]
    return ph


def ph_scan0(nc, es, pool, deps, pu_d, pg_d, a_d, hs_d):
    ph = Ph(nc, pool, deps)
    H = SC_H
    HW = 32
    NW_ = SC_STEPS // HW
    WOWN = SC_W // HW
    HWC = HW * SC_F
    puq = pu_d.rearrange("(tl q) f -> q tl f", q=128)
    pgq = pg_d.rearrange("(tl q) f -> q tl f", q=128)
    hs5 = hs_d.rearrange("(b k t) (gb f) -> b k gb t f", b=8, k=8, gb=2)
    with ExitStack() as ees:
        E = ees.enter_context
        u_sb = E(nc.sbuf_tensor("scu", [128, 2 * BLKC], BF16))
        g_sb = E(nc.sbuf_tensor("scg", [128, 2 * BLKC], BF16))
        h_sb = E(nc.sbuf_tensor("sch", [128, 2 * HWC], BF16))
        a_sb = E(nc.sbuf_tensor("sca", [128, SC_F], BF16))
        z_sb = E(nc.sbuf_tensor("scz", [128, SC_F], BF16))
        v_sb = E(nc.sbuf_tensor("scv", [128, SC_F], BF16))
        c_sb = E(nc.sbuf_tensor("scc", [128, SC_F], BF16))
        d_sb = E(nc.sbuf_tensor("scd", [128, SC_F], BF16))
        cld = ph.sem()
        ld = [ph.sem(), ph.sem()]
        vs = ph.sem(); cs = ph.sem()
        st3 = [ph.sem(), ph.sem(), ph.sem()]
        bcld = cld.n; bld = [x.n for x in ld]; bvs = vs.n; bcs = cs.n
        bst3 = [x.n for x in st3]
        shares = [list(range(0, 22)), list(range(22, 43)), list(range(43, 64))]
        block = E(nc.Block())

        def emit_spills(eng, qi, w):
            t0 = w * HW - SC_W
            par = w % 2
            eng.wait_ge(vs.h, bvs + 4 * HW * (w + 1))
            for pair in shares[qi]:
                k, b = pair // 8, pair % 8
                q0 = k * 16 + b * 2
                st3[qi].inc(eng.dma_start(
                    out=hs5[b, k, :, t0:t0 + HW, :],
                    in_=h_sb[par * HWC:par * HWC, 0:0] if False else
                        h_sb[q0:q0 + 2, par * HWC:(par + 1) * HWC]
                        .rearrange("p (t f) -> p t f", t=HW)), 16)

        @block.sync
        def _(sy):
            ph.entry(sy)
            for w in range(WOWN, NW_):
                emit_spills(sy, 1, w)

        @block.scalar
        def _(s_):
            ph.entry(s_)
            spw = iter(range(WOWN, NW_))
            nxt = next(spw, None)
            for s in range(SC_STEPS):
                for hf in range(2):
                    c0, c1 = hf * H, (hf + 1) * H
                    s_.wait_ge(vs.h, bvs + 4 * s + 1 + hf)
                    cs.inc(s_.activation(c_sb[:, c0:c1], v_sb[:, c0:c1], AF.Tanh))
                if nxt is not None and s == (nxt + 1) * HW - 1:
                    emit_spills(s_, 2, nxt)
                    nxt = next(spw, None)

        @block.gpsimd
        def _(g):
            ph.entry(g)
            cld.inc(g.dma_start(out=a_sb[:], in_=a_d), 16)
            for blk in range(NBLK):
                par = blk % 2
                if blk >= 2:
                    g.wait_ge(vs.h, bvs + 4 * SC_BLK * (blk - 1))
                ld[par].inc(g.dma_start(
                    out=u_sb[:, par * BLKC:(par + 1) * BLKC]
                        .rearrange("p (t f) -> p t f", t=SC_BLK),
                    in_=puq[:, blk * SC_BLK:(blk + 1) * SC_BLK, :]), 16)
                ld[par].inc(g.dma_start(
                    out=g_sb[:, par * BLKC:(par + 1) * BLKC]
                        .rearrange("p (t f) -> p t f", t=SC_BLK),
                    in_=pgq[:, blk * SC_BLK:(blk + 1) * SC_BLK, :]), 16)
                if blk % 4 == 0 and blk // 4 - 1 >= WOWN:
                    emit_spills(g, 0, blk // 4 - 1)
            for w in range(NBLK // 4 - 1, NW_):
                if w >= WOWN:
                    emit_spills(g, 0, w)

        @block.vector
        def _(v):
            ph.entry(v)
            v.memset(z_sb[:], 0.0)
            for s in range(SC_STEPS):
                blk = s // SC_BLK
                sl = s % SC_BLK
                par = blk % 2
                w = s // HW
                wl = s % HW
                wpar = w % 2
                if sl == 0:
                    if blk == 0:
                        v.wait_ge(cld.h, bcld + 16)
                    v.wait_ge(ld[par].h, bld[par] + 32 * (blk // 2 + 1))
                if wl == 0 and w >= 2 and w - 2 >= WOWN:
                    for qi in range(3):
                        nsp = len(shares[qi]) * 16 * (w - 2 - WOWN + 1)
                        v.wait_ge(st3[qi].h, bst3[qi] + nsp)
                if s == 0:
                    hprev = z_sb[:]
                elif wl == 0:
                    hprev = h_sb[:, (1 - wpar) * HWC + (HW - 1) * SC_F:
                                    (1 - wpar) * HWC + HW * SC_F]
                else:
                    hprev = h_sb[:, wpar * HWC + (wl - 1) * SC_F:wpar * HWC + wl * SC_F]
                uap = u_sb[:, par * BLKC + sl * SC_F:par * BLKC + (sl + 1) * SC_F]
                gap = g_sb[:, par * BLKC + sl * SC_F:par * BLKC + (sl + 1) * SC_F]
                hap = h_sb[:, wpar * HWC + wl * SC_F:wpar * HWC + (wl + 1) * SC_F]
                for hf in range(2):
                    c0, c1 = hf * H, (hf + 1) * H
                    if s >= 1:
                        v.wait_ge(cs.h, bcs + 2 * (s - 1) + hf + 1)
                    v.tensor_mul(v_sb[:, c0:c1], a_sb[:, c0:c1], hprev[:, c0:c1])
                    vs.inc(v.tensor_add(v_sb[:, c0:c1], v_sb[:, c0:c1], uap[:, c0:c1]))
                for hf in range(2):
                    c0, c1 = hf * H, (hf + 1) * H
                    v.wait_ge(cs.h, bcs + 2 * s + hf + 1)
                    v.tensor_sub(d_sb[:, c0:c1], hprev[:, c0:c1], c_sb[:, c0:c1])
                    v.tensor_mul(d_sb[:, c0:c1], gap[:, c0:c1], d_sb[:, c0:c1])
                    vs.inc(v.tensor_add(hap[:, c0:c1], d_sb[:, c0:c1], c_sb[:, c0:c1]))

        ph.done = [(st3[i], st3[i].n) for i in range(3)]
    return ph


def ph_shift(nc, es, pool, deps, xn_d, mur_d, muk_d, xr_d, xk_d):
    ph = Ph(nc, pool, deps)
    with ExitStack() as ees:
        E = ees.enter_context
        xc_sb = E(nc.sbuf_tensor("shx", [128, 2 * RB], BF16))
        d_sb = E(nc.sbuf_tensor("shd", [128, ROWS], F32))
        o_sb = E(nc.sbuf_tensor("sho", [128, 2 * 2 * ROWS], BF16))
        mu_sb = E(nc.sbuf_tensor("shmu", [128, 16], F32))
        cld = ph.sem(); tld = [ph.sem(), ph.sem()]
        dn = ph.sem(); stp = [ph.sem(), ph.sem()]
        bcld = cld.n; btld = [x.n for x in tld]; bdn = dn.n; bstp = [x.n for x in stp]
        block = E(nc.Block())

        @block.sync
        def _(sy):
            ph.entry(sy)
            cld.inc(sy.dma_start(out=mu_sb[:, 0:8], in_=mur_d), 16)
            cld.inc(sy.dma_start(out=mu_sb[:, 8:16], in_=muk_d), 16)
            for i in range(64):
                c, b = i // 8, i % 8
                par = i % 2
                if i >= 2:
                    sy.wait_ge(dn.h, bdn + i - 1)
                tld[par].inc(sy.dma_start_transpose(
                    xc_sb[:, par * RB:(par + 1) * RB],
                    xn_d[b * RB:(b + 1) * RB, c * 128:(c + 1) * 128]), 16)

        @block.vector
        def _(v):
            ph.entry(v)
            v.wait_ge(cld.h, bcld + 32)
            for i in range(64):
                c, b = i // 8, i % 8
                par = i % 2
                v.wait_ge(tld[par].h, btld[par] + 16 * (i // 2 + 1))
                if i >= 2:
                    v.wait_ge(stp[par].h, bstp[par] + 32 * (i // 2))
                xo = xc_sb[:, par * RB + PRE:par * RB + RB]
                xp = xc_sb[:, par * RB + PRE - 1:par * RB + RB - 1]
                v.tensor_sub(d_sb[:], xo, xp)
                v.scalar_tensor_tensor(o_sb[:, par * 2 * ROWS:par * 2 * ROWS + ROWS],
                                       d_sb[:], mu_sb[:, c:c + 1], xp,
                                       ALU.mult, ALU.add)
                dn.inc(v.scalar_tensor_tensor(
                    o_sb[:, par * 2 * ROWS + ROWS:par * 2 * ROWS + 2 * ROWS],
                    d_sb[:], mu_sb[:, 8 + c:8 + c + 1], xp, ALU.mult, ALU.add))

        @block.scalar
        def _(s):
            ph.entry(s)
            for i in range(64):
                c, b = i // 8, i % 8
                par = i % 2
                s.wait_ge(dn.h, bdn + i + 1)
                stp[par].inc(s.dma_start(out=xr_d[c * 128:(c + 1) * 128,
                                                  b * ROWS:(b + 1) * ROWS],
                                         in_=o_sb[:, par * 2 * ROWS:par * 2 * ROWS + ROWS]),
                             16)
                stp[par].inc(s.dma_start(out=xk_d[c * 128:(c + 1) * 128,
                                                  b * ROWS:(b + 1) * ROWS],
                                         in_=o_sb[:, par * 2 * ROWS + ROWS:
                                                     par * 2 * ROWS + 2 * ROWS]), 16)

        ph.done = [(stp[0], stp[0].n), (stp[1], stp[1].n)]
    return ph

def ph_au_states(nc, es, pool, deps, xln_d, win1n_d, binc_d, wau_d, oh_d,
                 agin_d, agout_d, states_d):
    ph = Ph(nc, pool, deps)
    KC = 8
    TG = TOWN // 512
    with ExitStack() as ees:
        E = ees.enter_context
        wn_sb = E(nc.sbuf_tensor("auwn", [128, 16 * 1024], BF16))
        wau_sb = E(nc.sbuf_tensor("auwu", [128, 16 * 64], BF16))
        bin_sb = E(nc.sbuf_tensor("aubi", [128, 16], BF16))
        wab_sb = E(nc.sbuf_tensor("auwb", [128, 8 * 64], BF16))
        ba_sb = E(nc.sbuf_tensor("auba", [64, 1], F32))
        a_sb = E(nc.sbuf_tensor("aua", [128, 2 * KC * 512], BF16))
        au_sb = E(nc.sbuf_tensor("auau", [64, TOWN], F32))
        al_sb = E(nc.sbuf_tensor("aual", [128, ROWS], F32))
        ul_sb = E(nc.sbuf_tensor("auul", [128, ROWS], F32))
        ones_sb = E(nc.sbuf_tensor("auon", [128, ROWS], F32))
        sl_sb = E(nc.sbuf_tensor("ausl", [128, ROWS], F32))
        cl_sb = E(nc.sbuf_tensor("aucl", [128, ROWS], F32))
        agi_sb = E(nc.sbuf_tensor("auag", [128, 2], F32))
        aa_sb = E(nc.sbuf_tensor("auaa", [128, 8], F32))
        uu_sb = E(nc.sbuf_tensor("auuu", [128, 8], F32))
        ss_sb = E(nc.sbuf_tensor("auss", [128, 8], F32))
        oh_sb = E(nc.sbuf_tensor("auoh", [128, 8], F32))
        si_sb = E(nc.sbuf_tensor("ausi", [128, 1], F32))
        st_out = E(nc.sbuf_tensor("aust", [128, ROWS], F32))
        psw = E(nc.psum_tensor("aupw", [128, 2 * 512], F32))
        psb = E(nc.psum_tensor("aupb", [64, 512], F32))
        ps = E(nc.psum_tensor("aups", [64, 2 * 512], F32))
        wld = ph.sem(); ald = [ph.sem(), ph.sem()]
        pmm = ph.sem(); pev = ph.sem()
        mmd = ph.sem(); sev = ph.sem(); locd = ph.sem(); vv = ph.sem()
        cc = ph.sem(); agl = ph.sem(); std = ph.sem()
        bwld = wld.n; bald = [x.n for x in ald]; bpmm = pmm.n; bpev = pev.n
        bmmd = mmd.n; bsev = sev.n
        blocd = locd.n; bvv = vv.n; bcc = cc.n; bagl = agl.n; bstd = std.n
        block = E(nc.Block())

        @block.sync
        def _(sy):
            ph.entry(sy)
            for gi in range(TG):
                par = gi % 2
                if gi >= 2:
                    sy.wait_ge(mmd.h, bmmd + gi - 1)
                for k in range(KC):
                    ald[par].inc(sy.dma_start_transpose(
                        a_sb[:, par * KC * 512 + k * 512:par * KC * 512 + (k + 1) * 512],
                        xln_d[gi * 512:(gi + 1) * 512, k * 128:(k + 1) * 128]), 16)

        @block.tensor
        def _(t):
            ph.entry(t)
            t.wait_ge(wld.h, bwld + 16 * 4)
            for m in range(8):
                bank = m % 2
                if m >= 2:
                    t.wait_ge(pev.h, bpev + m - 1)
                for k in range(16):
                    ins = t.matmul(psw[:, bank * 512:bank * 512 + 64],
                                   wn_sb[:, k * 1024 + m * 128:k * 1024 + (m + 1) * 128],
                                   wau_sb[:, k * 64:(k + 1) * 64],
                                   start=(k == 0), stop=(k == 15))
                pmm.inc(ins)
            for k in range(16):
                ins = t.matmul(psb[:, 0:1], wau_sb[:, k * 64:(k + 1) * 64],
                               bin_sb[:, k:k + 1], start=(k == 0), stop=(k == 15))
            pmm.inc(ins)
            t.wait_ge(pev.h, bpev + 9)
            for gi in range(TG):
                par = gi % 2
                t.wait_ge(ald[par].h, bald[par] + 16 * KC * (gi // 2 + 1))
                bank = gi % 2
                if gi >= 2:
                    t.wait_ge(sev.h, bsev + gi - 1)
                for k in range(KC):
                    ins = t.matmul(ps[:, bank * 512:(bank + 1) * 512],
                                   wab_sb[:, k * 64:(k + 1) * 64],
                                   a_sb[:, par * KC * 512 + k * 512:
                                           par * KC * 512 + (k + 1) * 512],
                                   start=(k == 0), stop=(k == KC - 1))
                mmd.inc(ins)

        @block.scalar
        def _(s):
            ph.entry(s)
            for gi in range(TG):
                bank = gi % 2
                s.wait_ge(mmd.h, bmmd + gi + 1)
                s.activation(au_sb[0:16, gi * 512:(gi + 1) * 512],
                             ps[0:16, bank * 512:(bank + 1) * 512], AF.Sigmoid,
                             bias=ba_sb[0:16, 0:1])
                sev.inc(s.activation(au_sb[32:48, gi * 512:(gi + 1) * 512],
                                     ps[32:48, bank * 512:(bank + 1) * 512], AF.Identity,
                                     bias=ba_sb[32:48, 0:1]))

        @block.gpsimd
        def _(g):
            ph.entry(g)
            wld.inc(g.dma_start(out=wn_sb[:].rearrange("p (k d) -> p k d", k=16),
                                in_=win1n_d.rearrange("(k p) d -> p k d", p=128)), 16)
            wld.inc(g.dma_start(out=wau_sb[:].rearrange("p (k n) -> p k n", k=16),
                                in_=wau_d.rearrange("(k p) n -> p k n", p=128)), 16)
            wld.inc(g.dma_start(out=bin_sb[:], in_=binc_d), 16)
            wld.inc(g.dma_start(out=oh_sb[:], in_=oh_d), 16)
            g.wait_ge(sev.h, bsev + TG)
            for b in range(8):
                locd.inc(g.dma_start(out=al_sb[b * 16:(b + 1) * 16, :],
                                     in_=au_sb[0:16, b * ROWS:(b + 1) * ROWS]), 16)
                locd.inc(g.dma_start(out=ul_sb[b * 16:(b + 1) * 16, :],
                                     in_=au_sb[32:48, b * ROWS:(b + 1) * ROWS]), 16)
            g.wait_ge(vv.h, bvv + 2)
            agl.inc(g.dma_start(out=agin_d, in_=agi_sb[:]), 16)
            g.wait_ge(agl.h, bagl + 16)
            cc.inc(g.collective_compute(
                "AllGather", ALU.bypass, replica_groups=[list(range(NCORES))],
                ins=[agin_d.opt()], outs=[agout_d.opt()]))
            g.wait_ge(cc.h, bcc + 1)
            for j in range(8):
                agl.inc(g.dma_start(out=aa_sb[:, j:j + 1],
                                    in_=agout_d[j * 128:(j + 1) * 128, 0:1],
                                    allow_slow_non_contiguous=True), 16)
                agl.inc(g.dma_start(out=uu_sb[:, j:j + 1],
                                    in_=agout_d[j * 128:(j + 1) * 128, 1:2],
                                    allow_slow_non_contiguous=True), 16)
            g.wait_ge(vv.h, bvv + 6)
            std.inc(g.dma_start(out=states_d, in_=st_out[:]), 16)
            g.wait_ge(std.h, bstd + 16)

        @block.vector
        def _(v):
            ph.entry(v)
            v.memset(ones_sb[:], 1.0)
            for m in range(8):
                bank = m % 2
                v.wait_ge(pmm.h, bpmm + m + 1)
                pev.inc(v.tensor_copy(wab_sb[:, m * 64:(m + 1) * 64],
                                      psw[:, bank * 512:bank * 512 + 64]))
            v.wait_ge(pmm.h, bpmm + 9)
            pev.inc(v.tensor_copy(ba_sb[:], psb[:, 0:1]))
            v.wait_ge(locd.h, blocd + 16 * 16)
            v.tensor_tensor_scan(sl_sb[:], al_sb[:], ul_sb[:], 0.0, ALU.mult, ALU.add)
            vv.inc(v.tensor_tensor_scan(cl_sb[:], al_sb[:], ones_sb[:], 1.0,
                                        ALU.mult, ALU.mult))
            v.wait_ge(vv.h, bvv + 1)
            v.tensor_copy(agi_sb[:, 0:1], cl_sb[:, ROWS - 1:ROWS])
            vv.inc(v.tensor_copy(agi_sb[:, 1:2], sl_sb[:, ROWS - 1:ROWS]))
            v.wait_ge(agl.h, bagl + 16 * 17)
            vv.inc(v.tensor_tensor_scan(ss_sb[:], aa_sb[:], uu_sb[:], 0.0,
                                        ALU.mult, ALU.add))
            v.wait_ge(vv.h, bvv + 3)
            vv.inc(v.tensor_mul(ss_sb[:], ss_sb[:], oh_sb[:]))
            v.wait_ge(vv.h, bvv + 4)
            vv.inc(v.reduce_sum(si_sb[:], ss_sb[:], axis=AX.X))
            v.wait_ge(vv.h, bvv + 5)
            vv.inc(v.scalar_tensor_tensor(st_out[:], cl_sb[:], si_sb[:, 0:1],
                                          sl_sb[:], ALU.mult, ALU.add))

        ph.done = [(std, std.n)]
    return ph
def ph_s1p(nc, es, pool, deps, states_d, cssm_d, woT_d, bout_d, xn1_d, s1p_d):
    ph = Ph(nc, pool, deps)
    with ExitStack() as ees:
        E = ees.enter_context
        c_sb = E(nc.sbuf_tensor("s1c", [128, 16 * 16], BF16))
        wo_sb = E(nc.sbuf_tensor("s1wo", [128, 16 * D], BF16))
        gf_sb = E(nc.sbuf_tensor("s1gf", [16, D], F32))
        gb_sb = E(nc.sbuf_tensor("s1gb", [16, D], BF16))
        sts_sb = E(nc.sbuf_tensor("s1st", [16, TOWN], BF16))
        bo_sb = E(nc.sbuf_tensor("s1bo", [128, D], F32))
        e_sb = E(nc.sbuf_tensor("s1e", [128, 2 * D], BF16))
        t_sb = E(nc.sbuf_tensor("s1t", [128, 2 * 512], F32))
        o_sb = E(nc.sbuf_tensor("s1o", [128, 2 * D], BF16))
        psg = E(nc.psum_tensor("s1pg", [16, 1024], F32))
        ps = E(nc.psum_tensor("s1ps", [128, 4 * 512], F32))
        wld = ph.sem(); eld = [ph.sem(), ph.sem()]; gdn = ph.sem()
        mmd = ph.sem(); vev = ph.sem(); stp = [ph.sem(), ph.sem()]
        bwld = wld.n; beld = [x.n for x in eld]; bgdn = gdn.n
        bmmd = mmd.n; bvev = vev.n; bstp = [x.n for x in stp]
        block = E(nc.Block())

        @block.gpsimd
        def _(g):
            ph.entry(g)
            for k in range(16):
                wld.inc(g.dma_start(out=c_sb[:, k * 16:(k + 1) * 16],
                                    in_=cssm_d[k * 128:(k + 1) * 128, :]), 16)
            for k in range(16):
                wld.inc(g.dma_start(out=wo_sb[:, k * D:(k + 1) * D],
                                    in_=woT_d[k * 128:(k + 1) * 128, :]), 16)
            wld.inc(g.dma_start(out=bo_sb[:], in_=bout_d), 16)
            wld.inc(g.dma_start(out=sts_sb[:].rearrange("p (b r) -> p b r", b=8),
                                in_=states_d.rearrange("(b n) r -> n b r", n=16)), 16)
            for mt in range(TOWN // 128):
                par = mt % 2
                g.wait_ge(vev.h, bvev + mt + 1)
                stp[par].inc(g.dma_start(out=s1p_d[mt * 128:(mt + 1) * 128, :],
                                         in_=o_sb[:, par * D:(par + 1) * D]), 16)

        @block.sync
        def _(sy):
            ph.entry(sy)
            for mt in range(TOWN // 128):
                par = mt % 2
                if mt >= 2:
                    sy.wait_ge(vev.h, bvev + mt - 1)
                eld[par].inc(sy.dma_start(out=e_sb[:, par * D:(par + 1) * D],
                                          in_=xn1_d[mt * 128:(mt + 1) * 128, :]), 16)

        @block.tensor
        def _(t):
            ph.entry(t)
            t.wait_ge(wld.h, bwld + 16 * 34)
            for k in range(16):
                ins = t.matmul(psg[:, 0:512], c_sb[:, k * 16:(k + 1) * 16],
                               wo_sb[:, k * D:k * D + 512], start=(k == 0), stop=(k == 15))
            for k in range(16):
                ins = t.matmul(psg[:, 512:1024], c_sb[:, k * 16:(k + 1) * 16],
                               wo_sb[:, k * D + 512:(k + 1) * D],
                               start=(k == 0), stop=(k == 15))
            mmd.inc(ins)
            t.wait_ge(gdn.h, bgdn + 1)
            for mt in range(TOWN // 128):
                for nt in range(2):
                    idx = mt * 2 + nt
                    bank = idx % 4
                    if idx >= 4:
                        t.wait_ge(vev.h, bvev + (idx - 3 + 1) // 2)
                    mmd.inc(t.matmul(ps[:, bank * 512:bank * 512 + 512],
                                     sts_sb[:, mt * 128:(mt + 1) * 128],
                                     gb_sb[:, nt * 512:(nt + 1) * 512],
                                     start=True, stop=True))

        @block.vector
        def _(v):
            ph.entry(v)
            v.wait_ge(mmd.h, bmmd + 1)
            v.tensor_copy(gf_sb[:], psg[:, 0:1024])
            gdn.inc(v.tensor_copy(gb_sb[:], gf_sb[:]))
            for mt in range(TOWN // 128):
                par = mt % 2
                v.wait_ge(eld[par].h, beld[par] + 16 * (mt // 2 + 1))
                if mt >= 2:
                    v.wait_ge(stp[par].h, bstp[par] + 16 * (mt // 2))
                for nt in range(2):
                    idx = mt * 2 + nt
                    bank = idx % 4
                    v.wait_ge(mmd.h, bmmd + idx + 2)
                    v.tensor_tensor(t_sb[:, nt * 512:(nt + 1) * 512],
                                    ps[:, bank * 512:bank * 512 + 512],
                                    bo_sb[:, nt * 512:(nt + 1) * 512], ALU.add)
                ins = v.tensor_tensor(o_sb[:, par * D:(par + 1) * D], t_sb[:, 0:1024],
                                      e_sb[:, par * D:(par + 1) * D], ALU.add)
                vev.inc(ins)

        ph.done = [(stp[0], stp[0].n), (stp[1], stp[1].n)]
    return ph


def ph_mma(nc, es, pool, deps, name, K, Mout, src_d, w_d, out_d):
    ph = Ph(nc, pool, deps)
    KC = K // 128
    MC = Mout // 128
    TG = TOWN // 512
    with ExitStack() as ees:
        E = ees.enter_context
        w_sb = E(nc.sbuf_tensor(name + "w", [128, KC * Mout], BF16))
        a_sb = E(nc.sbuf_tensor(name + "a", [128, 2 * KC * 512], BF16))
        o_sb = E(nc.sbuf_tensor(name + "o", [128, 2 * 512], BF16))
        ps = E(nc.psum_tensor(name + "ps", [128, 4 * 512], F32))
        wld = ph.sem(); ald = [ph.sem(), ph.sem()]; ald2 = [ph.sem(), ph.sem()]
        mmd = ph.sem(); vev = ph.sem(); stp = [ph.sem(), ph.sem()]
        bwld = wld.n; bald = [x.n for x in ald]; bald2 = [x.n for x in ald2]
        bmmd = mmd.n; bvev = vev.n
        bstp = [x.n for x in stp]
        block = E(nc.Block())

        @block.gpsimd
        def _(g):
            ph.entry(g)
            for k in range(KC):
                wld.inc(g.dma_start(out=w_sb[:, k * Mout:(k + 1) * Mout],
                                    in_=w_d[k * 128:(k + 1) * 128, :]), 16)
            for gi in range(TG):
                for m in range(MC):
                    idx = gi * MC + m
                    par = idx % 2
                    g.wait_ge(vev.h, bvev + idx + 1)
                    stp[par].inc(g.dma_start(
                        out=out_d[m * 128:(m + 1) * 128, gi * 512:(gi + 1) * 512],
                        in_=o_sb[:, par * 512:(par + 1) * 512]), 16)

        def emit_t2(eng, gi):
            par = gi % 2
            if gi >= 2:
                eng.wait_ge(mmd.h, bmmd + (gi - 1) * MC)
            for k in range(KC):
                ald[par].inc(eng.dma_start_transpose(
                    a_sb[:, par * KC * 512 + k * 512:par * KC * 512 + (k + 1) * 512],
                    src_d[gi * 512:(gi + 1) * 512, k * 128:(k + 1) * 128]), 16)

        @block.sync
        def _(sy):
            ph.entry(sy)
            for gi in range(TG):
                emit_t2(sy, gi)

        @block.tensor
        def _(t):
            ph.entry(t)
            t.wait_ge(wld.h, bwld + 16 * KC)
            for gi in range(TG):
                par = gi % 2
                t.wait_ge(ald[par].h, bald[par] + 16 * KC * (gi // 2 + 1))
                for m in range(MC):
                    idx = gi * MC + m
                    bank = idx % 4
                    if idx >= 4:
                        t.wait_ge(vev.h, bvev + idx - 3)
                    for k in range(KC):
                        ins = t.matmul(ps[:, bank * 512:(bank + 1) * 512],
                                       w_sb[:, k * Mout + m * 128:k * Mout + (m + 1) * 128],
                                       a_sb[:, par * KC * 512 + k * 512:
                                               par * KC * 512 + (k + 1) * 512],
                                       start=(k == 0), stop=(k == KC - 1))
                    mmd.inc(ins)

        @block.vector
        def _(v):
            ph.entry(v)
            for gi in range(TG):
                for m in range(MC):
                    idx = gi * MC + m
                    par = idx % 2
                    bank = idx % 4
                    v.wait_ge(mmd.h, bmmd + idx + 1)
                    if idx >= 2:
                        v.wait_ge(stp[par].h, bstp[par] + 16 * (idx // 2))
                    vev.inc(v.tensor_copy(o_sb[:, par * 512:(par + 1) * 512],
                                          ps[:, bank * 512:(bank + 1) * 512]))

        ph.done = [(stp[0], stp[0].n), (stp[1], stp[1].n)]
    return ph

def ph_ffn1loop(nc, es, pool, deps, xw_d, u3_d, b3_d, c0_d, id_d, cf_d, RC=512):
    ph = Ph(nc, pool, deps)
    NPASS = 1024 // RC
    MC = D // 128
    NT = RC
    NB = B

    def mm_after(step, m, gi):
        return step * 3 * MC + m * 3 + gi + 1

    def upd_after(p, t, m):
        return p * (1 + NB * MC) + 1 + t * MC + m + 1

    def upd_init(p):
        return p * (1 + NB * MC) + 1

    with ExitStack() as ees:
        E = ees.enter_context
        u_sb = E(nc.sbuf_tensor("f1u", [128, MC * 3 * D], BF16))
        id_sb = E(nc.sbuf_tensor("f1id", [128, 128], BF16))
        b_sb = E(nc.sbuf_tensor("f1b", [128, 24], F32))
        c_sb = E(nc.sbuf_tensor("f1c", [128, MC * RC], F32))
        cb_sb = E(nc.sbuf_tensor("f1cb", [128, 2 * MC * RC], BF16))
        xw_sb = E(nc.sbuf_tensor("f1xw", [128, 2 * 3 * MC * RC], BF16))
        ga_sb = E(nc.sbuf_tensor("f1ga", [128, 2 * 3 * NT], F32))
        ps = E(nc.psum_tensor("f1ps", [128, 8 * 512], F32))
        wld = ph.sem(); xld = [ph.sem(), ph.sem()]
        mmd = ph.sem(); gev = ph.sem(); upd = ph.sem(); stq = ph.sem()
        bwld = wld.n; bxld = [x.n for x in xld]; bmmd = mmd.n
        bgev = gev.n; bupd = upd.n; bstq = stq.n
        block = E(nc.Block())

        @block.gpsimd
        def _(g):
            ph.entry(g)
            for k in range(MC):
                wld.inc(g.dma_start(out=u_sb[:, k * 3 * D:(k + 1) * 3 * D],
                                    in_=u3_d[k * 128:(k + 1) * 128, :]), 16)
            wld.inc(g.dma_start(out=id_sb[:], in_=id_d), 16)
            wld.inc(g.dma_start(out=b_sb[:], in_=b3_d), 16)
            for p in range(NPASS):
                if p >= 1:
                    g.wait_ge(stq.h, bstq + 16 * p)
                wld.inc(g.dma_start(out=c_sb[:].rearrange("p (m r) -> p m r", m=MC),
                                    in_=c0_d.rearrange("(m p) r -> p m r", p=128)
                                    [:, :, p * RC:(p + 1) * RC]), 16)
                for t in range(NB):
                    step = p * NB + t
                    if step >= 2:
                        g.wait_ge(gev.h, bgev + (step - 1) * 3 * MC)
                    for gi in range(3):
                        xld[t % 2].inc(g.dma_start(
                            out=xw_sb[:, ((t % 2) * 3 + gi) * MC * RC:
                                         ((t % 2) * 3 + gi + 1) * MC * RC]
                                .rearrange("p (m r) -> p m r", m=MC),
                            in_=xw_d.rearrange("(m p) t -> p m t", p=128)
                                [:, gi * MC:(gi + 1) * MC,
                                 t * 1024 + p * RC:t * 1024 + (p + 1) * RC]), 16)
                g.wait_ge(upd.h, bupd + upd_after(p, NB - 1, MC - 1))
                stq.inc(g.dma_start(out=cf_d.rearrange("(m p) r -> p m r", p=128)
                                    [:, :, p * RC:(p + 1) * RC],
                                    in_=c_sb[:].rearrange("p (m r) -> p m r", m=MC)), 16)
            g.wait_ge(stq.h, bstq + 16 * NPASS)

        @block.tensor
        def _(t_):
            ph.entry(t_)
            t_.wait_ge(wld.h, bwld + 16 * (MC + 2))
            for p in range(NPASS):
                t_.wait_ge(wld.h, bwld + 16 * (MC + 2 + p + 1))
                for t in range(NB):
                    step = p * NB + t
                    t_.wait_ge(xld[t % 2].h, bxld[t % 2] + 48 * (step // 2 + 1))
                    t_.wait_ge(upd.h, bupd + (upd_after(p, t - 1, MC - 1) if t > 0
                                              else upd_init(p)))
                    for m in range(MC):
                        for gi in range(3):
                            bidx = gi * 2 + (m % 2)
                            if m >= 2:
                                t_.wait_ge(gev.h, bgev + mm_after(step, m - 2, gi))
                            elif step >= 1:
                                t_.wait_ge(gev.h, bgev + mm_after(step - 1, MC - 2 + m, gi))
                            t_.matmul(ps[:, bidx * 512:bidx * 512 + NT], id_sb[:],
                                      xw_sb[:, ((t % 2) * 3 + gi) * MC * RC + m * RC:
                                               ((t % 2) * 3 + gi) * MC * RC + m * RC + NT],
                                      start=True, stop=False)
                            for k in range(MC):
                                ins = t_.matmul(
                                    ps[:, bidx * 512:bidx * 512 + NT],
                                    u_sb[:, k * 3 * D + gi * D + m * 128:
                                            k * 3 * D + gi * D + (m + 1) * 128],
                                    cb_sb[:, ((step - 1) % 2) * MC * RC + k * RC:
                                             ((step - 1) % 2) * MC * RC + k * RC + NT],
                                    start=False, stop=(k == MC - 1))
                            mmd.inc(ins)

        @block.scalar
        def _(s):
            ph.entry(s)
            AFS = [AF.Sigmoid, AF.Sigmoid, AF.Tanh]
            for p in range(NPASS):
                for t in range(NB):
                    step = p * NB + t
                    for m in range(MC):
                        for gi in range(3):
                            s.wait_ge(mmd.h, bmmd + mm_after(step, m, gi))
                            if m >= 1:
                                s.wait_ge(upd.h, bupd + upd_after(p, t, m - 1))
                            elif step >= 1:
                                pp, tt = (p, t - 1) if t > 0 else (p - 1, NB - 1)
                                s.wait_ge(upd.h, bupd + upd_after(pp, tt, MC - 1))
                            bidx = gi * 2 + (m % 2)
                            gev.inc(s.activation(
                                ga_sb[:, ((step % 2) * 3 + gi) * NT:
                                         ((step % 2) * 3 + gi) * NT + NT],
                                ps[:, bidx * 512:bidx * 512 + NT],
                                AFS[gi], bias=b_sb[:, gi * MC + m:gi * MC + m + 1]))

        @block.vector
        def _(v):
            ph.entry(v)
            for p in range(NPASS):
                v.wait_ge(wld.h, bwld + 16 * (MC + 2 + p + 1))
                for k in range(MC):
                    ins = v.tensor_copy(cb_sb[:, MC * RC + k * RC:MC * RC + (k + 1) * RC],
                                        c_sb[:, k * RC:(k + 1) * RC])
                upd.inc(ins)
                for t in range(NB):
                    step = p * NB + t
                    for m in range(MC):
                        v.wait_ge(gev.h, bgev + step * 3 * MC + m * 3 + 3)
                        sp = (step % 2) * 3
                        f_ap = ga_sb[:, (sp + 0) * NT:(sp + 0) * NT + NT]
                        i_ap = ga_sb[:, (sp + 1) * NT:(sp + 1) * NT + NT]
                        g_ap = ga_sb[:, (sp + 2) * NT:(sp + 2) * NT + NT]
                        c_ap = c_sb[:, m * RC:m * RC + NT]
                        cb_ap = cb_sb[:, (step % 2) * MC * RC + m * RC:
                                         (step % 2) * MC * RC + m * RC + NT]
                        v.tensor_mul(c_ap, f_ap, c_ap)
                        v.tensor_mul(i_ap, i_ap, g_ap)
                        v.tensor_add(c_ap, c_ap, i_ap)
                        upd.inc(v.tensor_copy(cb_ap, c_ap))

        ph.done = [(stq, stq.n)]
    return ph


def ph_final(nc, es, pool, deps, cf_d, uoT_d, bo_d, xwo_d, s1p_d, x1_d, x2_d):
    ph = Ph(nc, pool, deps)
    MC = 8
    with ExitStack() as ees:
        E = ees.enter_context
        uo_sb = E(nc.sbuf_tensor("fno", [128, MC * D], BF16))
        cb_sb = E(nc.sbuf_tensor("fncb", [128, MC * ROWS], BF16))
        bo_sb = E(nc.sbuf_tensor("fnbo", [128, D], F32))
        cuo_sb = E(nc.sbuf_tensor("fncuo", [128, 8 * D], F32))
        xw_sb = E(nc.sbuf_tensor("fnxw", [128, 2 * D], BF16))
        s1_sb = E(nc.sbuf_tensor("fns1", [128, 2 * D], BF16))
        x1_sb = E(nc.sbuf_tensor("fnx1", [128, 2 * D], F32))
        t_sb = E(nc.sbuf_tensor("fnt", [128, 2 * D], F32))
        t2_sb = E(nc.sbuf_tensor("fnt2", [128, 2 * D], F32))
        o_sb = E(nc.sbuf_tensor("fnoo", [128, 2 * D], F32))
        ps = E(nc.psum_tensor("fnps", [128, 4 * 512], F32))
        wld = ph.sem(); eld = [ph.sem(), ph.sem()]
        mmd = ph.sem(); cud = ph.sem(); vad = ph.sem(); tad = ph.sem()
        oad = ph.sem(); stp = [ph.sem(), ph.sem()]
        bwld = wld.n; beld = [x.n for x in eld]; bmmd = mmd.n; bcud = cud.n
        bvad = vad.n; btad = tad.n; boad = oad.n; bstp = [x.n for x in stp]
        block = E(nc.Block())

        @block.gpsimd
        def _(g):
            ph.entry(g)
            for k in range(MC):
                wld.inc(g.dma_start(out=uo_sb[:, k * D:(k + 1) * D],
                                    in_=uoT_d[k * 128:(k + 1) * 128, :]), 16)
            wld.inc(g.dma_start(out=cb_sb[:].rearrange("p (m r) -> p m r", m=MC),
                                in_=cf_d.rearrange("(m p) r -> p m r", p=128)), 16)
            wld.inc(g.dma_start(out=bo_sb[:], in_=bo_d), 16)
            for mt in range(TOWN // 128):
                par = mt % 2
                g.wait_ge(oad.h, boad + mt + 1)
                stp[par].inc(g.dma_start(out=x2_d[mt * 128:(mt + 1) * 128, :],
                                         in_=o_sb[:, par * D:(par + 1) * D]), 16)
            g.wait_ge(stp[0].h, bstp[0] + 16 * 32)
            g.wait_ge(stp[1].h, bstp[1] + 16 * 32)

        @block.sync
        def _(sy):
            ph.entry(sy)
            for mt in range(TOWN // 128):
                par = mt % 2
                if mt >= 2:
                    sy.wait_ge(oad.h, boad + mt - 1)
                eld[par].inc(sy.dma_start(out=xw_sb[:, par * D:(par + 1) * D],
                                          in_=xwo_d[mt * 128:(mt + 1) * 128, :]), 16)
                eld[par].inc(sy.dma_start(out=s1_sb[:, par * D:(par + 1) * D],
                                          in_=s1p_d[mt * 128:(mt + 1) * 128, :]), 16)
                eld[par].inc(sy.dma_start(out=x1_sb[:, par * D:(par + 1) * D],
                                          in_=x1_d[mt * 128:(mt + 1) * 128, :]), 16)

        @block.tensor
        def _(t):
            ph.entry(t)
            t.wait_ge(wld.h, bwld + 16 * (MC + 2))
            for rc in range(8):
                for nt in range(2):
                    idx = rc * 2 + nt
                    bank = idx % 4
                    if idx >= 4:
                        t.wait_ge(cud.h, bcud + idx - 3)
                    for k in range(MC):
                        ins = t.matmul(ps[:, bank * 512:(bank + 1) * 512],
                                       cb_sb[:, k * ROWS + rc * 128:k * ROWS + rc * 128 + 128],
                                       uo_sb[:, k * D + nt * 512:k * D + (nt + 1) * 512],
                                       start=(k == 0), stop=(k == MC - 1))
                    mmd.inc(ins)

        @block.vector
        def _(v):
            ph.entry(v)
            for rc in range(8):
                for nt in range(2):
                    idx = rc * 2 + nt
                    bank = idx % 4
                    v.wait_ge(mmd.h, bmmd + idx + 1)
                    cud.inc(v.tensor_copy(cuo_sb[:, rc * D + nt * 512:rc * D + (nt + 1) * 512],
                                          ps[:, bank * 512:(bank + 1) * 512]))
            for mt in range(TOWN // 128):
                par = mt % 2
                rc = mt % 8
                v.wait_ge(eld[par].h, beld[par] + 48 * (mt // 2 + 1))
                if mt >= 2:
                    v.wait_ge(stp[par].h, bstp[par] + 16 * (mt // 2))
                if mt >= 2:
                    v.wait_ge(tad.h, btad + mt - 1)
                v.tensor_tensor(t_sb[:, par * D:(par + 1) * D],
                                xw_sb[:, par * D:(par + 1) * D],
                                cuo_sb[:, rc * D:(rc + 1) * D], ALU.add)
                vad.inc(v.tensor_add(t_sb[:, par * D:(par + 1) * D],
                                     t_sb[:, par * D:(par + 1) * D], bo_sb[:]))
                v.wait_ge(tad.h, btad + mt + 1)
                v.tensor_add(o_sb[:, par * D:(par + 1) * D],
                             t2_sb[:, par * D:(par + 1) * D],
                             x1_sb[:, par * D:(par + 1) * D])
                oad.inc(v.tensor_add(o_sb[:, par * D:(par + 1) * D],
                                     o_sb[:, par * D:(par + 1) * D],
                                     s1_sb[:, par * D:(par + 1) * D]))

        @block.scalar
        def _(s):
            ph.entry(s)
            for mt in range(TOWN // 128):
                par = mt % 2
                s.wait_ge(vad.h, bvad + mt + 1)
                if mt >= 2:
                    s.wait_ge(oad.h, boad + mt - 1)
                tad.inc(s.activation(t2_sb[:, par * D:(par + 1) * D],
                                     t_sb[:, par * D:(par + 1) * D], AF.Tanh))

        ph.done = [(stp[0], stp[0].n), (stp[1], stp[1].n)]
    return ph

def build_fused(dbg=()):
    nc = bass.Bass("TRN2", target_bir_lowering=False, debug=False)

    def dram(nm, shape, dt):
        kind = "ExternalOutput" if nm in dbg else "Internal"
        return nc.dram_tensor(nm, shape, dt, kind=kind).ap()

    def inp(nm, shape, dt=F32):
        return nc.dram_tensor(nm, shape, dt, kind="ExternalInput").ap()

    x_d = inp("x", [TOK, D])
    ln0g_d = inp("ln0g", [128, D]); ln0b_d = inp("ln0b", [128, D])
    win0_d = inp("win0", [D, 2 * D]); bin0_d = inp("bin0", [128, 2 * D])
    mask_d = inp("mask", [128, 1]); asc_d = inp("asc", [128, SC_F])
    wout0_d = inp("wout0", [D, D]); bout0_d = inp("bout0", [128, D])
    mur_d = inp("mur", [128, 8]); muk_d = inp("muk", [128, 8])
    wrT_d = inp("wrT", [D, D]); wkT_d = inp("wkT", [D, D]); wvT_d = inp("wvT", [D, D])
    ln1g_d = inp("ln1g", [128, D]); ln1b_d = inp("ln1b", [128, D])
    win1n_d = inp("win1n", [2 * D, D]); binc_d = inp("binc", [128, 16])
    wau_d = inp("wau", [2 * D, 64]); oh_d = inp("oh", [128, 8])
    cssm_d = inp("cssm", [2 * D, 16]); woT1_d = inp("woT1", [2 * D, D])
    bout1_d = inp("bout1", [128, D])
    wficT_d = inp("wficT", [D, 3 * D]); wfoT_d = inp("wfoT", [D, D])
    u3_d = inp("u3", [D, 3 * D]); b3_d = inp("b3", [128, 24])
    c0_d = inp("c0", [D, ROWS]); id_d = inp("ident", [128, 128])
    uoT_d = inp("uoT", [D, D]); bo_d = inp("bo", [128, D])

    xn0_d = dram("xn0", [TOK, D], BF16)
    z_d = dram("z", [TOK, D], BF16)
    pu_d = dram("pu", [SC_STEPS * 128, SC_F], BF16)
    pg_d = dram("pg", [SC_STEPS * 128, SC_F], BF16)
    hs_d = dram("hs", [TOWN, D], BF16)
    attn0_d = dram("attn0", [TOWN, D], BF16)
    xr_d = dram("xr", [D, TOWN], BF16)
    xk_d = dram("xk", [D, TOWN], BF16)
    r_d = dram("r", [TOWN, D], BF16)
    kact_d = dram("kact", [TOWN, D], BF16)
    x1_d = dram("x1", [TOWN, D], F32)
    xn1_d = dram("xn1", [TOWN, D], BF16)
    xln_d = dram("xln", [TOWN, D], BF16)
    agin_d = dram("agin", [128, 2], F32)
    agout_d = dram("agout", [NCORES * 128, 2], F32)
    states_d = dram("states", [128, ROWS], F32)
    s1p_d = dram("s1p", [TOWN, D], BF16)
    xw_d = dram("xw", [3 * D, TOWN], BF16)
    xwo_d = dram("xwo", [TOWN, D], BF16)
    cf_d = dram("cf", [D, ROWS], F32)
    x2_d = nc.dram_tensor("x2", [TOWN, D], F32, kind="ExternalOutput").ap()

    es = ExitStack()
    with es:
        pool = Pool(nc, es, n=48)
        p2 = ph_lnug(nc, es, pool, [], x_d, ln0g_d, ln0b_d, xn0_d, z_d,
                     win0_d, bin0_d, mask_d, pu_d, pg_d)
        p3 = ph_scan0(nc, es, pool, p2.done, pu_d, pg_d, asc_d, hs_d)
        p4 = ph_mmb(nc, es, pool, p3.done, "at0", TOWN, D, D, ("t2", hs_d),
                    wout0_d, attn0_d, bias_d=bout0_d, acts=[(0, D, AF.Copy)])
        p5 = ph_shift(nc, es, pool, p4.done, xn0_d, mur_d, muk_d, xr_d, xk_d)
        p6 = ph_mmb(nc, es, pool, p5.done, "rr", TOWN, D, D, ("ch", xr_d),
                    wrT_d, r_d, acts=[(0, D, AF.Sigmoid)])
        p7 = ph_mmb(nc, es, pool, p6.done, "kk", TOWN, D, D, ("ch", xk_d),
                    wkT_d, kact_d, acts=[(0, D, AF.Relu)], relu_sq=True)

        xn0o = xn0_d.rearrange("(b rr) d -> b rr d", b=B)
        x_o = x_d.rearrange("(b rr) d -> b rr d", b=B)

        def rm_r(mt):
            return r_d[mt * 128:(mt + 1) * 128, :]

        def rm_at0(mt):
            return attn0_d[mt * 128:(mt + 1) * 128, :]

        def rm_xn0(mt):
            b, rc = mt // 8, mt % 8
            return xn0o[b, PRE + rc * 128:PRE + (rc + 1) * 128, :]

        def rm_x(mt):
            b, rc = mt // 8, mt % 8
            return x_o[b, PRE + rc * 128:PRE + (rc + 1) * 128, :]

        p8 = ph_mmb(nc, es, pool, p7.done, "x1", TOWN, D, D, ("t2", kact_d),
                    wvT_d, x1_d, out_dtype=F32,
                    extras=[(rm_r, ALU.mult), (rm_at0, ALU.add),
                            (rm_xn0, ALU.add), (rm_x, ALU.add)])
        p9 = ph_ln(nc, es, pool, p8.done, TOWN, x1_d, ln1g_d, ln1b_d, xn1_d, xln_d, False)
        p11 = ph_au_states(nc, es, pool, p9.done, xln_d, win1n_d, binc_d, wau_d,
                           oh_d, agin_d, agout_d, states_d)
        p13 = ph_s1p(nc, es, pool, p11.done, states_d, cssm_d, woT1_d, bout1_d,
                     xn1_d, s1p_d)
        p14 = ph_mma(nc, es, pool, p13.done, "xwf", D, 3 * D, xn1_d, wficT_d, xw_d)
        p15 = ph_mmb(nc, es, pool, p14.done, "xwo", TOWN, D, D, ("t2", xn1_d),
                     wfoT_d, xwo_d, acts=[(0, D, AF.Copy)])
        p16 = ph_ffn1loop(nc, es, pool, p15.done, xw_d, u3_d, b3_d, c0_d, id_d, cf_d)
        p17 = ph_final(nc, es, pool, p16.done, cf_d, uoT_d, bo_d, xwo_d, s1p_d,
                       x1_d, x2_d)
    return nc




def _t(a):
    return np.ascontiguousarray(np.asarray(a, np.float32).T)


def _bcast(v):
    a = np.asarray(v, np.float32)
    return np.ascontiguousarray(np.broadcast_to(a, (128, a.shape[0])))


def _col(v):
    a = np.asarray(v, np.float32)
    return np.ascontiguousarray(a.reshape(-1, 128).T)

def host_maps(inp):
    x = np.asarray(inp["L0_attn_x"], np.float32)
    A = np.asarray(inp["L0_attn_A"], np.float32)
    asc = np.empty((128, SC_F), np.float32)
    asc[0::2, :] = A[:512][None, :]
    asc[1::2, :] = A[512:][None, :]
    wau = np.zeros((2 * D, 64), np.float32)
    wau[:, 0:16] = np.asarray(inp["L1_attn_A"], np.float32)
    wau[:, 32:48] = np.asarray(inp["L1_attn_B_ssm"], np.float32)
    wfic = np.concatenate([_t(np.asarray(inp["L1_ffn_wf"], np.float32)),
                           _t(np.asarray(inp["L1_ffn_wi"], np.float32)),
                           _t(np.asarray(inp["L1_ffn_wc"], np.float32))], axis=1)
    u3 = np.concatenate([_t(np.asarray(inp["L1_ffn_uf"], np.float32)),
                         _t(np.asarray(inp["L1_ffn_ui"], np.float32)),
                         _t(np.asarray(inp["L1_ffn_uc"], np.float32))], axis=1)
    b3 = _col(np.concatenate([np.asarray(inp["L1_ffn_bf"], np.float32),
                              np.asarray(inp["L1_ffn_bi"], np.float32),
                              np.asarray(inp["L1_ffn_bc"], np.float32)]))
    c0 = np.ascontiguousarray(
        np.broadcast_to(np.asarray(inp["L1_ffn_in_c"], np.float32)[:, None], (D, ROWS)))
    common = dict(
        ln0g=_bcast(inp["L0_attn_gamma"]), ln0b=_bcast(inp["L0_attn_beta"]),
        win0=np.asarray(inp["L0_attn_W_in"], np.float32),
        bin0=_bcast(inp["L0_attn_b_in"]), asc=asc,
        wout0=np.asarray(inp["L0_attn_W_out"], np.float32),
        bout0=_bcast(inp["L0_attn_b_out"]),
        mur=_col(inp["L0_ffn_mu_r"]), muk=_col(inp["L0_ffn_mu_k"]),
        wrT=_t(np.asarray(inp["L0_ffn_W_r"], np.float32)),
        wkT=_t(np.asarray(inp["L0_ffn_W_k"], np.float32)),
        wvT=_t(np.asarray(inp["L0_ffn_W_v"], np.float32)),
        ln1g=_bcast(inp["L1_attn_norm_w"]), ln1b=_bcast(inp["L1_attn_norm_b"]),
        win1n=np.asarray(inp["L1_attn_W_in"], np.float32),
        binc=_col(inp["L1_attn_b_in"]), wau=wau,
        cssm=np.asarray(inp["L1_attn_C_ssm"], np.float32),
        woT1=_t(np.asarray(inp["L1_attn_W_out"], np.float32)),
        bout1=_bcast(inp["L1_attn_b_out"]),
        wficT=wfic, wfoT=_t(np.asarray(inp["L1_ffn_wo"], np.float32)),
        u3=u3, b3=b3, c0=c0, ident=np.eye(128, dtype=np.float32),
        uoT=_t(np.asarray(inp["L1_ffn_uo"], np.float32)),
        bo=_bcast(inp["L1_ffn_bo"]),
    )
    maps = []
    for k in range(NCORES):
        xs = np.zeros((B, RB, D), np.float32)
        r0 = k * ROWS
        pre = min(PRE, r0)
        if pre > 0:
            xs[:, PRE - pre:PRE, :] = x[:, r0 - pre:r0, :]
        xs[:, PRE:, :] = x[:, r0:r0 + ROWS, :]
        oh = np.zeros(8, np.float32)
        if k >= 1:
            oh[k - 1] = 1.0
        m = dict(common)
        m["x"] = np.ascontiguousarray(xs.reshape(TOK, D))
        m["mask"] = np.full((128, 1), 0.0 if k == 0 else 1.0, np.float32)
        m["oh"] = _bcast(oh)
        maps.append(m)
    return maps


def kernel(**inp):
    nc = build_fused()
    maps = host_maps(inp)
    res = run_bass_kernel_spmd(nc, maps, core_ids=list(range(NCORES)))
    out = np.empty((B, L, D), np.float32)
    for k in range(NCORES):
        x2 = res.results[k]["x2"]
        out[:, k * ROWS:(k + 1) * ROWS, :] = x2.reshape(B, ROWS, D)
    return out
